# version 15 (final)
# baseline (speedup 1.0000x reference)
import numpy as np
from contextlib import ExitStack

import concourse.bass as bass
import concourse.mybir as mybir
from concourse.bass_utils import run_bass_kernel_spmd

F32 = mybir.dt.float32
F32R = mybir.dt.float32r
BF16 = mybir.dt.bfloat16
I32 = mybir.dt.int32
U8 = mybir.dt.uint8
AF = mybir.ActivationFunctionType
ALU = mybir.AluOpType
AX = mybir.AxisListType.X

NCORES = 8
D = 1024
DC = 8
TOWN = 2048
THALO = 512
TFULL = TOWN + THALO
H0 = 3072
HC0 = 24
H1 = 4096
HC1 = 32
NE = 8
NHEAD = 16
HDIM = 64
WIN = 512
LN_EPS = 1e-5
RMS_EPS = 1e-6

ISZ = {F32: 4, F32R: 4, BF16: 2, I32: 4, U8: 1}


class Sem:
    def __init__(self, nc, es, name):
        self.h = es.enter_context(nc.semaphore(name))
        self.n = 0

    def inc(self, instr, k=1):
        instr.then_inc(self.h, k)
        self.n += k
        return self.n

    def dma(self, instr):
        return self.inc(instr, 16)

    def tok(self):
        return (self, self.n)


class Arena:

    def __init__(self, nc, name, kib):
        self.t = nc.alloc_sbuf_tensor(name, [128, kib * 1024], U8)
        self.base = self.t.ap()
        self.off = 0
        self.hi = kib * 1024

    def alloc(self, cols, dt, n=1, align=64):
        out = []
        for _ in range(n):
            sz = cols * ISZ[dt]
            off = (self.off + align - 1) // align * align
            assert off + sz <= self.hi, f"arena overflow {off}+{sz}>{self.hi}"
            out.append(self.base.bitcast(dt)[:, off // ISZ[dt]: off // ISZ[dt] + cols])
            self.off = off + sz
        return out[0] if n == 1 else out

    def row(self, cols, dt=F32, n=1):
        r = self.alloc(cols, dt, n)
        if n == 1:
            return r[0:1, :]
        return [x[0:1, :] for x in r]

    def mark(self):
        return self.off

    def reset(self, m):
        self.off = m


class Pctx:

    def __init__(self, name, arena_kib=200):
        self.nc = bass.Bass("TRN2", target_bir_lowering=False)
        self.es = ExitStack()
        self.A = Arena(self.nc, "A", arena_kib)
        self.PSt = self.nc.alloc_psum_tensor("PS", [128, 4096], F32)
        self.PS = self.PSt.ap()
        self._sems = {}
        self.bank_readers = {}
        self._nt = 0

    def sbuf(self, cols, dt, n=1):
        self._nt += 1
        t = self.nc.alloc_sbuf_tensor(f"T{self._nt}", [128, cols * n], dt).ap()
        out = [t[:, i * cols:(i + 1) * cols] for i in range(n)]
        return out[0] if n == 1 else out

    def sem(self, name):
        if name not in self._sems:
            self._sems[name] = Sem(self.nc, self.es, name)
        return self._sems[name]

    def bank(self, b, w=512):
        return self.PS[:, b * 512: b * 512 + w]

    def bank_bf(self, b, w=1024):
        return self.PS.bitcast(BF16)[:, b * 1024: b * 1024 + w]

    def reader(self, bank, sem):
        d = self.bank_readers.setdefault(bank, {})
        d[id(sem)] = (sem, sem.n)

    def acquire(self, eng, bank):
        for s, v in self.bank_readers.pop(bank, {}).values():
            eng.wait_ge(s.h, v)

    def alloc_consts(self, const_vals=()):
        self.ones_col_f32 = self.A.alloc(1, F32)
        self.ones_row_f32 = self.A.alloc(128, F32)[0:1, :]
        onesr = self.sbuf(129, F32R)
        self.ones_col = onesr[:, 0:1]
        self.ones_row = onesr[0:1, 1:129]
        self.ones_row_bf = self.A.alloc(128, BF16)[0:1, :]
        self.const_tiles = {}
        for v in const_vals:
            t = self.A.alloc(1, F32)
            self.const_tiles[v] = t
            self.nc.const_aps.register(F32, v, t[:, :])

    def alloc_idents(self):
        self.ident_bf = self.A.alloc(128, BF16)
        self.ident_f32 = self.A.alloc(128, F32)
        self._onet_bf = self.A.alloc(128, BF16)
        self._onet_f32 = self.A.alloc(128, F32)

    def init_consts(self, g):
        sGP = self.sem("gp")
        sGP.inc(g.memset(self.ones_col_f32[:, :], 1.0))
        sGP.inc(g.memset(self.ones_row_f32[:, :], 1.0))
        sGP.inc(g.memset(self.ones_row_bf[:, :], 1.0))
        for v, t in self.const_tiles.items():
            sGP.inc(g.memset(t[:, :], v))
        if getattr(self, "ident_bf", None) is not None:
            sGP.inc(g.memset(self._onet_bf[:, :], 1.0))
            sGP.inc(g.memset(self._onet_f32[:, :], 1.0))
            g.wait_ge(sGP.h, sGP.n)
            sGP.inc(g.affine_select(self.ident_bf[:, :], self._onet_bf[:, :],
                                    pattern=[[-1, 128]],
                                    compare_op=ALU.is_equal, fill=0.0, base=0,
                                    channel_multiplier=1))
            sGP.inc(g.affine_select(self.ident_f32[:, :], self._onet_f32[:, :],
                                    pattern=[[-1, 128]],
                                    compare_op=ALU.is_equal, fill=0.0, base=0,
                                    channel_multiplier=1))
        self._n_gp_init = sGP.n

    def init_consts_dve(self, v):
        sGP = self.sem("gp")
        v.wait_ge(sGP.h, self._n_gp_init)
        sGP.inc(v.tensor_copy(self.ones_col[:, :], self.ones_col_f32[:, :]))
        sGP.inc(v.tensor_copy(self.ones_row[:, :], self.ones_row_f32[:, :]))
        self._gate_val = sGP.n

    def gate_consts(self, block):
        sGP = self.sem("gp")

        @block.vector
        def _(v):
            self.init_consts_dve(v)

        @block.scalar
        def _(a):
            a.wait_ge(sGP.h, self._gate_val)

        @block.tensor
        def _(t):
            t.wait_ge(sGP.h, self._gate_val)

        @block.sync
        def _(s):
            s.wait_ge(sGP.h, self._gate_val)


LN_EIGHTH = -2.0794415416798357


def _mk_P(arena_kib=199):
    P = Pctx("k", arena_kib=arena_kib)
    P.alloc_consts(const_vals=(LN_EPS, RMS_EPS, LN_EIGHTH))
    P.alloc_idents()
    P.lb = LnBufs(P, 512, rms2=True, need_xn32=True)
    return P


def phase_barrier(P, block):
    sems = [s for s in P._sems.values() if s.n > 0]

    def wait_all(e):
        for s in sems:
            e.wait_ge(s.h, s.n)

    for name in ("tensor", "vector", "scalar", "gpsimd", "sync"):
        getattr(block, name)(wait_all)



class LnBufs:
    def __init__(self, P, TBmax, rms2=False, need_xn32=True):
        A = P.A
        self.sq = P.sbuf(512, F32R, 2)
        self.t32 = A.alloc(512, F32, 2)
        self.xn32_r = P.sbuf(512, F32R, 2) if need_xn32 else [None, None]
        self.xn32_f = A.alloc(512, F32, 2) if rms2 else [None, None]
        nrows = 5 if rms2 else 4
        rows = A.row(TBmax, F32, nrows)
        self.rows = rows + ([None] if not rms2 else [])
        assert len(self.rows) == 5


def emit_ln(P, block, x, xn, TB, lb, tok_x=None, mode="ln",
            nw_cols=None, logits=None, xn_war=None,
            stats_banks=(6, 7), bc_banks=(4, 5)):
    sPE, sACT, sDVE = P.sem("pe"), P.sem("act"), P.sem("dve")

    def AI(a, instr):
        sACT.inc(instr)
        return sACT.n

    def VI(v, instr):
        sDVE.inc(instr)
        return sDVE.n

    NS = TB // 512
    mu, m2, rstd, nmr, tr = lb.rows
    sb_sx, sb_sq = stats_banks
    bb1, bb2 = bc_banks
    sq, t32 = lb.sq, lb.t32
    xn32 = lb.xn32_r if mode == "moe" else lb.xn32_f
    prev_stats_tok = getattr(P, "_ln_stats_tok", None)
    logit_toks = []

    for sp in range(NS):
        cs = slice(sp * 512, (sp + 1) * 512)
        rs = cs

        @block.tensor
        def _(t, cs=cs):
            P.acquire(t, sb_sx)
            if tok_x is not None:
                seen = {}
                for s, v in tok_x:
                    seen[id(s)] = (s, max(v, seen.get(id(s), (s, 0))[1]))
                for s, v in seen.values():
                    t.wait_ge(s.h, v)
            for c in range(DC):
                mm = t.matmul(P.bank(sb_sx)[0:1, :], P.ones_col_f32,
                              x[c][:, cs], start=(c == 0), stop=(c == DC - 1))
            sPE.inc(mm)
        sGPs = P.sem("gp")
        pe_sq_toks = []
        for c in range(DC):
            @block.gpsimd
            def _(g, c=c, cs=cs):
                if tok_x is not None and sp == 0:
                    s, v = tok_x[c]
                    g.wait_ge(s.h, v)
                if len(pe_sq_toks) >= 2:
                    g.wait_ge(sPE.h, pe_sq_toks[-2])
                elif prev_stats_tok is not None and c < 2:
                    g.wait_ge(sPE.h, prev_stats_tok)
                g.wait_ge(sGPs.h, sGPs.n)
                g.tensor_tensor(sq[c % 2][:, :], x[c][:, cs], x[c][:, cs],
                                ALU.mult).then_inc(sGPs.h, 1)
            sGPs.n += 1
            tok_sqc = sGPs.n

            @block.tensor
            def _(t, c=c, tok=tok_sqc):
                t.wait_ge(sGPs.h, tok)
                if c == 0:
                    P.acquire(t, sb_sq)
                t.matmul(P.bank(sb_sq)[0:1, :], P.ones_col,
                         sq[c % 2][:, :], start=(c == 0),
                         stop=(c == DC - 1)).then_inc(sPE.h, 1)
            sPE.n += 1
            pe_sq_toks.append(sPE.n)
        tok_stats = sPE.n
        prev_stats_tok = tok_stats

        @block.scalar
        def _(a, rs=rs, tok=tok_stats):
            a.wait_ge(sPE.h, tok - 1)
            a.wait_ge(sACT.h, sACT.n)
            AI(a, a.mul(mu[:, 0:512], P.bank(sb_sx)[0:1, :], 1.0 / D))
            a.wait_ge(sPE.h, tok)
            a.wait_ge(sACT.h, sACT.n)
            AI(a, a.activation(m2[:, 0:512], P.bank(sb_sq)[0:1, :], AF.Copy,
                               scale=1.0 / D))
        tok_m = sACT.n
        P.reader(sb_sx, sACT)
        P.reader(sb_sq, sACT)

        @block.vector
        def _(v, rs=rs, tok=tok_m):
            v.wait_ge(sACT.h, tok)
            v.wait_ge(sDVE.h, sDVE.n)
            VI(v, v.tensor_tensor(rstd[:, 0:512], mu[:, 0:512], mu[:, 0:512], ALU.mult))
            v.wait_ge(sDVE.h, sDVE.n)
            VI(v, v.tensor_tensor(m2[:, 0:512], m2[:, 0:512], rstd[:, 0:512],
                                  ALU.subtract))
        tok_var = sDVE.n

        if mode in ("ln", "moe"):
            @block.scalar
            def _(a, rs=rs, tok=tok_var):
                a.wait_ge(sDVE.h, tok)
                a.wait_ge(sACT.h, sACT.n)
                AI(a, a.activation(rstd[:, 0:512], m2[:, 0:512], AF.Ln,
                                   bias=LN_EPS))
                a.wait_ge(sACT.h, sACT.n)
                AI(a, a.activation(rstd[:, 0:512], rstd[:, 0:512], AF.Exp,
                                   scale=-0.5))
            tok_s = sACT.n

            @block.vector
            def _(v, rs=rs, tok=tok_s):
                v.wait_ge(sACT.h, tok)
                v.wait_ge(sDVE.h, sDVE.n)
                VI(v, v.scalar_tensor_tensor(nmr[:, 0:512], mu[:, 0:512], -1.0,
                                             rstd[:, 0:512], ALU.mult, ALU.mult))
            tok_r = sDVE.n
        else:
            @block.scalar
            def _(a, rs=rs, tok=tok_var):
                a.wait_ge(sDVE.h, tok)
                a.wait_ge(sACT.h, sACT.n)
                AI(a, a.add(tr[:, 0:512], m2[:, 0:512], LN_EPS))
            tok_d1 = sACT.n

            @block.vector
            def _(v, rs=rs, tok=tok_d1):
                v.wait_ge(sACT.h, tok)
                v.wait_ge(sDVE.h, sDVE.n)
                VI(v, v.reciprocal(rstd[:, 0:512], tr[:, 0:512]))
                v.wait_ge(sDVE.h, sDVE.n)
                VI(v, v.tensor_tensor(nmr[:, 0:512], m2[:, 0:512], rstd[:, 0:512],
                                      ALU.mult))
            tok_q = sDVE.n

            @block.scalar
            def _(a, rs=rs, tok=tok_q):
                a.wait_ge(sDVE.h, tok)
                a.wait_ge(sACT.h, sACT.n)
                AI(a, a.add(nmr[:, 0:512], nmr[:, 0:512], RMS_EPS))
            tok_d2 = sACT.n

            @block.vector
            def _(v, rs=rs, tok=tok_d2):
                v.wait_ge(sACT.h, tok)
                v.wait_ge(sDVE.h, sDVE.n)
                VI(v, v.tensor_tensor(tr[:, 0:512], tr[:, 0:512], nmr[:, 0:512],
                                      ALU.mult))
            tok_p = sDVE.n

            @block.scalar
            def _(a, rs=rs, tok=tok_p):
                a.wait_ge(sDVE.h, tok)
                a.wait_ge(sACT.h, sACT.n)
                AI(a, a.activation(rstd[:, 0:512], tr[:, 0:512], AF.Ln))
                a.wait_ge(sACT.h, sACT.n)
                AI(a, a.activation(rstd[:, 0:512], rstd[:, 0:512], AF.Exp,
                                   scale=-0.5))
            tok_s = sACT.n

            @block.vector
            def _(v, rs=rs, tok=tok_s):
                v.wait_ge(sACT.h, tok)
                v.wait_ge(sDVE.h, sDVE.n)
                VI(v, v.scalar_tensor_tensor(nmr[:, 0:512], mu[:, 0:512], -1.0,
                                             rstd[:, 0:512], ALU.mult, ALU.mult))
            tok_r = sDVE.n

        @block.tensor
        def _(t, rs=rs, tok=tok_r):
            t.wait_ge(sDVE.h, tok)
            P.acquire(t, bb1)
            t.matmul(P.bank(bb1), P.ones_row_f32, rstd[:, 0:512],
                     start=True, stop=True)
            P.acquire(t, bb2)
            sPE.inc(t.matmul(P.bank(bb2), P.ones_row_f32, nmr[:, 0:512],
                             start=True, stop=True))
        tok_bc = sPE.n

        for c in range(DC):
            if mode == "ln":
                @block.vector
                def _(v, c=c, cs=cs, tok=tok_bc, sp=sp):
                    if c == 0:
                        v.wait_ge(sPE.h, tok)
                        if sp == 0 and xn_war is not None:
                            for ws, wv_ in xn_war:
                                v.wait_ge(ws.h, wv_)
                    v.wait_ge(sDVE.h, sDVE.n)
                    VI(v, v.tensor_tensor(t32[c % 2][:, :], x[c][:, cs],
                                          P.bank(bb1), ALU.mult))
                    v.wait_ge(sDVE.h, sDVE.n)
                    VI(v, v.tensor_tensor(xn[c][:, cs], t32[c % 2][:, :],
                                          P.bank(bb2), ALU.add))
            elif mode == "moe":
                @block.vector
                def _(v, c=c, cs=cs, tok=tok_bc, sp=sp):
                    if c == 0:
                        v.wait_ge(sPE.h, tok)
                        if sp == 0 and xn_war is not None:
                            for ws, wv_ in xn_war:
                                v.wait_ge(ws.h, wv_)
                    if len(logit_toks) >= 2:
                        v.wait_ge(sPE.h, logit_toks[-2])
                    v.wait_ge(sDVE.h, sDVE.n)
                    VI(v, v.tensor_tensor(t32[c % 2][:, :], x[c][:, cs],
                                          P.bank(bb1), ALU.mult))
                    v.wait_ge(sDVE.h, sDVE.n)
                    VI(v, v.tensor_tensor(xn32[c % 2][:, :], t32[c % 2][:, :],
                                          P.bank(bb2), ALU.add))
                    v.wait_ge(sDVE.h, sDVE.n)
                    VI(v, v.tensor_copy(xn[c][:, cs], xn32[c % 2][:, :]))
                tok_xn32 = sDVE.n

                @block.tensor
                def _(t, c=c, tok=tok_xn32):
                    t.wait_ge(sDVE.h, tok)
                    if c == 0:
                        P.acquire(t, logits[1])
                    t.matmul(P.bank(logits[1])[0:NE, :], logits[0][c],
                             xn32[c % 2][:, :],
                             start=(c == 0), stop=(c == DC - 1)).then_inc(sPE.h, 1)
                sPE.n += 1
                logit_toks.append(sPE.n)
            else:
                @block.vector
                def _(v, c=c, cs=cs, tok=tok_bc, sp=sp):
                    if c == 0:
                        v.wait_ge(sPE.h, tok)
                        if sp == 0 and xn_war is not None:
                            for ws, wv_ in xn_war:
                                v.wait_ge(ws.h, wv_)
                    v.wait_ge(sDVE.h, sDVE.n)
                    VI(v, v.tensor_tensor(xn32[c % 2][:, :], x[c][:, cs],
                                          P.bank(bb1), ALU.mult))
                    v.wait_ge(sDVE.h, sDVE.n)
                    VI(v, v.tensor_tensor(xn32[c % 2][:, :], xn32[c % 2][:, :],
                                          P.bank(bb2), ALU.add))
                    v.wait_ge(sDVE.h, sDVE.n)
                    VI(v, v.tensor_scalar(t32[c % 2][:, :], xn32[c % 2][:, :],
                                          nw_cols[c], None, ALU.mult))
                    v.wait_ge(sDVE.h, sDVE.n)
                    VI(v, v.tensor_copy(xn[c][:, cs], t32[c % 2][:, :]))
                    v.wait_ge(sDVE.h, sDVE.n)
                    VI(v, v.tensor_tensor(x[c][:, cs], x[c][:, cs],
                                          t32[c % 2][:, :], ALU.add))
        P.reader(bb1, sDVE)
        P.reader(bb2, sDVE)
        if mode == "moe":
            logits[2](sp, logit_toks[-1])

    P._ln_stats_tok = prev_stats_tok
    return sDVE.tok()



def build_l0ffn(T):
    P = _mk_P()
    nc = P.nc
    aps = {
        "x": nc.dram_tensor("x", [D, T], F32, kind="ExternalInput").ap(),
        "wg": nc.dram_tensor("wg", [D, H0], BF16, kind="ExternalInput").ap(),
        "wu": nc.dram_tensor("wu", [D, H0], BF16, kind="ExternalInput").ap(),
        "wd": nc.dram_tensor("wd", [H0, D], BF16, kind="ExternalInput").ap(),
        "xo": nc.dram_tensor("xo", [D, T], F32, kind="ExternalOutput").ap(),
    }
    with P.es:
        block = P.es.enter_context(nc.Block())

        @block.gpsimd
        def _(g):
            P.init_consts(g)
        P.gate_consts(block)
        emit_l0ffn(P, block, aps, T)
        sST = P.sem("st")

        @block.gpsimd
        def _(g):
            g.wait_ge(sST.h, sST.n)
    return nc


def emit_l0ffn(P, block, aps, T):
    nc = P.nc
    x_d, wg_d, wu_d, wd_d, xo_d = (aps["x"], aps["wg"], aps["wu"], aps["wd"],
                                   aps["xo"])
    if T == 2560:
        blocks = [(0, 1024), (1024, 1024), (2048, 512)]
    else:
        blocks = [(0, 1024), (1024, 1024)]
    TBmax = max(tb for _, tb in blocks)

    A = P.A
    lb = P.lb
    x_blk = A.alloc(TBmax, F32, DC)
    xn_blk = A.alloc(TBmax, BF16, DC)
    h_blk = A.alloc(TBmax, BF16, HC0)
    wd_bf = A.alloc(D, BF16, HC0)
    wgu_st = A.alloc(2048, BF16, 3)
    wd_st = [wgu_st[0][:, 0:1024], wgu_st[1][:, 0:1024]]
    wgu_bf = A.alloc(2048, BF16, 2)
    sg = A.alloc(512, F32, 2)
    assert A.off <= A.hi

    sPE, sACT, sDVE = P.sem("pe"), P.sem("act"), P.sem("dve")
    sLX, sST = P.sem("lx"), P.sem("st")
    sLW = [P.sem("lw0"), P.sem("lw1")]
    sLWG = [P.sem("lwga"), P.sem("lwgb"), P.sem("lwgc")]
    sGP = P.sem("gp")

    if True:
        wd_conv = [0] * HC0
        for hc in range(HC0):
            @block.sync
            def _(s, hc=hc):
                if hc >= 2:
                    s.wait_ge(sDVE.h, wd_conv[hc - 2])
                sLW[hc % 2].dma(s.dma_start(out=wd_st[hc % 2][:, :],
                                            in_=wd_d[hc * 128:(hc + 1) * 128, :]))
            tok_l = sLW[hc % 2].n

            @block.vector
            def _(v, hc=hc, tok_l=tok_l):
                v.wait_ge(sLW[hc % 2].h, tok_l)
                v.wait_ge(sDVE.h, sDVE.n)
                v.tensor_scalar(wd_bf[hc][:, :], wd_st[hc % 2][:, :],
                                1.0 / 127.0, None, ALU.mult).then_inc(sDVE.h, 1)
            sDVE.n += 1
            wd_conv[hc] = sDVE.n
        tok_wd = sDVE.n

        x_store_tok = [0] * DC
        prev_block_loadgate = None
        wgu_mm_last = [0]
        silu_tok = {}
        h_tok = {}
        wgu_conv = {}
        wgu_mm = {}

        for bi, (tb0, TB) in enumerate(blocks):
            NSb = TB // 512
            for c in range(DC):
                @block.gpsimd
                def _(g, c=c, tb0=tb0, TB=TB):
                    if prev_block_loadgate is not None:
                        g.wait_ge(sST.h, prev_block_loadgate)
                    sLX.dma(g.dma_start(out=x_blk[c][:, 0:TB],
                                        in_=x_d[c * 128:(c + 1) * 128, tb0:tb0 + TB]))
            tok_x = [(sLX, sLX.n)] * DC

            xn_war = [(sPE, wgu_mm_last[0])] if bi > 0 else None
            tok_xn = emit_ln(P, block, [x_blk[c][:, 0:TB] for c in range(DC)],
                             [xn_blk[c][:, 0:TB] for c in range(DC)], TB, lb,
                             tok_x=tok_x, mode="ln", xn_war=xn_war)

            for hc in range(HC0):
                sl3 = hc % 3
                @block.sync
                def _(s, hc=hc, bi=bi, sl3=sl3):
                    if bi == 0 and hc < 2:
                        s.wait_ge(sDVE.h, tok_wd)
                    if (bi, hc - 3) in wgu_conv:
                        s.wait_ge(sDVE.h, wgu_conv[(bi, hc - 3)])
                    elif hc < 3 and bi > 0 and (bi - 1, HC0 - 3 + hc) in wgu_conv:
                        s.wait_ge(sDVE.h, wgu_conv[(bi - 1, HC0 - 3 + hc)])
                    sLWG[sl3].dma(s.dma_start(
                        out=wgu_st[sl3][:, 0:1024].rearrange(
                            "p (c h) -> p c h", c=DC),
                        in_=wg_d[:, hc * 128:(hc + 1) * 128].rearrange(
                            "(c p) h -> p c h", p=128)))
                    sLWG[sl3].dma(s.dma_start(
                        out=wgu_st[sl3][:, 1024:2048].rearrange(
                            "p (c h) -> p c h", c=DC),
                        in_=wu_d[:, hc * 128:(hc + 1) * 128].rearrange(
                            "(c p) h -> p c h", p=128)))
                tok_l = sLWG[sl3].n

                @block.vector
                def _(v, hc=hc, bi=bi, sl3=sl3, tok_l=tok_l):
                    v.wait_ge(sLWG[sl3].h, tok_l)
                    key = (bi, hc - 2) if hc >= 2 else (bi - 1, HC0 - 2 + hc)
                    if key in wgu_mm:
                        v.wait_ge(sPE.h, wgu_mm[key])
                    v.wait_ge(sDVE.h, sDVE.n)
                    v.tensor_scalar(wgu_bf[hc % 2][:, :], wgu_st[sl3][:, :],
                                    1.0 / 127.0, None,
                                    ALU.mult).then_inc(sDVE.h, 1)
                sDVE.n += 1
                wgu_conv[(bi, hc)] = sDVE.n

                g_tok = {}
                u_tok = {}

                @block.tensor
                def _(t, hc=hc, bi=bi, NSb=NSb):
                    t.wait_ge(sDVE.h, wgu_conv[(bi, hc)])
                    s, v = tok_xn
                    t.wait_ge(s.h, v)
                    for sp in range(NSb):
                        P.acquire(t, sp)
                        P.acquire(t, 3 + sp)
                    for cin in range(DC):
                        for which in (0, 1):
                            lhs = wgu_bf[hc % 2][:, which * 1024 + cin * 128:
                                                 which * 1024 + (cin + 1) * 128]
                            for sp in range(NSb):
                                bank = sp if which == 0 else 3 + sp
                                mm = t.matmul(P.bank(bank), lhs,
                                              xn_blk[cin][:, sp * 512:(sp + 1) * 512],
                                              start=(cin == 0), stop=(cin == DC - 1))
                                if cin == DC - 1:
                                    sPE.inc(mm)
                                    if which == 0:
                                        g_tok[sp] = sPE.n
                                    else:
                                        u_tok[sp] = sPE.n
                wgu_mm[(bi, hc)] = sPE.n
                wgu_mm_last[0] = sPE.n

                for sp in range(NSb):
                    @block.scalar
                    def _(a, hc=hc, sp=sp, bi=bi):
                        a.wait_ge(sPE.h, g_tok[sp])
                        prev = (bi, hc - 1, sp) if hc > 0 else (bi - 1, HC0 - 1, sp)
                        if prev in h_tok:
                            a.wait_ge(sDVE.h, h_tok[prev])
                        a.wait_ge(sACT.h, sACT.n)
                        a.activation(sg[sp][:, :], P.bank(sp),
                                     AF.Silu).then_inc(sACT.h, 1)
                    sACT.n += 1
                    silu_tok[(bi, hc, sp)] = sACT.n
                    P.reader(sp, sACT)

                    @block.vector
                    def _(v, hc=hc, sp=sp, bi=bi):
                        v.wait_ge(sACT.h, silu_tok[(bi, hc, sp)])
                        v.wait_ge(sPE.h, u_tok[sp])
                        v.wait_ge(sDVE.h, sDVE.n)
                        v.tensor_tensor(h_blk[hc][:, sp * 512:(sp + 1) * 512],
                                        sg[sp][:, :], P.bank(3 + sp),
                                        ALU.mult).then_inc(sDVE.h, 1)
                    sDVE.n += 1
                    h_tok[(bi, hc, sp)] = sDVE.n
                    P.reader(3 + sp, sDVE)

            for sp in range(NSb):
                dmm_tok = {}

                @block.tensor
                def _(t, sp=sp, NSb=NSb, bi=bi):
                    for dc in range(DC):
                        P.acquire(t, dc)
                    t.wait_ge(sDVE.h, tok_wd)
                    t.wait_ge(sDVE.h, h_tok[(bi, HC0 - 1, sp)])
                    for hc in range(HC0):
                        if (hc, sp) not in h_tok:
                            pass
                        for dc in range(DC):
                            mm = t.matmul(P.bank(dc), wd_bf[hc][:, dc * 128:(dc + 1) * 128],
                                          h_blk[hc][:, sp * 512:(sp + 1) * 512],
                                          start=(hc == 0), stop=(hc == HC0 - 1))
                            if hc == HC0 - 1:
                                sPE.inc(mm)
                                dmm_tok[dc] = sPE.n

                for dc in range(DC):
                    @block.vector
                    def _(v, dc=dc, sp=sp):
                        v.wait_ge(sPE.h, dmm_tok[dc])
                        v.wait_ge(sDVE.h, sDVE.n)
                        v.scalar_tensor_tensor(
                            x_blk[dc][:, sp * 512:(sp + 1) * 512], P.bank(dc), 0.5,
                            x_blk[dc][:, sp * 512:(sp + 1) * 512],
                            ALU.mult, ALU.add).then_inc(sDVE.h, 1)
                    sDVE.n += 1
                    x_store_tok[dc] = sDVE.n
                    P.reader(dc, sDVE)

            for c in range(DC):
                @block.gpsimd
                def _(g, c=c, tb0=tb0, TB=TB, tok=x_store_tok[c]):
                    g.wait_ge(sDVE.h, tok)
                    sST.dma(g.dma_start(out=xo_d[c * 128:(c + 1) * 128, tb0:tb0 + TB],
                                        in_=x_blk[c][:, 0:TB]))
            prev_block_loadgate = sST.n



def build_l0attn():
    P = _mk_P()
    nc = P.nc
    aps = {
        "x": nc.dram_tensor("x", [D, TFULL], F32, kind="ExternalInput").ap(),
        "wq": nc.dram_tensor("wq", [D, D], F32, kind="ExternalInput").ap(),
        "wk": nc.dram_tensor("wk", [D, D], F32, kind="ExternalInput").ap(),
        "wv": nc.dram_tensor("wv", [D, D], F32, kind="ExternalInput").ap(),
        "wo": nc.dram_tensor("wo", [D, D], F32, kind="ExternalInput").ap(),
        "bq": nc.dram_tensor("bq", [128, 8], F32, kind="ExternalInput").ap(),
        "bk": nc.dram_tensor("bk", [128, 8], F32, kind="ExternalInput").ap(),
        "bo": nc.dram_tensor("bo", [128, 8], F32, kind="ExternalInput").ap(),
        "bvr": nc.dram_tensor("bvr", [1, D], F32, kind="ExternalInput").ap(),
        "maskA": nc.dram_tensor("maskA", [640, 512], BF16,
                                kind="ExternalInput").ap(),
        "maskB": nc.dram_tensor("maskB", [128, 128], BF16,
                                kind="ExternalInput").ap(),
        "xo": nc.dram_tensor("xo", [D, TOWN], F32, kind="ExternalOutput").ap(),
    }
    with P.es:
        block = P.es.enter_context(nc.Block())

        @block.gpsimd
        def _(g):
            P.init_consts(g)
        P.gate_consts(block)
        emit_l0attn(P, block, aps)
        sST = P.sem("st")

        @block.gpsimd
        def _(g):
            g.wait_ge(sST.h, sST.n)
    return nc


def emit_l0attn(P, block, aps, tag=""):
    nc = P.nc
    x_d, wq_d, wk_d, wv_d, wo_d = (aps["x"], aps["wq"], aps["wk"], aps["wv"],
                                   aps["wo"])
    bq_d, bk_d, bo_d, bvr_d = aps["bq"], aps["bk"], aps["bo"], aps["bvr"]
    mA_d, mB_d, xo_d = aps["maskA"], aps["maskB"], aps["xo"]
    xn_dram = nc.dram_tensor("xn_scr2" + tag, [D, TFULL], BF16).ap()

    A = P.A
    lb = P.lb
    K_fm = A.alloc(TFULL, BF16, DC)
    V_tm = A.alloc(D, BF16, TFULL // 128)
    Q_fm = A.alloc(TOWN, BF16, DC)
    bq_sb = A.alloc(8, F32)
    bk_sb = A.alloc(8, F32)
    bo_sb = A.alloc(8, F32)
    bvB = A.alloc(D, BF16)
    ident = P.ident_bf
    ao = A.alloc(512, BF16, DC)
    markR = A.mark()
    w_bf = A.alloc(8 * D, BF16, 2)
    w_st0 = A.alloc(D, F32)
    w_st = [w_st0, w_st0]
    bvr_sb = w_st0[0:1, :]
    x_sp = A.alloc(512, F32, DC)
    endA = A.mark()
    A.reset(markR)
    wo_bf = A.alloc(8 * D, BF16)
    wo_st = A.alloc(D, F32)
    mA_sb = A.alloc(512, BF16, 5)
    mB_sb = A.alloc(128, BF16)
    x_sp2 = A.alloc(512, F32, DC)
    Pbuf2 = A.alloc(640, BF16, 2)
    PT2 = A.alloc(640, BF16, 2)
    mrow = A.alloc(1, F32, 2)
    nrm2p = A.alloc(1, F32, 32)
    negm2 = A.alloc(1, F32, 16)
    srow = A.alloc(1, F32, 6)
    endB = A.mark()
    A.off = max(endA, endB)
    xn_sp = ao
    assert A.off <= A.hi, A.off

    sPE, sACT, sDVE = P.sem("pe"), P.sem("act"), P.sem("dve")
    sLX, sST = P.sem("lx"), P.sem("st")
    sLW = [P.sem("lw0"), P.sem("lw1")]

    NSP = TFULL // 512

    if True:
        sGP = P.sem("gp")
        tok_ident = P._gate_val

        sSM = P.sem("sm")

        @block.sync
        def _(s):
            for dst, srcd in ((bq_sb, bq_d), (bk_sb, bk_d), (bo_sb, bo_d)):
                sSM.dma(s.dma_start(out=dst[:, :], in_=srcd[:, :]))
            sSM.dma(s.dma_start(out=bvr_sb[:, :], in_=bvr_d[:, :]))
        tok_small = sSM.n

        @block.tensor
        def _(t):
            t.wait_ge(sSM.h, tok_small)
            t.matmul(P.bank(0), P.ones_row_f32, bvr_sb[:, 0:512],
                     start=True, stop=True)
            sPE.inc(t.matmul(P.bank(1), P.ones_row_f32, bvr_sb[:, 512:1024],
                             start=True, stop=True))
        tok_bvmm = sPE.n

        @block.scalar
        def _(a):
            a.wait_ge(sPE.h, tok_bvmm)
            a.wait_ge(sACT.h, sACT.n)
            sACT.inc(a.activation(bvB[:, 0:512], P.bank(0), AF.Copy))
            a.wait_ge(sACT.h, sACT.n)
            sACT.inc(a.activation(bvB[:, 512:1024], P.bank(1), AF.Copy))
        P.reader(0, sACT)
        P.reader(1, sACT)
        tok_bvB = sACT.n

        def load_weights(wslot, wd_ap, gate_tok=None, dst=None, st2=None):
            toks = []
            for cin in range(DC):
                @block.sync
                def _(s, cin=cin):
                    if gate_tok is not None:
                        s.wait_ge(gate_tok[0].h, gate_tok[1])
                    prev = toks[-1] if toks else getattr(P, "_wst_last", 0)
                    if prev:
                        s.wait_ge(sDVE.h, prev)
                    stg = st2 if st2 is not None else w_st[cin % 2]
                    sLW[cin % 2].dma(s.dma_start(
                        out=stg[:, :],
                        in_=wd_ap[cin * 128:(cin + 1) * 128, :]))
                tok_l = sLW[cin % 2].n

                @block.vector
                def _(v, cin=cin, tok_l=tok_l):
                    v.wait_ge(sLW[cin % 2].h, tok_l)
                    v.wait_ge(sDVE.h, sDVE.n)
                    d = dst if dst is not None else w_bf[wslot]
                    stg = st2 if st2 is not None else w_st[cin % 2]
                    v.tensor_scalar(d[:, cin * 1024:(cin + 1) * 1024],
                                    stg[:, :], 1.0, None,
                                    ALU.mult).then_inc(sDVE.h, 1)
                sDVE.n += 1
                toks.append(sDVE.n)
            P._wst_last = sDVE.n
            return sDVE.n

        tok_wq = load_weights(0, wq_d, gate_tok=(sACT, tok_bvB))
        tok_wk = load_weights(1, wk_d)

        qk_mm_toks = {}
        prev_norm_gate = None
        for sp in range(NSP):
            @block.gpsimd
            def _(g, sp=sp):
                if sp >= 1:
                    g.wait_ge(sPE.h, P._ln_stats_tok or 0)
                    g.wait_ge(sDVE.h, sDVE.n)
                for c in range(DC):
                    sLX.dma(g.dma_start(out=x_sp[c][:, :],
                                        in_=x_d[c * 128:(c + 1) * 128,
                                                sp * 512:(sp + 1) * 512]))
            tok_x = [(sLX, sLX.n)] * DC

            xn_war = ([(sPE, qk_mm_toks.get(sp - 1, 0)), (sST, sST.n)]
                      if sp >= 1 else None)
            tok_xn = emit_ln(P, block, x_sp, xn_sp, 512, lb, tok_x=tok_x,
                             mode="ln", xn_war=xn_war)

            @block.gpsimd
            def _(g, sp=sp, tok=tok_xn):
                g.wait_ge(tok[0].h, tok[1])
                for c in range(DC):
                    sST.dma(g.dma_start(
                        out=xn_dram[c * 128:(c + 1) * 128, sp * 512:(sp + 1) * 512],
                        in_=xn_sp[c][:, :]))

            for co in range(DC):
                bank = co % 2

                @block.tensor
                def _(t, sp=sp, co=co, bank=bank, tok=tok_xn):
                    if co == 0:
                        t.wait_ge(tok[0].h, tok[1])
                        if sp == 0:
                            t.wait_ge(sDVE.h, max(tok_wq, tok_wk))
                    P.acquire(t, bank)
                    for cin in range(DC):
                        mm = t.matmul(P.bank(bank),
                                      w_bf[1][:, cin * 1024 + co * 128:
                                              cin * 1024 + co * 128 + 128],
                                      xn_sp[cin][:, :],
                                      start=(cin == 0), stop=(cin == DC - 1))
                    sPE.inc(mm)
                ktok = sPE.n

                @block.vector
                def _(v, sp=sp, co=co, bank=bank, ktok=ktok):
                    v.wait_ge(sPE.h, ktok)
                    v.wait_ge(sDVE.h, sDVE.n)
                    sDVE.inc(v.tensor_scalar(
                        K_fm[co][:, sp * 512:(sp + 1) * 512], P.bank(bank),
                        bk_sb[:, co:co + 1], None, ALU.add))
                P.reader(bank, sDVE)

                if sp >= 1:
                    bank2 = 2 + co % 2

                    @block.tensor
                    def _(t, sp=sp, co=co, bank2=bank2):
                        P.acquire(t, bank2)
                        for cin in range(DC):
                            mm = t.matmul(P.bank(bank2),
                                          w_bf[0][:, cin * 1024 + co * 128:
                                                  cin * 1024 + co * 128 + 128],
                                          xn_sp[cin][:, :],
                                          start=(cin == 0), stop=(cin == DC - 1))
                        sPE.inc(mm)
                    qtok = sPE.n

                    @block.vector
                    def _(v, sp=sp, co=co, bank2=bank2, qtok=qtok):
                        v.wait_ge(sPE.h, qtok)
                        v.wait_ge(sDVE.h, sDVE.n)
                        sDVE.inc(v.tensor_scalar(
                            Q_fm[co][:, (sp - 1) * 512:sp * 512],
                            P.bank(bank2), bq_sb[:, co:co + 1], None, ALU.add))
                    P.reader(bank2, sDVE)
            qk_mm_toks[sp] = sPE.n

        tok_qk_all = (sDVE, sDVE.n)

        tok_wv = load_weights(1, wv_d, gate_tok=(sPE, qk_mm_toks[NSP - 1]))
        tok_xn_store = sST.n
        v_mm_toks = {}
        for sp in range(NSP):
            @block.gpsimd
            def _(g, sp=sp):
                g.wait_ge(sST.h, tok_xn_store)
                if sp >= 1:
                    g.wait_ge(sPE.h, v_mm_toks[sp - 1])
                else:
                    g.wait_ge(sPE.h, qk_mm_toks[NSP - 1])
                for c in range(DC):
                    sLX.dma(g.dma_start(out=xn_sp[c][:, :],
                                        in_=xn_dram[c * 128:(c + 1) * 128,
                                                    sp * 512:(sp + 1) * 512]))
            tok_xnl = sLX.n

            for tci in range(4):
                for dsp in range(2):
                    tc = sp * 4 + tci
                    bank = 2 * (tci % 2) + dsp

                    @block.tensor
                    def _(t, sp=sp, tci=tci, dsp=dsp, bank=bank, tok_xnl=tok_xnl):
                        if tci == 0 and dsp == 0:
                            t.wait_ge(sLX.h, tok_xnl)
                            if sp == 0:
                                t.wait_ge(sDVE.h, tok_wv)
                        P.acquire(t, bank)
                        for cin in range(DC):
                            mm = t.matmul(
                                P.bank(bank),
                                xn_sp[cin][:, tci * 128:(tci + 1) * 128],
                                w_bf[1][:, cin * 1024 + dsp * 512:
                                        cin * 1024 + dsp * 512 + 512],
                                start=(cin == 0), stop=(cin == DC - 1))
                        sPE.inc(mm)
                    vtok = sPE.n

                    @block.vector
                    def _(v, tc=tc, dsp=dsp, bank=bank, vtok=vtok):
                        v.wait_ge(sPE.h, vtok)
                        v.wait_ge(sACT.h, tok_bvB)
                        v.wait_ge(sDVE.h, sDVE.n)
                        v.tensor_tensor(
                            V_tm[tc][:, dsp * 512:(dsp + 1) * 512],
                            P.bank(bank), bvB[:, dsp * 512:(dsp + 1) * 512],
                            ALU.add).then_inc(sDVE.h, 1)
                    sDVE.n += 1
                    P.reader(bank, sDVE)
            v_mm_toks[sp] = sPE.n
        tok_v_all = (sDVE, sDVE.n)

        @block.sync
        def _(s):
            s.wait_ge(sPE.h, v_mm_toks[NSP - 1])
            for i in range(5):
                sSM.dma(s.dma_start(out=mA_sb[i][:, :],
                                    in_=mA_d[i * 128:(i + 1) * 128, :]))
            sSM.dma(s.dma_start(out=mB_sb[:, :], in_=mB_d[:, :]))
        tok_masks = sSM.n
        tok_wo = load_weights(None, wo_d, gate_tok=(sPE, v_mm_toks[NSP - 1]),
                              dst=wo_bf, st2=wo_st)
        sqr = lb.sq
        nrm_mm2 = []
        sq_cnt = [0]
        for which, F, nsp in ((0, Q_fm, 4), (1, K_fm, 5)):
            for c0 in range(DC):
                for sp8 in range(nsp):
                    idx = sq_cnt[0]
                    sq_cnt[0] += 1
                    par = idx % 2

                    @block.vector
                    def _(v, F=F, c0=c0, sp8=sp8, par=par, idx=idx):
                        if idx == 0:
                            s, vv = tok_v_all
                            v.wait_ge(s.h, vv)
                        if len(nrm_mm2) >= 2:
                            v.wait_ge(sPE.h, nrm_mm2[-2])
                        v.wait_ge(sDVE.h, sDVE.n)
                        v.tensor_tensor(
                            sqr[par][:, :],
                            F[c0][:, sp8 * 512:(sp8 + 1) * 512],
                            F[c0][:, sp8 * 512:(sp8 + 1) * 512],
                            ALU.mult).then_inc(sDVE.h, 1)
                    sDVE.n += 1
                    tok_sq = sDVE.n

                    for pbi in (0, 64):
                        hh = c0 * 2 + pbi // 64
                        bank = 2 + idx % 2

                        @block.tensor
                        def _(t, par=par, pbi=pbi, bank=bank, tok_sq=tok_sq):
                            t.wait_ge(sDVE.h, tok_sq)
                            P.acquire(t, bank)
                            t.matmul(P.bank(bank)[0:1, :],
                                     P.ones_col[pbi:pbi + 64, :],
                                     sqr[par][pbi:pbi + 64, :],
                                     start=True, stop=True).then_inc(sPE.h, 1)
                        sPE.n += 1
                        tok_mm = sPE.n

                        @block.vector
                        def _(v, which=which, hh=hh, bank=bank,
                              tok_mm=tok_mm, first=(sp8 == 0)):
                            v.wait_ge(sPE.h, tok_mm)
                            v.wait_ge(sDVE.h, sDVE.n)
                            sDVE.inc(v.reduce_max(mrow[0][0:1, :],
                                                  P.bank(bank)[0:1, :],
                                                  axis=AX))
                            v.wait_ge(sDVE.h, sDVE.n)
                            dst = nrm2p[which * 16 + hh][0:1, :]
                            if first:
                                sDVE.inc(v.tensor_copy(dst, mrow[0][0:1, :]))
                            else:
                                sDVE.inc(v.tensor_tensor(dst, dst,
                                                         mrow[0][0:1, :],
                                                         ALU.max))
                        P.reader(bank, sDVE)
                    nrm_mm2.append(sPE.n)

        for h in range(NHEAD):
            @block.vector
            def _(v, h=h):
                v.wait_ge(sDVE.h, sDVE.n)
                sDVE.inc(v.tensor_tensor(mrow[0][0:1, :], nrm2p[h][0:1, :],
                                         nrm2p[16 + h][0:1, :], ALU.mult))
            tok_t = sDVE.n

            @block.scalar
            def _(a, tok_t=tok_t):
                a.wait_ge(sDVE.h, tok_t)
                a.wait_ge(sACT.h, sACT.n)
                sACT.inc(a.activation(mrow[1][0:1, :], mrow[0][0:1, :], AF.Ln))
                a.wait_ge(sACT.h, sACT.n)
                sACT.inc(a.activation(mrow[1][0:1, :], mrow[1][0:1, :], AF.Exp,
                                      scale=0.5, bias=LN_EIGHTH))
            tok_mh = sACT.n

            @block.vector
            def _(v, tok_mh=tok_mh):
                v.wait_ge(sACT.h, tok_mh)
                v.wait_ge(sDVE.h, sDVE.n)
                sDVE.inc(v.tensor_scalar(mrow[0][0:1, :], mrow[1][0:1, :],
                                         -1.0, None, ALU.mult))
            tok_ng = sDVE.n

            @block.tensor
            def _(t, tok_ng=tok_ng):
                t.wait_ge(sDVE.h, tok_ng)
                P.acquire(t, 2)
                t.matmul(P.bank(2)[:, 0:1], P.ones_row_f32,
                         mrow[0][0:1, 0:1], start=True,
                         stop=True).then_inc(sPE.h, 1)
            sPE.n += 1
            tok_bc = sPE.n

            @block.vector
            def _(v, h=h, tok_bc=tok_bc):
                v.wait_ge(sPE.h, tok_bc)
                v.wait_ge(sDVE.h, sDVE.n)
                sDVE.inc(v.tensor_copy(negm2[h][:, :], P.bank(2)[:, 0:1]))
            P.reader(2, sDVE)
        tok_negm = sDVE.n

        tp_tok = {0: 0, 1: 0}
        pv_tok = {0: 0, 1: 0}
        wo_last = [0]
        xr_toks = {}
        first_it = [True]

        def stage1(h, qt, p):
            c0, pb = h // 2, 64 * (h % 2)
            qo = qt * 128
            bA = 2 * p

            @block.tensor
            def _(t, c0=c0, pb=pb, qo=qo, bA=bA):
                P.acquire(t, bA)
                qsl = Q_fm[c0][pb:pb + 64, qo:qo + 128]
                t.matmul(P.bank(bA), qsl,
                         K_fm[c0][pb:pb + 64, qo:qo + 512],
                         start=True, stop=True).then_inc(sPE.h, 1)
                P.acquire(t, bA + 1)
                t.matmul(P.bank(bA + 1, w=128), qsl,
                         K_fm[c0][pb:pb + 64, qo + 512:qo + 640],
                         start=True, stop=True).then_inc(sPE.h, 1)
            sPE.n += 2
            tok_sc = sPE.n

            @block.scalar
            def _(a, h=h, p=p, bA=bA, tok_sc=tok_sc):
                if first_it[0]:
                    a.wait_ge(sDVE.h, tok_negm)
                    first_it[0] = False
                a.wait_ge(sPE.h, tp_tok[p])
                a.wait_ge(sPE.h, tok_sc)
                a.wait_ge(sACT.h, sACT.n)
                sACT.inc(a.activation(Pbuf2[p][:, :],
                                      P.bank(bA, w=640), AF.Exp,
                                      scale=0.125, bias=negm2[h][:, :]))
            tok_exp = sACT.n
            P.reader(bA, sACT)
            P.reader(bA + 1, sACT)

            @block.vector
            def _(v, p=p, qt=qt, tok_exp=tok_exp):
                v.wait_ge(sACT.h, tok_exp)
                v.wait_ge(sSM.h, tok_masks)
                v.wait_ge(sDVE.h, sDVE.n)
                sDVE.inc(v.scalar_tensor_tensor(
                    Pbuf2[p][:, 0:512], Pbuf2[p][:, 0:512], 1.0,
                    mA_sb[min(qt, 4)][:, :], ALU.mult, ALU.mult,
                    accum_out=srow[3 * p][:, :]))
                v.wait_ge(sDVE.h, sDVE.n)
                sDVE.inc(v.scalar_tensor_tensor(
                    Pbuf2[p][:, 512:640], Pbuf2[p][:, 512:640], 1.0,
                    mB_sb[:, :], ALU.mult, ALU.mult,
                    accum_out=srow[3 * p + 1][:, :]))
                v.wait_ge(sDVE.h, sDVE.n)
                sDVE.inc(v.tensor_tensor(srow[3 * p][:, :], srow[3 * p][:, :],
                                         srow[3 * p + 1][:, :], ALU.add))
                v.wait_ge(sDVE.h, sDVE.n)
                sDVE.inc(v.reciprocal(srow[3 * p][:, :], srow[3 * p][:, :]))
                v.wait_ge(sDVE.h, sDVE.n)
                sDVE.inc(v.tensor_scalar(Pbuf2[p][:, :], Pbuf2[p][:, :],
                                         srow[3 * p][:, :], None, ALU.mult))
            return sDVE.n

        def stage2a(h, qt, p, tok_norm):
            @block.tensor
            def _(t, p=p, tok_norm=tok_norm):
                t.wait_ge(sDVE.h, tok_norm)
                P.acquire(t, 4 + p)
                for k in range(5):
                    mm = t.transpose(
                        P.bank_bf(4 + p, w=640)[:, k * 128:(k + 1) * 128],
                        Pbuf2[p][:, k * 128:(k + 1) * 128], ident[:, :])
                sPE.inc(mm)
            tp_tok[p] = sPE.n
            ttok = sPE.n

            @block.vector
            def _(v, p=p, ttok=ttok):
                v.wait_ge(sPE.h, ttok)
                v.wait_ge(sDVE.h, sDVE.n)
                sDVE.inc(v.tensor_copy(PT2[p][:, :],
                                       P.bank_bf(4 + p, w=640)))
            P.reader(4 + p, sDVE)
            return sDVE.n

        def stage2b(h, qt, p, tok_pt):
            c0, pb = h // 2, 64 * (h % 2)
            qi = qt % 4

            @block.tensor
            def _(t, h=h, qt=qt, p=p, tok_pt=tok_pt):
                t.wait_ge(sDVE.h, tok_pt)
                P.acquire(t, 6 + p)
                for k in range(5):
                    mm = t.matmul(P.bank(6 + p, w=128)[0:64, :],
                                  V_tm[qt + k][:, h * 64:(h + 1) * 64],
                                  PT2[p][:, k * 128:(k + 1) * 128],
                                  start=(k == 0), stop=(k == 4))
                sPE.inc(mm)
            pv_tok[p] = sPE.n
            tok_pv = sPE.n

            @block.vector
            def _(v, p=p, c0=c0, pb=pb, qi=qi, tok_pv=tok_pv):
                v.wait_ge(sPE.h, tok_pv)
                v.wait_ge(sPE.h, wo_last[0])
                v.wait_ge(sDVE.h, sDVE.n)
                sDVE.inc(v.tensor_copy(
                    ao[c0][pb:pb + 64, qi * 128:(qi + 1) * 128],
                    P.bank(6 + p, w=128)[0:64, :]))
            P.reader(6 + p, sDVE)
            return sDVE.n

        def prefetch_x(qs):
            @block.gpsimd
            def _(g, qs=qs):
                g.wait_ge(sDVE.h, sDVE.n)
                g.wait_ge(sST.h, sST.n)
                g.wait_ge(sPE.h, sPE.n if qs == 0 else 0)
                for c in range(DC):
                    sLX.dma(g.dma_start(
                        out=x_sp2[c][:, :],
                        in_=x_d[c * 128:(c + 1) * 128,
                                512 + qs * 512:512 + (qs + 1) * 512]))
            xr_toks[qs] = sLX.n

        def wo_proj(qs, tok_ao):
            tok_xr = xr_toks[qs]
            for dc in range(DC):
                @block.tensor
                def _(t, dc=dc, qs=qs, tok_ao=tok_ao):
                    if dc == 0:
                        t.wait_ge(sDVE.h, tok_ao)
                        if qs == 0:
                            t.wait_ge(sDVE.h, tok_wo)
                    bank = 2 + dc % 2
                    P.acquire(t, bank)
                    for cin in range(DC):
                        mm = t.matmul(P.bank(bank),
                                      wo_bf[:, cin * 1024 + dc * 128:
                                            cin * 1024 + dc * 128 + 128],
                                      ao[cin][:, :],
                                      start=(cin == 0), stop=(cin == DC - 1))
                    sPE.inc(mm)
                tok_wo_mm = sPE.n

                @block.vector
                def _(v, dc=dc, tok=tok_wo_mm, tok_xr=tok_xr):
                    v.wait_ge(sPE.h, tok)
                    v.wait_ge(sLX.h, tok_xr)
                    bank = 2 + dc % 2
                    v.wait_ge(sDVE.h, sDVE.n)
                    v.scalar_tensor_tensor(
                        x_sp2[dc][:, :], P.bank(bank), bo_sb[:, dc:dc + 1],
                        x_sp2[dc][:, :], ALU.add, ALU.add).then_inc(sDVE.h, 1)
                sDVE.n += 1
                P.reader(2 + dc % 2, sDVE)
                tok_res = sDVE.n

                @block.gpsimd
                def _(g, dc=dc, qs=qs, tok=tok_res):
                    g.wait_ge(sDVE.h, tok)
                    sST.dma(g.dma_start(
                        out=xo_d[dc * 128:(dc + 1) * 128,
                                 qs * 512:(qs + 1) * 512],
                        in_=x_sp2[dc][:, :]))
            wo_last[0] = tok_wo_mm

        prefetch_x(0)
        pend1 = None
        pend2 = None
        it = 0

        def run2b(args):
            h2, qt2, p2, tok_pt = args
            tok_ao_e = stage2b(h2, qt2, p2, tok_pt)
            if qt2 % 4 == 3 and h2 == NHEAD - 1:
                pqs = qt2 // 4
                wo_proj(pqs, tok_ao_e)
                if pqs + 1 < 4:
                    prefetch_x(pqs + 1)

        for qs in range(4):
            for h in range(NHEAD):
                for qi in range(4):
                    qt = qs * 4 + qi
                    p = it % 2
                    it += 1
                    tn = stage1(h, qt, p)
                    np1 = (h, qt, p, tn)
                    np2 = None
                    if pend1 is not None:
                        tok_pt = stage2a(*pend1)
                        np2 = (pend1[0], pend1[1], pend1[2], tok_pt)
                    if pend2 is not None:
                        run2b(pend2)
                    pend1, pend2 = np1, np2
        tok_pt = stage2a(*pend1)
        run2b(pend2)
        run2b((pend1[0], pend1[1], pend1[2], tok_pt))



def build_moe():
    P = _mk_P()
    nc = P.nc
    aps = {
        "x": nc.dram_tensor("x", [D, TOWN], F32, kind="ExternalInput").ap(),
        "w1": nc.dram_tensor("w1", [NE, D, H1], F32, kind="ExternalInput").ap(),
        "w2": nc.dram_tensor("w2", [NE, H1, D], F32, kind="ExternalInput").ap(),
        "wg": nc.dram_tensor("wg", [128, 64], F32, kind="ExternalInput").ap(),
        "bg": nc.dram_tensor("bg", [8, 1], F32, kind="ExternalInput").ap(),
        "sel8": nc.dram_tensor("sel8", [8, 1024], BF16,
                               kind="ExternalInput").ap(),
        "xo": nc.dram_tensor("xo", [D, TOWN], F32, kind="ExternalOutput").ap(),
    }
    with P.es:
        block = P.es.enter_context(nc.Block())

        @block.gpsimd
        def _(g):
            P.init_consts(g)
        P.gate_consts(block)
        P._moe_wgr = None
        emit_moe(P, block, aps, tag="t")
        sST = P.sem("st")

        @block.gpsimd
        def _(g):
            g.wait_ge(sST.h, sST.n)
    return nc


def emit_moe(P, block, aps, tag=""):
    nc = P.nc
    x_d, w1_d, w2_d = aps["x"], aps["w1"], aps["w2"]
    wg_d, bg_d, sel8_d, xo_d = aps["wg"], aps["bg"], aps["sel8"], aps["xo"]
    xn_dram = nc.dram_tensor("xn_moe_" + tag, [D, TOWN], BF16).ap()

    A = P.A
    lb = P.lb
    wg_f32 = A.alloc(64, F32)
    if getattr(P, "_moe_wgr", None) is None:
        P._moe_wgr = P.sbuf(64, F32R)
    wg_r = P._moe_wgr
    bg_sb = A.alloc(1, F32)
    x_res = A.alloc(TOWN, F32, DC)
    xn_buf = A.alloc(1024, BF16, DC)
    mxn = A.alloc(1024, BF16, DC)
    hbuf = A.alloc(1024, BF16, 8)
    w1_st = A.alloc(1024, F32, 2)
    w1_bf = A.alloc(1024, BF16, 2)
    w2_st = A.alloc(1024, F32, 2)
    w2_ring = A.alloc(1024, BF16, 16)
    maskT = A.alloc(TOWN, BF16)
    lgsb = A.alloc(512, F32)
    lgtm = A.alloc(8, F32)
    rmx = A.alloc(1, F32)
    msk_tm = A.alloc(8, F32)
    idf32 = P.ident_f32
    mbc = A.alloc(1024, BF16)
    sel8 = A.alloc(1024, BF16)
    assert A.off <= A.hi, A.off

    sPE, sACT, sDVE = P.sem("pe"), P.sem("act"), P.sem("dve")
    sLX, sST = P.sem("lx"), P.sem("st")
    sLW = [P.sem("lw0"), P.sem("lw1")]
    sLW2 = [P.sem("lw2a"), P.sem("lw2b")]
    sSM = P.sem("sm")

    if True:
        sGP = P.sem("gp")

        @block.gpsimd
        def _(g):
            g.wait_ge(sGP.h, P._gate_val)
            sGP.inc(g.memset(maskT[:, :], 0.0))
            sGP.inc(g.memset(sel8[:, :], 0.0))
            sGP.inc(g.memset(lgsb[:, :], 0.0))
            g.wait_ge(sDVE.h, sDVE.n)
            g.wait_ge(sPE.h, sPE.n)
            for c in range(DC):
                sLX.dma(g.dma_start(out=x_res[c][:, :],
                                    in_=x_d[c * 128:(c + 1) * 128, :]))
        tok_ident = sGP.n
        tok_x = [(sLX, sLX.n)] * DC

        @block.sync
        def _(s):
            s.wait_ge(sGP.h, tok_ident)
            s.wait_ge(sDVE.h, sDVE.n)
            sSM.dma(s.dma_start(out=wg_f32[:, :], in_=wg_d[:, :]))
            sSM.dma(s.dma_start(out=bg_sb[0:8, :], in_=bg_d[:, :]))
            sSM.dma(s.dma_start(out=sel8[0:8, :], in_=sel8_d[:, :]))
        tok_wg = sSM.n

        @block.vector
        def _(v):
            v.wait_ge(sSM.h, tok_wg)
            v.wait_ge(sDVE.h, sDVE.n)
            v.tensor_copy(wg_r[:, :], wg_f32[:, :]).then_inc(sDVE.h, 1)
        sDVE.n += 1

        def logits_cb(sp, tok_pe):
            @block.vector
            def _(v, sp=sp, tok=tok_pe):
                v.wait_ge(sPE.h, tok)
                v.wait_ge(sDVE.h, sDVE.n)
                v.tensor_scalar(lgsb[0:8, :], P.bank(3)[0:8, :],
                                bg_sb[0:8, :], None, ALU.add).then_inc(sDVE.h, 1)
            sDVE.n += 1
            tok_lg = sDVE.n
            P.reader(3, sDVE)

            for tc in range(4):
                @block.tensor
                def _(t, sp=sp, tc=tc, tok=tok_lg):
                    if tc == 0:
                        t.wait_ge(sDVE.h, tok)
                        t.wait_ge(sGP.h, tok_ident)
                    P.acquire(t, 2)
                    sPE.inc(t.transpose(P.bank(2)[:, 0:8],
                                        lgsb[0:8, tc * 128:(tc + 1) * 128],
                                        idf32[0:8, 0:8]))
                tok_tp = sPE.n

                @block.vector
                def _(v, tok=tok_tp):
                    v.wait_ge(sPE.h, tok)
                    v.wait_ge(sDVE.h, sDVE.n)
                    sDVE.inc(v.tensor_copy(lgtm[:, :], P.bank(2)[:, 0:8]))
                    v.wait_ge(sDVE.h, sDVE.n)
                    sDVE.inc(v.reduce_max(rmx[:, :], lgtm[:, :], axis=AX))
                    v.wait_ge(sDVE.h, sDVE.n)
                    sDVE.inc(v.tensor_scalar(msk_tm[:, :], lgtm[:, :],
                                             rmx[:, :], None, ALU.is_ge))
                tok_msk = sDVE.n
                P.reader(2, sDVE)

                @block.tensor
                def _(t, tok=tok_msk):
                    t.wait_ge(sDVE.h, tok)
                    P.acquire(t, 2)
                    sPE.inc(t.transpose(P.bank(2)[0:8, 0:128], msk_tm[:, :],
                                        idf32[:, :]))
                tok_tp2 = sPE.n

                @block.scalar
                def _(a, sp=sp, tc=tc, tok=tok_tp2):
                    a.wait_ge(sPE.h, tok)
                    a.wait_ge(sACT.h, sACT.n)
                    sACT.inc(a.activation(
                        maskT[0:8, sp * 512 + tc * 128:sp * 512 + (tc + 1) * 128],
                        P.bank(2)[0:8, 0:128], AF.Copy))
                P.reader(2, sACT)

        wg_cols = [wg_r[:, c * 8:(c + 1) * 8] for c in range(DC)]
        xn_all = [A.base.bitcast(BF16)[:, 0:1] for _ in range(1)]
        xn_full_view = [None] * DC
        ln_store_toks = []
        for sp in range(4):
            xv = [x_res[c][:, sp * 512:(sp + 1) * 512] for c in range(DC)]
            xnv = [mxn[c][:, 0:512] for c in range(DC)]
            xn_war = None
            if ln_store_toks:
                xn_war = [(sST, ln_store_toks[-1])]
            tok_xn = emit_ln(
                P, block, xv, xnv, 512, lb, tok_x=tok_x, mode="moe",
                logits=(wg_cols, 3,
                        (lambda _s, tok, SP=sp: logits_cb(SP, tok))),
                xn_war=xn_war)

            @block.gpsimd
            def _(g, sp=sp, tok=tok_xn):
                g.wait_ge(tok[0].h, tok[1])
                for c in range(DC):
                    sST.dma(g.dma_start(
                        out=xn_dram[c * 128:(c + 1) * 128,
                                    sp * 512:(sp + 1) * 512],
                        in_=mxn[c][:, 0:512]))
            ln_store_toks.append(sST.n)
        tok_ln_done = sST.n
        tok_mask_done = (sACT, sACT.n)

        w1_conv = {}
        w1_mm = {}
        w2_conv = {}
        xn_load_toks = {}
        mxn_toks = {}
        last_mm1 = [0]
        last_mm2 = [0]
        mm2_by_unit = {}
        last_mxn_dve = [0]
        NHCG = 4

        units = [(ts, e, hcg) for ts in range(2) for e in range(NE)
                 for hcg in range(NHCG)]

        for ui, (ts, e, hcg) in enumerate(units):
            first_of_ts = (e == 0 and hcg == 0)
            if first_of_ts:
                @block.gpsimd
                def _(g, ts=ts):
                    g.wait_ge(sST.h, tok_ln_done)
                    if ts > 0:
                        g.wait_ge(sPE.h, sPE.n)
                    for c in range(DC):
                        sLX.dma(g.dma_start(
                            out=xn_buf[c][:, :],
                            in_=xn_dram[c * 128:(c + 1) * 128,
                                        ts * 1024:(ts + 1) * 1024]))
                xn_load_toks[ts] = sLX.n

            def emit_mxn(ts2, e2, gate_mm1):
                @block.tensor
                def _(t, ts2=ts2, e2=e2):
                    s, v = tok_mask_done
                    t.wait_ge(s.h, v)
                    t.wait_ge(sSM.h, tok_wg)
                    t.wait_ge(sLX.h, xn_load_toks[ts2])
                    for half in range(2):
                        P.acquire(t, 4 + half)
                        sPE.inc(t.matmul(
                            P.bank(4 + half),
                            sel8[0:8, e2 * 128:(e2 + 1) * 128],
                            maskT[0:8, ts2 * 1024 + half * 512:
                                  ts2 * 1024 + (half + 1) * 512],
                            start=True, stop=True))
                tok_mbcmm = sPE.n

                @block.scalar
                def _(a, tok=tok_mbcmm):
                    a.wait_ge(sPE.h, tok)
                    a.wait_ge(sDVE.h, last_mxn_dve[0])
                    for half in range(2):
                        a.wait_ge(sACT.h, sACT.n)
                        sACT.inc(a.activation(mbc[:, half * 512:(half + 1) * 512],
                                              P.bank(4 + half), AF.Copy))
                P.reader(4, sACT)
                P.reader(5, sACT)
                tok_mbc = sACT.n

                for c in range(DC):
                    @block.vector
                    def _(v, c=c, tok=tok_mbc, gate_mm1=gate_mm1):
                        if c == 0:
                            v.wait_ge(sACT.h, tok)
                            v.wait_ge(sPE.h, gate_mm1)
                        v.wait_ge(sDVE.h, sDVE.n)
                        v.tensor_tensor(mxn[c][:, :], xn_buf[c][:, :],
                                        mbc[:, :], ALU.mult).then_inc(sDVE.h, 1)
                    sDVE.n += 1
                mxn_toks[(ts2, e2)] = sDVE.n
                last_mxn_dve[0] = sDVE.n

            if (ts, e) not in mxn_toks:
                emit_mxn(ts, e, last_mm1[0])

            gelu_last = [0]
            gelu_toks = []
            w2c_toks = []
            for j in range(8):
                hc = hcg * 8 + j
                par = j % 2
                b0 = 6 if par == 0 else 4

                @block.sync
                def _(s, e=e, hc=hc, par=par):
                    prev = w1_conv.get(("last", par))
                    if prev:
                        s.wait_ge(sGP.h, prev)
                    sLW[par].dma(s.dma_start(
                        out=w1_st[par][:, :].rearrange("p (c h) -> p c h", c=DC),
                        in_=w1_d[e, :, hc * 128:(hc + 1) * 128].rearrange(
                            "(c p) h -> p c h", p=128)))
                tok_l = sLW[par].n

                @block.gpsimd
                def _(g, par=par, tok_l=tok_l):
                    g.wait_ge(sLW[par].h, tok_l)
                    prevmm = w1_mm.get(("last", par))
                    if prevmm:
                        g.wait_ge(sPE.h, prevmm)
                    g.wait_ge(sGP.h, sGP.n)
                    g.tensor_copy(w1_bf[par][:, :],
                                  w1_st[par][:, :]).then_inc(sGP.h, 1)
                sGP.n += 1
                w1_conv[("last", par)] = sGP.n
                tok_conv = sGP.n

                sub_toks = []

                @block.tensor
                def _(t, ts=ts, e=e, j=j, par=par, b0=b0, tok=tok_conv):
                    t.wait_ge(sGP.h, tok)
                    t.wait_ge(sDVE.h, mxn_toks[(ts, e)])
                    P.acquire(t, b0)
                    P.acquire(t, b0 + 1)
                    for cin in range(DC):
                        for sub in range(2):
                            mm = t.matmul(
                                P.bank(b0 + sub),
                                w1_bf[par][:, cin * 128:(cin + 1) * 128],
                                mxn[cin][:, sub * 512:(sub + 1) * 512],
                                start=(cin == 0), stop=(cin == DC - 1))
                            if cin == DC - 1:
                                sPE.inc(mm)
                                sub_toks.append(sPE.n)
                w1_mm[("last", par)] = sPE.n

                @block.scalar
                def _(a, j=j, b0=b0, sub_toks=tuple(sub_toks)):
                    for sub in range(2):
                        a.wait_ge(sPE.h, sub_toks[sub])
                        if sub == 0:
                            a.wait_ge(sPE.h, last_mm2[0])
                        a.wait_ge(sACT.h, sACT.n)
                        sACT.inc(a.activation(
                            hbuf[j][:, sub * 512:(sub + 1) * 512],
                            P.bank(b0 + sub), AF.Gelu_apprx_tanh))
                P.reader(b0, sACT)
                P.reader(b0 + 1, sACT)
                gelu_last[0] = sACT.n
                gelu_toks.append(sACT.n)

                @block.sync
                def _(s, e=e, hc=hc, par=par):
                    prev = w2_conv.get(("last", par))
                    if prev:
                        s.wait_ge(sDVE.h, prev)
                    sLW2[par].dma(s.dma_start(
                        out=w2_st[par][:, :],
                        in_=w2_d[e, hc * 128:(hc + 1) * 128, :]))
                tok_l2 = sLW2[par].n

                @block.vector
                def _(v, j=j, par=par, tok_l2=tok_l2, ui=ui):
                    v.wait_ge(sLW2[par].h, tok_l2)
                    if ui >= 2:
                        v.wait_ge(sPE.h, mm2_by_unit[ui - 2])
                    v.wait_ge(sDVE.h, sDVE.n)
                    v.tensor_copy(w2_ring[(ui % 2) * 8 + j][:, :],
                                  w2_st[par][:, :]).then_inc(sDVE.h, 1)
                sDVE.n += 1
                w2_conv[("last", par)] = sDVE.n
                w2c_toks.append(sDVE.n)
            tok_gelu = sACT.n
            ringbase = (ui % 2) * 8
            tok_w2 = sDVE.n

            if hcg == NHCG - 1 and e < NE - 1:
                cur_mm1 = max(w1_mm.get(("last", 0), 0),
                              w1_mm.get(("last", 1), 0))
                emit_mxn(ts, e + 1, cur_mm1)

            for dgrp, dcs in ((0, tuple(range(6))), (1, (6, 7))):
                for sub in range(2):
                    dmm_tok = {}

                    @block.tensor
                    def _(t, ts=ts, sub=sub, dcs=dcs, dgrp=dgrp,
                          ringbase=ringbase, gelu_toks=tuple(gelu_toks),
                          w2c_toks=tuple(w2c_toks)):
                        first = dgrp == 0 and sub == 0
                        for j in range(8):
                            if first:
                                t.wait_ge(sACT.h, gelu_toks[j])
                                t.wait_ge(sDVE.h, w2c_toks[j])
                            for dc in dcs:
                                bank = dc if dgrp == 0 else dc - 6
                                if j == 0:
                                    P.acquire(t, bank)
                                mm = t.matmul(
                                    P.bank(bank),
                                    w2_ring[ringbase + j][:, dc * 128:
                                                          (dc + 1) * 128],
                                    hbuf[j][:, sub * 512:(sub + 1) * 512],
                                    start=(j == 0), stop=(j == 7))
                                if j == 7:
                                    sPE.inc(mm)
                                    dmm_tok[dc] = sPE.n

                    for dc in dcs:
                        bank = dc if dgrp == 0 else dc - 6

                        @block.vector
                        def _(v, ts=ts, dc=dc, sub=sub, bank=bank,
                              tok=dmm_tok[dc]):
                            v.wait_ge(sPE.h, tok)
                            v.wait_ge(sDVE.h, sDVE.n)
                            v.scalar_tensor_tensor(
                                x_res[dc][:, ts * 1024 + sub * 512:
                                          ts * 1024 + (sub + 1) * 512],
                                P.bank(bank), 0.5,
                                x_res[dc][:, ts * 1024 + sub * 512:
                                          ts * 1024 + (sub + 1) * 512],
                                ALU.mult, ALU.add).then_inc(sDVE.h, 1)
                        sDVE.n += 1
                        P.reader(bank, sDVE)
            last_mm2[0] = sPE.n
            mm2_by_unit[ui] = sPE.n
            last_mm1[0] = w1_mm[("last", 0)]
            if ("last", 1) in w1_mm:
                last_mm1[0] = max(last_mm1[0], w1_mm[("last", 1)])

        @block.gpsimd
        def _(g):
            g.wait_ge(sDVE.h, sDVE.n)
            for c in range(DC):
                sST.dma(g.dma_start(out=xo_d[c * 128:(c + 1) * 128, :],
                                    in_=x_res[c][:, :]))
            g.wait_ge(sST.h, sST.n)



def build_l1attn():
    P = _mk_P()
    nc = P.nc
    aps = {
        "x": nc.dram_tensor("x", [D, TOWN], F32, kind="ExternalInput").ap(),
        "wq": nc.dram_tensor("wq", [D, 512], F32, kind="ExternalInput").ap(),
        "wk": nc.dram_tensor("wk", [D, 512], F32, kind="ExternalInput").ap(),
        "wv": nc.dram_tensor("wv", [D, 512], F32, kind="ExternalInput").ap(),
        "wo": nc.dram_tensor("wo", [D, D], F32, kind="ExternalInput").ap(),
        "bq": nc.dram_tensor("bq", [128, 4], F32, kind="ExternalInput").ap(),
        "bk": nc.dram_tensor("bk", [128, 4], F32, kind="ExternalInput").ap(),
        "bo": nc.dram_tensor("bo", [128, 8], F32, kind="ExternalInput").ap(),
        "bvr": nc.dram_tensor("bvr", [1, 512], F32, kind="ExternalInput").ap(),
        "nw": nc.dram_tensor("nw", [128, 8], F32, kind="ExternalInput").ap(),
        "mcol": nc.dram_tensor("mcol", [128, 1], F32, kind="ExternalInput").ap(),
        "maskB": nc.dram_tensor("maskB", [128, 128], BF16,
                                kind="ExternalInput").ap(),
        "xo": nc.dram_tensor("xo", [D, TOWN], F32, kind="ExternalOutput").ap(),
    }
    with P.es:
        block = P.es.enter_context(nc.Block())

        @block.gpsimd
        def _(g):
            P.init_consts(g)
        P.gate_consts(block)
        emit_l1attn(P, block, aps)
        sST = P.sem("st")

        @block.gpsimd
        def _(g):
            g.wait_ge(sST.h, sST.n)
    return nc


def emit_l1attn(P, block, aps, tag=""):
    nc = P.nc
    x_d, wq_d, wk_d, wv_d, wo_d = (aps["x"], aps["wq"], aps["wk"], aps["wv"],
                                   aps["wo"])
    bq_d, bk_d, bo_d, bvr_d = aps["bq"], aps["bk"], aps["bo"], aps["bvr"]
    nw_d, mcol_d, mB_d, xo_d = (aps["nw"], aps["mcol"], aps["maskB"],
                                aps["xo"])
    x1_dram = nc.dram_tensor("x1_scr" + tag, [D, TOWN], F32).ap()
    cc_xn_in = nc.dram_tensor("cc_xn_in", [D, TOWN], BF16).ap()
    cc_xn_out = nc.dram_tensor("cc_xn_out", [2 * D, TOWN], BF16).ap()
    cc_ao_in = nc.dram_tensor("cc_ao_in", [512, 2 * TOWN], BF16).ap()
    cc_ao_out = nc.dram_tensor("cc_ao_out", [D, 2 * TOWN], BF16).ap()

    SEQ = 2 * TOWN
    A = P.A
    lb = P.lb
    K_fm = A.alloc(SEQ, BF16, 4)
    V_tm = A.alloc(512, BF16, 32)
    bq_sb = A.alloc(4, F32)
    bk_sb = A.alloc(4, F32)
    bo_sb = A.alloc(8, F32)
    nw_sb = A.alloc(8, F32)
    mcol = A.alloc(1, F32)
    omcol = A.alloc(1, F32)
    bvB = A.alloc(512, BF16)
    mB_sb = A.alloc(128, BF16)
    ident = P.ident_bf
    mrow = A.alloc(1, F32, 10)
    ssum = A.alloc(1, F32, 9 * 2)
    negm = A.alloc(1, F32, 8)
    nrm2 = A.alloc(1, F32, 16)
    aotile = A.alloc(128, BF16, 2)
    markP = A.mark()

    Q_fm = A.alloc(SEQ, BF16, 4)
    markR1 = A.mark()
    A.reset(markP)
    _skip = A.alloc(SEQ, BF16, 4)
    markR2 = A.mark()
    wqkv_bf = A.alloc(4096, BF16, 3)
    w_st0 = A.alloc(D, F32)
    markR2_end = A.mark()
    A.reset(markR2)
    Pbuf = A.alloc(4096, BF16, 2)
    PT = A.alloc(4096, BF16, 2)
    sqn = A.alloc(512, F32, 1)
    markR2b = A.mark()
    A.reset(markR2)
    wo_bf = A.alloc(8 * D, BF16)
    wo_st = A.alloc(D, F32)
    gA = A.alloc(512, BF16, DC)
    gB = A.alloc(512, BF16, DC)
    markR2c = A.mark()
    A.off = max(markR2_end, markR2b, markR2c)
    x_sp = A.alloc(512, F32, DC)
    xn_sp = A.alloc(512, BF16, DC)
    assert A.off <= A.hi, A.off
    bvr_sb_view = w_st0[0:1, 0:512]

    sPE, sACT, sDVE = P.sem("pe"), P.sem("act"), P.sem("dve")
    sLX, sST = P.sem("lx"), P.sem("st")
    sLW = [P.sem("lw0"), P.sem("lw1")]
    sSM = P.sem("sm")
    sCC = P.sem("cc")

    if True:
        sGP = P.sem("gp")
        tok_ident = P._gate_val

        @block.sync
        def _(s):
            s.wait_ge(sDVE.h, sDVE.n)
            s.wait_ge(sPE.h, sPE.n)
            for dst, srcd in ((bq_sb[:, 0:4], bq_d), (bk_sb[:, 0:4], bk_d),
                              (bo_sb, bo_d), (nw_sb, nw_d), (mcol, mcol_d)):
                sSM.dma(s.dma_start(out=dst[:, :], in_=srcd[:, :]))
            sSM.dma(s.dma_start(out=bvr_sb_view[:, :], in_=bvr_d[:, :]))
            sSM.dma(s.dma_start(out=mB_sb[:, :], in_=mB_d[:, :]))
        tok_small = sSM.n

        @block.vector
        def _(v):
            v.wait_ge(sSM.h, tok_small)
            v.wait_ge(sDVE.h, sDVE.n)
            v.tensor_scalar(omcol[:, :], mcol[:, :], -1.0, 1.0, ALU.mult,
                            ALU.add).then_inc(sDVE.h, 1)
        sDVE.n += 1
        tok_om = sDVE.n

        @block.tensor
        def _(t):
            t.wait_ge(sSM.h, tok_small)
            P.acquire(t, 0)
            sPE.inc(t.matmul(P.bank(0), P.ones_row_f32, bvr_sb_view[:, :],
                             start=True, stop=True))
        tok_bvmm = sPE.n

        @block.scalar
        def _(a):
            a.wait_ge(sPE.h, tok_bvmm)
            a.wait_ge(sACT.h, sACT.n)
            sACT.inc(a.activation(bvB[:, :], P.bank(0), AF.Copy))
        P.reader(0, sACT)
        tok_bvB = sACT.n

        nw_cols = [nw_sb[:, c:c + 1] for c in range(DC)]
        for sp in range(4):
            @block.gpsimd
            def _(g, sp=sp):
                if sp > 0:
                    g.wait_ge(sDVE.h, sDVE.n)
                    g.wait_ge(sST.h, sST.n)
                for c in range(DC):
                    sLX.dma(g.dma_start(out=x_sp[c][:, :],
                                        in_=x_d[c * 128:(c + 1) * 128,
                                                sp * 512:(sp + 1) * 512]))
            tok_x = [(sLX, sLX.n)] * DC

            tok_xn = emit_ln(P, block, x_sp, xn_sp, 512, lb, tok_x=tok_x,
                             mode="rms2", nw_cols=nw_cols)

            @block.gpsimd
            def _(g, sp=sp, tok=tok_xn):
                g.wait_ge(tok[0].h, tok[1])
                for c in range(DC):
                    sST.dma(g.dma_start(
                        out=cc_xn_in[c * 128:(c + 1) * 128,
                                     sp * 512:(sp + 1) * 512],
                        in_=xn_sp[c][:, :]))
                    sST.dma(g.dma_start(
                        out=x1_dram[c * 128:(c + 1) * 128,
                                    sp * 512:(sp + 1) * 512],
                        in_=x_sp[c][:, :]))
        tok_phaseA = sST.n

        @block.gpsimd
        def _(g):
            g.wait_ge(sST.h, tok_phaseA)
            g.collective_compute(
                "AllGather", ALU.bypass,
                replica_groups=[[0, 1], [2, 3], [4, 5], [6, 7]],
                ins=[cc_xn_in[:, :].opt()],
                outs=[cc_xn_out[:, :].opt()]).then_inc(sCC.h, 1)
        sCC.n += 1
        tok_cc1 = sCC.n

        tok_wq = load_weights_generic(P, block, wqkv_bf[0], w_st0, wq_d, 512,
                                      sLW, gate=(sACT, tok_bvB))
        tok_wk = load_weights_generic(P, block, wqkv_bf[1], w_st0, wk_d, 512,
                                      sLW, gate=None)
        tok_wv = load_weights_generic(P, block, wqkv_bf[2], w_st0, wv_d, 512,
                                      sLW, gate=None)
        qkv_mm_last = [0]
        for sp in range(8):
            @block.gpsimd
            def _(g, sp=sp):
                g.wait_ge(sCC.h, tok_cc1)
                if sp > 0:
                    g.wait_ge(sPE.h, qkv_mm_last[0])
                    g.wait_ge(sST.h, sST.n)
                for c in range(DC):
                    row0 = c * 128 if sp < 4 else D + c * 128
                    col0 = (sp % 4) * 512
                    sLX.dma(g.dma_start(
                        out=xn_sp[c][:, :].bitcast(BF16)[:, 0:512],
                        in_=cc_xn_out[row0:row0 + 128, col0:col0 + 512]))
            tok_xnl = sLX.n

            for which, wslot, bias in ((0, 0, bq_sb), (1, 1, bk_sb)):
                for co in range(4):
                    bank = (2 * co + which) % 4

                    @block.tensor
                    def _(t, sp=sp, which=which, wslot=wslot, co=co, bank=bank,
                          tok_xnl=tok_xnl):
                        if co == 0 and which == 0:
                            t.wait_ge(sLX.h, tok_xnl)
                            if sp == 0:
                                t.wait_ge(sDVE.h, tok_wv)
                        P.acquire(t, bank)
                        for cin in range(DC):
                            mm = t.matmul(
                                P.bank(bank),
                                wqkv_bf[wslot][:, cin * 512 + co * 128:
                                               cin * 512 + co * 128 + 128],
                                xn_sp[cin][:, :].bitcast(BF16)[:, 0:512],
                                start=(cin == 0), stop=(cin == DC - 1))
                        sPE.inc(mm)
                    mtok = sPE.n

                    @block.vector
                    def _(v, sp=sp, which=which, co=co, bank=bank, mtok=mtok,
                          bias=bias):
                        v.wait_ge(sPE.h, mtok)
                        v.wait_ge(sDVE.h, sDVE.n)
                        dst = (Q_fm if which == 0 else K_fm)[co]
                        v.tensor_scalar(dst[:, sp * 512:(sp + 1) * 512],
                                        P.bank(bank), bias[:, co:co + 1],
                                        None, ALU.add).then_inc(sDVE.h, 1)
                    sDVE.n += 1
                    P.reader(bank, sDVE)

            for tc in range(4):
                bank = 4 + tc % 2

                @block.tensor
                def _(t, sp=sp, tc=tc, bank=bank):
                    P.acquire(t, bank)
                    for cin in range(DC):
                        mm = t.matmul(
                            P.bank(bank),
                            xn_sp[cin][:, :].bitcast(BF16)[:, tc * 128:
                                                           (tc + 1) * 128],
                            wqkv_bf[2][:, cin * 512:(cin + 1) * 512],
                            start=(cin == 0), stop=(cin == DC - 1))
                    sPE.inc(mm)
                vtok = sPE.n

                @block.vector
                def _(v, sp=sp, tc=tc, bank=bank, vtok=vtok):
                    v.wait_ge(sPE.h, vtok)
                    v.wait_ge(sACT.h, tok_bvB)
                    v.wait_ge(sDVE.h, sDVE.n)
                    v.tensor_tensor(V_tm[sp * 4 + tc][:, :], P.bank(bank),
                                    bvB[:, :], ALU.add).then_inc(sDVE.h, 1)
                sDVE.n += 1
                P.reader(bank, sDVE)
            qkv_mm_last[0] = sPE.n
        tok_qkv = (sDVE, sDVE.n)

        sqr = P.lb.sq
        sGPq = P.sem("gp")

        nrm_dve = [0]
        sq_toks = []
        mm_toks2 = []
        for which, F in ((0, Q_fm), (1, K_fm)):
            for c0 in range(4):
                for sp8 in range(8):
                    idx = which * 32 + c0 * 8 + sp8
                    par = idx % 2

                    @block.vector
                    def _(v, which=which, c0=c0, sp8=sp8, par=par):
                        if not sq_toks:
                            s, vv = tok_qkv
                            v.wait_ge(s.h, vv)
                        if len(mm_toks2) >= 2:
                            v.wait_ge(sPE.h, mm_toks2[-2])
                        v.wait_ge(sDVE.h, sDVE.n)
                        v.tensor_tensor(
                            sqr[par][:, :], F[c0][:, sp8 * 512:(sp8 + 1) * 512],
                            F[c0][:, sp8 * 512:(sp8 + 1) * 512],
                            ALU.mult).then_inc(sDVE.h, 1)
                    sDVE.n += 1
                    sq_toks.append(sDVE.n)
                    tok_sq = sDVE.n

                    for pbi in (0, 64):
                        hloc = c0 * 2 + (pbi // 64)
                        bank = 2 + (idx % 2)

                        @block.tensor
                        def _(t, par=par, pbi=pbi, bank=bank, tok_sq=tok_sq):
                            t.wait_ge(sDVE.h, tok_sq)
                            P.acquire(t, bank)
                            t.matmul(P.bank(bank)[0:1, :],
                                     P.ones_col[pbi:pbi + 64, :],
                                     sqr[par][pbi:pbi + 64, :],
                                     start=True, stop=True).then_inc(sPE.h, 1)
                        sPE.n += 1
                        tok_mm = sPE.n

                        @block.vector
                        def _(v, which=which, hloc=hloc, bank=bank,
                              tok_mm=tok_mm, first=(sp8 == 0)):
                            v.wait_ge(sPE.h, tok_mm)
                            v.wait_ge(sDVE.h, sDVE.n)
                            sDVE.inc(v.reduce_max(mrow[8][0:1, :],
                                                  P.bank(bank)[0:1, :], axis=AX))
                            v.wait_ge(sDVE.h, sDVE.n)
                            dst = nrm2[which * 8 + hloc][0:1, :]
                            if first:
                                sDVE.inc(v.tensor_copy(dst, mrow[8][0:1, :]))
                            else:
                                sDVE.inc(v.tensor_tensor(dst, dst,
                                                         mrow[8][0:1, :],
                                                         ALU.max))
                        P.reader(bank, sDVE)
                    mm_toks2.append(sPE.n)
        tok_nrm = sDVE.n

        negm_toks = []
        for h in range(8):
            @block.vector
            def _(v, h=h):
                if h == 0:
                    v.wait_ge(sDVE.h, tok_nrm)
                v.wait_ge(sDVE.h, sDVE.n)
                sDVE.inc(v.tensor_tensor(mrow[8][0:1, :], nrm2[h][0:1, :],
                                         nrm2[8 + h][0:1, :], ALU.mult))
            tok_t = sDVE.n

            @block.scalar
            def _(a, h=h, tok_t=tok_t):
                a.wait_ge(sDVE.h, tok_t)
                a.wait_ge(sACT.h, sACT.n)
                sACT.inc(a.activation(mrow[9][0:1, :], mrow[8][0:1, :], AF.Ln))
                a.wait_ge(sACT.h, sACT.n)
                sACT.inc(a.activation(mrow[9][0:1, :], mrow[9][0:1, :], AF.Exp,
                                      scale=0.5, bias=LN_EIGHTH))
            tok_mh = sACT.n

            @block.vector
            def _(v, h=h, tok_mh=tok_mh):
                v.wait_ge(sACT.h, tok_mh)
                v.wait_ge(sDVE.h, sDVE.n)
                sDVE.inc(v.tensor_scalar(mrow[8][0:1, :], mrow[9][0:1, :],
                                         -1.0, None, ALU.mult))
            tok_ng = sDVE.n

            @block.tensor
            def _(t, h=h, tok_ng=tok_ng):
                t.wait_ge(sDVE.h, tok_ng)
                P.acquire(t, 2)
                t.matmul(P.bank(2)[:, 0:1], P.ones_row_f32,
                         mrow[8][0:1, 0:1], start=True,
                         stop=True).then_inc(sPE.h, 1)
            sPE.n += 1
            tok_bc2 = sPE.n

            @block.vector
            def _(v, h=h, tok_bc2=tok_bc2):
                v.wait_ge(sPE.h, tok_bc2)
                v.wait_ge(sDVE.h, sDVE.n)
                sDVE.inc(v.tensor_copy(negm[h][:, :], P.bank(2)[:, 0:1]))
            P.reader(2, sDVE)
            negm_toks.append(sDVE.n)
        tok_negm = sDVE.n

        tp_tok = {0: 0, 1: 0}
        pv_tok = {0: 0, 1: 0}
        ao_st = {0: 0, 1: 0}
        sSTa = [P.sem("sta0"), P.sem("sta1")]
        first_iter = [True]
        act_pred = [sACT.n]

        def stage1(h, qt, p):
            c0, pb = h // 2, 64 * (h % 2)
            qo = qt * 128
            nfull, rem = qo // 512, qo % 512
            spans = [(k * 512, 512) for k in range(nfull)]
            if rem:
                spans.append((nfull * 512, rem))
            spans.append((qo, 128))
            nspan = len(spans)
            sc_toks = []
            for si, (k0, w) in enumerate(spans):
                bank = 2 * p + (si % 2)

                @block.tensor
                def _(t, si=si, k0=k0, w=w, bank=bank, c0=c0, pb=pb, qo=qo):
                    if first_iter[0] and si == 0:
                        t.wait_ge(sDVE.h, tok_negm)
                        first_iter[0] = False
                    P.acquire(t, bank)
                    if si >= 2:
                        t.wait_ge(sACT.h, act_pred[0] + si - 1)
                    t.matmul(P.bank(bank, w=w),
                             Q_fm[c0][pb:pb + 64, qo:qo + 128],
                             K_fm[c0][pb:pb + 64, k0:k0 + w],
                             start=True, stop=True).then_inc(sPE.h, 1)
                sPE.n += 1
                sc_toks.append(sPE.n)

            @block.scalar
            def _(a, h=h, p=p, spans=tuple(spans), sc_toks=tuple(sc_toks)):
                a.wait_ge(sPE.h, tp_tok[p])
                for si, (k0, w) in enumerate(spans):
                    bank = 2 * p + (si % 2)
                    a.wait_ge(sPE.h, sc_toks[si])
                    a.wait_ge(sACT.h, sACT.n)
                    kw = {}
                    if si < len(spans) - 1:
                        kw["accum_out"] = ssum[p * 9 + si][:, :]
                    sACT.inc(a.activation(Pbuf[p][:, k0:k0 + w],
                                          P.bank(bank, w=w), AF.Exp,
                                          scale=0.125, bias=negm[h][:, :],
                                          **kw))
            tok_exp = sACT.n
            assert tok_exp == act_pred[0] + nspan, (tok_exp, act_pred[0], nspan)
            act_pred[0] = tok_exp
            for si in range(nspan):
                P.reader(2 * p + (si % 2), sACT)

            @block.vector
            def _(v, p=p, qo=qo, nspan=nspan, tok_exp=tok_exp):
                v.wait_ge(sACT.h, tok_exp)
                v.wait_ge(sDVE.h, sDVE.n)
                sDVE.inc(v.scalar_tensor_tensor(
                    Pbuf[p][:, qo:qo + 128], Pbuf[p][:, qo:qo + 128], 1.0,
                    mB_sb[:, :], ALU.mult, ALU.mult,
                    accum_out=ssum[p * 9 + nspan - 1][:, :]))
                for si in range(1, nspan):
                    v.wait_ge(sDVE.h, sDVE.n)
                    sDVE.inc(v.tensor_tensor(ssum[p * 9][:, :], ssum[p * 9][:, :],
                                             ssum[p * 9 + si][:, :], ALU.add))
                v.wait_ge(sDVE.h, sDVE.n)
                sDVE.inc(v.reciprocal(ssum[p * 9][:, :], ssum[p * 9][:, :]))
                v.wait_ge(sDVE.h, sDVE.n)
                sDVE.inc(v.tensor_scalar(Pbuf[p][:, 0:qo + 128],
                                         Pbuf[p][:, 0:qo + 128],
                                         ssum[p * 9][:, :], None, ALU.mult))
            return sDVE.n

        def stage2a(h, qt, p, tok_norm):
            nkc = qt + 1
            for grp in range(0, nkc, 8):
                gn = min(8, nkc - grp)

                @block.tensor
                def _(t, grp=grp, gn=gn, p=p, tok_norm=tok_norm):
                    if grp == 0:
                        t.wait_ge(sDVE.h, tok_norm)
                    P.acquire(t, 4 + p)
                    for k in range(gn):
                        mm = t.transpose(
                            P.bank_bf(4 + p)[:, k * 128:(k + 1) * 128],
                            Pbuf[p][:, (grp + k) * 128:(grp + k + 1) * 128],
                            ident[:, :])
                    sPE.inc(mm)
                ttok = sPE.n

                @block.vector
                def _(v, grp=grp, gn=gn, p=p, ttok=ttok):
                    v.wait_ge(sPE.h, ttok)
                    v.wait_ge(sDVE.h, sDVE.n)
                    sDVE.inc(v.tensor_copy(PT[p][:, grp * 128:(grp + gn) * 128],
                                           P.bank_bf(4 + p)[:, 0:gn * 128]))
                P.reader(4 + p, sDVE)
            tp_tok[p] = sPE.n
            return sDVE.n

        def stage2b(h, qt, p, tok_pt):
            nkc = qt + 1

            @block.tensor
            def _(t, h=h, p=p, nkc=nkc, tok_pt=tok_pt):
                t.wait_ge(sDVE.h, tok_pt)
                P.acquire(t, 6 + p)
                for kc in range(nkc):
                    mm = t.matmul(P.bank(6 + p, w=128)[0:64, :],
                                  V_tm[kc][:, h * 64:(h + 1) * 64],
                                  PT[p][:, kc * 128:(kc + 1) * 128],
                                  start=(kc == 0), stop=(kc == nkc - 1))
                sPE.inc(mm)
            pv_tok[p] = sPE.n
            tok_pv = sPE.n

            @block.vector
            def _(v, p=p, tok_pv=tok_pv):
                v.wait_ge(sPE.h, tok_pv)
                v.wait_ge(sSTa[p].h, ao_st[p])
                v.wait_ge(sDVE.h, sDVE.n)
                sDVE.inc(v.tensor_copy(aotile[p][0:64, :],
                                       P.bank(6 + p, w=128)[0:64, :]))
            P.reader(6 + p, sDVE)
            tok_ao = sDVE.n

            @block.gpsimd
            def _(g, h=h, qt=qt, p=p, tok_ao=tok_ao):
                g.wait_ge(sDVE.h, tok_ao)
                sSTa[p].dma(g.dma_start(
                    out=cc_ao_in[h * 64:(h + 1) * 64,
                                 qt * 128:(qt + 1) * 128],
                    in_=aotile[p][0:64, :]))
            ao_st[p] = sSTa[p].n

        pend1 = None
        pend2 = None
        for h in range(8):
            for qt in range(32):
                p = (h * 32 + qt) % 2
                tn = stage1(h, qt, p)
                np1 = (h, qt, p, tn)
                np2 = None
                if pend1 is not None:
                    tok_pt = stage2a(*pend1)
                    np2 = (pend1[0], pend1[1], pend1[2], tok_pt)
                if pend2 is not None:
                    stage2b(*pend2)
                pend1, pend2 = np1, np2
        tok_pt = stage2a(*pend1)
        stage2b(*pend2)
        stage2b(pend1[0], pend1[1], pend1[2], tok_pt)
        tok_attn_st = sST.n

        @block.gpsimd
        def _(g):
            g.wait_ge(sST.h, tok_attn_st)
            g.wait_ge(sSTa[0].h, sSTa[0].n)
            g.wait_ge(sSTa[1].h, sSTa[1].n)
            g.collective_compute(
                "AllGather", ALU.bypass,
                replica_groups=[[0, 1], [2, 3], [4, 5], [6, 7]],
                ins=[cc_ao_in[:, :].opt()],
                outs=[cc_ao_out[:, :].opt()]).then_inc(sCC.h, 1)
        sCC.n += 1
        tok_cc2 = sCC.n

        tok_wo = load_weights_generic(P, block, wo_bf, wo_st, wo_d, D,
                                      sLW, gate=(sPE, max(pv_tok[0], pv_tok[1])))
        for osp in range(4):
            @block.gpsimd
            def _(g, osp=osp):
                g.wait_ge(sCC.h, tok_cc2)
                if osp > 0:
                    g.wait_ge(sPE.h, sPE.n)
                    g.wait_ge(sST.h, sST.n)
                    g.wait_ge(sDVE.h, sDVE.n)
                for c in range(DC):
                    sLX.dma(g.dma_start(
                        out=gA[c][:, :],
                        in_=cc_ao_out[c * 128:(c + 1) * 128,
                                      osp * 512:(osp + 1) * 512]))
                    sLX.dma(g.dma_start(
                        out=gB[c][:, :],
                        in_=cc_ao_out[c * 128:(c + 1) * 128,
                                      TOWN + osp * 512:TOWN + (osp + 1) * 512]))
                    sLX.dma(g.dma_start(
                        out=x_sp[c][:, :],
                        in_=x1_dram[c * 128:(c + 1) * 128,
                                    osp * 512:(osp + 1) * 512]))
            tok_g = sLX.n

            for dc in range(DC):
                mm_toks = []
                for half, gsb in ((0, gA), (1, gB)):
                    bank = 2 * (dc % 2) + half

                    @block.tensor
                    def _(t, dc=dc, half=half, gsb=gsb, bank=bank, osp=osp,
                          tok_g=tok_g):
                        if dc == 0 and half == 0:
                            t.wait_ge(sLX.h, tok_g)
                            if osp == 0:
                                t.wait_ge(sDVE.h, tok_wo)
                        P.acquire(t, bank)
                        for cin in range(DC):
                            mm = t.matmul(
                                P.bank(bank),
                                wo_bf[:, cin * 1024 + dc * 128:
                                      cin * 1024 + dc * 128 + 128],
                                gsb[cin][:, :],
                                start=(cin == 0), stop=(cin == DC - 1))
                        sPE.inc(mm)
                    mm_toks.append(sPE.n)

                @block.vector
                def _(v, dc=dc, osp=osp, mm_toks=tuple(mm_toks)):
                    bA = 2 * (dc % 2)
                    v.wait_ge(sPE.h, mm_toks[1])
                    v.wait_ge(sDVE.h, max(sDVE.n, tok_om))
                    sDVE.inc(v.scalar_tensor_tensor(
                        x_sp[dc][:, :], P.bank(bA), mcol[:, :],
                        x_sp[dc][:, :], ALU.mult, ALU.add))
                    v.wait_ge(sDVE.h, sDVE.n)
                    sDVE.inc(v.scalar_tensor_tensor(
                        x_sp[dc][:, :], P.bank(bA + 1), omcol[:, :],
                        x_sp[dc][:, :], ALU.mult, ALU.add))
                    v.wait_ge(sDVE.h, sDVE.n)
                    sDVE.inc(v.tensor_scalar(x_sp[dc][:, :], x_sp[dc][:, :],
                                             bo_sb[:, dc:dc + 1], None,
                                             ALU.add))
                tok_res = sDVE.n
                P.reader(2 * (dc % 2), sDVE)
                P.reader(2 * (dc % 2) + 1, sDVE)

                @block.gpsimd
                def _(g, dc=dc, osp=osp, tok=tok_res):
                    g.wait_ge(sDVE.h, tok)
                    sST.dma(g.dma_start(
                        out=xo_d[dc * 128:(dc + 1) * 128,
                                 osp * 512:(osp + 1) * 512],
                        in_=x_sp[dc][:, :]))


def load_weights_generic(P, block, dst_bf, st, wd_ap, ncols, sLW, gate=None):
    nc = P.nc
    sDVE = P.sem("dve")
    toks = []
    for cin in range(DC):
        @block.sync
        def _(s, cin=cin):
            if gate is not None:
                s.wait_ge(gate[0].h, gate[1])
            prev = toks[-1] if toks else getattr(P, "_wst_last", 0)
            if prev:
                s.wait_ge(sDVE.h, prev)
            sLW[cin % 2].dma(s.dma_start(
                out=st[:, 0:ncols],
                in_=wd_ap[cin * 128:(cin + 1) * 128, 0:ncols]))
        tok_l = sLW[cin % 2].n

        @block.vector
        def _(v, cin=cin, tok_l=tok_l):
            v.wait_ge(sLW[cin % 2].h, tok_l)
            v.wait_ge(sDVE.h, sDVE.n)
            v.tensor_copy(dst_bf[:, cin * ncols:(cin + 1) * ncols],
                          st[:, 0:ncols]).then_inc(sDVE.h, 1)
        sDVE.n += 1
        toks.append(sDVE.n)
    P._wst_last = sDVE.n
    return sDVE.n



def _shard_x(x):
    out = []
    for c in range(NCORES):
        b, h = c // 2, c % 2
        own = x[b, h * TOWN:(h + 1) * TOWN, :]
        if h == 0:
            halo = np.zeros((THALO, D), np.float32)
        else:
            halo = x[b, TOWN - THALO:TOWN, :]
        xc = np.concatenate([halo, own], axis=0)
        out.append(np.ascontiguousarray(xc.T))
    return out


def _bcol(b):
    return np.ascontiguousarray(b.reshape(8, 128).T.astype(np.float32))


def _p2_masks(core):
    import ml_dtypes
    hv = 1.0 if core % 2 == 1 else 0.0
    qi_ = np.arange(128)[:, None]
    j_ = np.arange(512)[None, :]
    gen = (j_ >= qi_ + 1).astype(np.float32)
    tiles = []
    for qt in range(4):
        kabs = qt * 128 - 512 + j_
        tiles.append(gen * np.where(kabs < 0, hv, 1.0))
    tiles.append(gen)
    maskA = np.concatenate(tiles, axis=0).astype(ml_dtypes.bfloat16)
    maskB = (np.arange(128)[None, :] <= qi_).astype(np.float32).astype(
        ml_dtypes.bfloat16)
    return maskA, maskB


_PROG_CACHE = {}


def _prog(key, builder, *args):
    if key not in _PROG_CACHE:
        _PROG_CACHE[key] = builder(*args)
    return _PROG_CACHE[key]


def run_l0attn_unfused(x_cores, Wq, bq, Wk, bk, Wv, bv, Wo, bo):
    nc = _prog("p2", build_l0attn)
    wqT = np.ascontiguousarray(Wq.T.astype(np.float32))
    wkT = np.ascontiguousarray(Wk.T.astype(np.float32))
    wvT = np.ascontiguousarray(Wv.T.astype(np.float32))
    woT = np.ascontiguousarray(Wo.T.astype(np.float32))
    in_maps = []
    for c, xc in enumerate(x_cores):
        mA, mB = _p2_masks(c)
        in_maps.append({"x": np.ascontiguousarray(xc, np.float32),
                        "wq": wqT, "wk": wkT, "wv": wvT, "wo": woT,
                        "bq": _bcol(bq), "bk": _bcol(bk), "bo": _bcol(bo),
                        "bvr": np.ascontiguousarray(bv[None, :], np.float32),
                        "maskA": mA, "maskB": mB})
    res = run_bass_kernel_spmd(nc, in_maps, core_ids=list(range(len(x_cores))))
    return [r["xo"] for r in res.results]


_P1_CACHE = {}


def run_l0ffn_unfused(x_cores, wg, wu, wd, T):
    key = ("p1", T)
    if key not in _P1_CACHE:
        _P1_CACHE[key] = build_l0ffn(T)
    nc = _P1_CACHE[key]
    import ml_dtypes
    bf = ml_dtypes.bfloat16
    wgT = np.ascontiguousarray(wg.T.astype(bf))
    wuT = np.ascontiguousarray(wu.T.astype(bf))
    wdT = np.ascontiguousarray(wd.T.astype(bf))
    in_maps = [{"x": np.ascontiguousarray(xc), "wg": wgT, "wu": wuT, "wd": wdT}
               for xc in x_cores]
    res = run_bass_kernel_spmd(nc, in_maps, core_ids=list(range(len(x_cores))))
    return [r["xo"] for r in res.results]


def run_moe_unfused(x_cores, W1, W2, Wg, bg):
    import ml_dtypes
    nc = _prog("p4", build_moe)
    wgT8 = np.ascontiguousarray(
        Wg.astype(np.float32).reshape(8, 128, 8).transpose(1, 0, 2).reshape(128, 64))
    sel8 = np.zeros((8, 1024), np.float32)
    for e in range(NE):
        sel8[e, e * 128:(e + 1) * 128] = 1.0
    sel8 = sel8.astype(ml_dtypes.bfloat16)
    W1f = np.ascontiguousarray(W1, np.float32)
    W2f = np.ascontiguousarray(W2, np.float32)
    bgc = np.ascontiguousarray(bg.astype(np.float32)[:, None])
    in_maps = [{"x": np.ascontiguousarray(xc, np.float32), "w1": W1f, "w2": W2f,
                "wg": wgT8, "bg": bgc, "sel8": sel8} for xc in x_cores]
    res = run_bass_kernel_spmd(nc, in_maps, core_ids=list(range(len(x_cores))))
    return [r["xo"] for r in res.results]


def run_l1attn_unfused(x_cores, nw, Wq, bq, Wk, bk, Wv, bv, Wo, bo):
    import ml_dtypes
    nc = _prog("p5", build_l1attn)
    qi_ = np.arange(128)[:, None]
    maskB = (np.arange(128)[None, :] <= qi_).astype(np.float32).astype(
        ml_dtypes.bfloat16)
    woT = np.ascontiguousarray(Wo.T.astype(np.float32))
    nwc = np.ascontiguousarray(nw.astype(np.float32).reshape(8, 128).T)
    boc = _bcol(bo)
    in_maps = []
    for c, xc in enumerate(x_cores):
        hsl = slice((c % 2) * 512, (c % 2) * 512 + 512)
        in_maps.append({
            "x": np.ascontiguousarray(xc, np.float32),
            "wq": np.ascontiguousarray(Wq.T[:, hsl].astype(np.float32)),
            "wk": np.ascontiguousarray(Wk.T[:, hsl].astype(np.float32)),
            "wv": np.ascontiguousarray(Wv.T[:, hsl].astype(np.float32)),
            "wo": woT,
            "bq": np.ascontiguousarray(
                bq[hsl].astype(np.float32).reshape(4, 128).T),
            "bk": np.ascontiguousarray(
                bk[hsl].astype(np.float32).reshape(4, 128).T),
            "bo": boc,
            "bvr": np.ascontiguousarray(bv[None, hsl].astype(np.float32)),
            "nw": nwc,
            "mcol": np.full((128, 1), 1.0 if c % 2 == 0 else 0.0, np.float32),
            "maskB": maskB})
    res = run_bass_kernel_spmd(nc, in_maps, core_ids=list(range(len(x_cores))))
    return [r["xo"] for r in res.results]


def build_fused():
    P = _mk_P()
    nc = P.nc

    def din(name, shape, dt):
        return nc.dram_tensor(name, shape, dt, kind="ExternalInput").ap()

    x0_d = din("x", [D, TFULL], F32)
    wg_d = din("wg", [D, H0], BF16)
    wu_d = din("wu", [D, H0], BF16)
    wd_d = din("wd", [H0, D], BF16)
    wq0 = din("wq0", [D, D], F32)
    wk0 = din("wk0", [D, D], F32)
    wv0 = din("wv0", [D, D], F32)
    wo0 = din("wo0", [D, D], F32)
    bq0 = din("bq0", [128, 8], F32)
    bk0 = din("bk0", [128, 8], F32)
    bo0 = din("bo0", [128, 8], F32)
    bvr0 = din("bvr0", [1, D], F32)
    mA_d = din("maskA", [640, 512], BF16)
    mB_d = din("maskB", [128, 128], BF16)
    w1_d = din("w1", [NE, D, H1], F32)
    w2_d = din("w2", [NE, H1, D], F32)
    wgg_d = din("wgg", [128, 64], F32)
    bg_d = din("bg", [8, 1], F32)
    sel8_d = din("sel8", [8, 1024], BF16)
    wq1 = din("wq1", [D, 512], F32)
    wk1 = din("wk1", [D, 512], F32)
    wv1 = din("wv1", [D, 512], F32)
    wo1 = din("wo1", [D, D], F32)
    bq1 = din("bq1", [128, 4], F32)
    bk1 = din("bk1", [128, 4], F32)
    bo1 = din("bo1", [128, 8], F32)
    bvr1 = din("bvr1", [1, 512], F32)
    nw_d = din("nw", [128, 8], F32)
    mcol_d = din("mcol", [128, 1], F32)
    out_d = nc.dram_tensor("out", [D, TOWN], F32, kind="ExternalOutput").ap()
    x1_d = nc.dram_tensor("x1i", [D, TFULL], F32).ap()
    x2_d = nc.dram_tensor("x2i", [D, TOWN], F32).ap()
    x3_d = nc.dram_tensor("x3i", [D, TOWN], F32).ap()
    x4_d = nc.dram_tensor("x4i", [D, TOWN], F32).ap()
    x5_d = nc.dram_tensor("x5i", [D, TOWN], F32).ap()

    with P.es:
        block = P.es.enter_context(nc.Block())

        @block.gpsimd
        def _(g):
            P.init_consts(g)
        P.gate_consts(block)
        P._moe_wgr = None
        mark = P.A.mark()

        emit_l0ffn(P, block, {"x": x0_d, "wg": wg_d, "wu": wu_d, "wd": wd_d,
                              "xo": x1_d}, TFULL)
        phase_barrier(P, block)
        P.A.reset(mark)

        emit_l0attn(P, block, {"x": x1_d, "wq": wq0, "wk": wk0, "wv": wv0,
                               "wo": wo0, "bq": bq0, "bk": bk0, "bo": bo0,
                               "bvr": bvr0, "maskA": mA_d, "maskB": mB_d,
                               "xo": x2_d})
        phase_barrier(P, block)
        P.A.reset(mark)

        emit_l0ffn(P, block, {"x": x2_d, "wg": wg_d, "wu": wu_d, "wd": wd_d,
                              "xo": x3_d}, TOWN)
        phase_barrier(P, block)
        P.A.reset(mark)

        emit_moe(P, block, {"x": x3_d, "w1": w1_d, "w2": w2_d, "wg": wgg_d,
                            "bg": bg_d, "sel8": sel8_d, "xo": x4_d}, tag="a")
        phase_barrier(P, block)
        P.A.reset(mark)

        emit_l1attn(P, block, {"x": x4_d, "wq": wq1, "wk": wk1, "wv": wv1,
                               "wo": wo1, "bq": bq1, "bk": bk1, "bo": bo1,
                               "bvr": bvr1, "nw": nw_d, "mcol": mcol_d,
                               "maskB": mB_d, "xo": x5_d})
        phase_barrier(P, block)
        P.A.reset(mark)

        emit_moe(P, block, {"x": x5_d, "w1": w1_d, "w2": w2_d, "wg": wgg_d,
                            "bg": bg_d, "sel8": sel8_d, "xo": out_d}, tag="b")

        sST = P.sem("st")

        @block.gpsimd
        def _(g):
            g.wait_ge(sST.h, sST.n)
    return nc


def kernel(L0_attn_x, L0_attn_Wq, L0_attn_bq, L0_attn_Wk, L0_attn_bk,
           L0_attn_Wv, L0_attn_bv, L0_attn_Wo, L0_attn_bo,
           L0_ffn_w_gate, L0_ffn_w_up, L0_ffn_w_down,
           L1_attn_norm_w, L1_attn_Wq, L1_attn_bq, L1_attn_Wk, L1_attn_bk,
           L1_attn_Wv, L1_attn_bv, L1_attn_Wo, L1_attn_bo,
           L1_ffn_W1, L1_ffn_W2, L1_ffn_Wg, L1_ffn_bg):
    import ml_dtypes
    x = np.asarray(L0_attn_x, np.float32)
    x_cores = _shard_x(x)
    nc = _prog("fused", build_fused)

    import ml_dtypes
    f32 = np.float32
    bf = ml_dtypes.bfloat16
    shared = {
        "wg": np.ascontiguousarray(L0_ffn_w_gate.T.astype(bf)),
        "wu": np.ascontiguousarray(L0_ffn_w_up.T.astype(bf)),
        "wd": np.ascontiguousarray(L0_ffn_w_down.T.astype(bf)),
        "wq0": np.ascontiguousarray(L0_attn_Wq.T.astype(f32)),
        "wk0": np.ascontiguousarray(L0_attn_Wk.T.astype(f32)),
        "wv0": np.ascontiguousarray(L0_attn_Wv.T.astype(f32)),
        "wo0": np.ascontiguousarray(L0_attn_Wo.T.astype(f32)),
        "bq0": _bcol(L0_attn_bq), "bk0": _bcol(L0_attn_bk),
        "bo0": _bcol(L0_attn_bo),
        "bvr0": np.ascontiguousarray(L0_attn_bv[None, :].astype(f32)),
        "maskB": (np.arange(128)[None, :] <= np.arange(128)[:, None]
                  ).astype(f32).astype(ml_dtypes.bfloat16),
        "w1": np.ascontiguousarray(L1_ffn_W1, f32),
        "w2": np.ascontiguousarray(L1_ffn_W2, f32),
        "wgg": np.ascontiguousarray(
            L1_ffn_Wg.astype(f32).reshape(8, 128, 8).transpose(1, 0, 2
                                                               ).reshape(128, 64)),
        "bg": np.ascontiguousarray(L1_ffn_bg.astype(f32)[:, None]),
        "wo1": np.ascontiguousarray(L1_attn_Wo.T.astype(f32)),
        "bo1": _bcol(L1_attn_bo),
        "nw": np.ascontiguousarray(L1_attn_norm_w.astype(f32).reshape(8, 128).T),
    }
    sel8 = np.zeros((8, 1024), f32)
    for e in range(NE):
        sel8[e, e * 128:(e + 1) * 128] = 1.0
    shared["sel8"] = sel8.astype(ml_dtypes.bfloat16)

    in_maps = []
    for c in range(NCORES):
        mA, _mB = _p2_masks(c)
        hsl = slice((c % 2) * 512, (c % 2) * 512 + 512)
        m = dict(shared)
        m["x"] = np.ascontiguousarray(x_cores[c], f32)
        m["maskA"] = mA
        m["wq1"] = np.ascontiguousarray(L1_attn_Wq.T[:, hsl].astype(f32))
        m["wk1"] = np.ascontiguousarray(L1_attn_Wk.T[:, hsl].astype(f32))
        m["wv1"] = np.ascontiguousarray(L1_attn_Wv.T[:, hsl].astype(f32))
        m["bq1"] = np.ascontiguousarray(
            L1_attn_bq[hsl].astype(f32).reshape(4, 128).T)
        m["bk1"] = np.ascontiguousarray(
            L1_attn_bk[hsl].astype(f32).reshape(4, 128).T)
        m["bvr1"] = np.ascontiguousarray(L1_attn_bv[None, hsl].astype(f32))
        m["mcol"] = np.full((128, 1), 1.0 if c % 2 == 0 else 0.0, f32)
        in_maps.append(m)

    res = run_bass_kernel_spmd(nc, in_maps, core_ids=list(range(NCORES)))
    out = np.empty((4, 4096, 1024), np.float32)
    for c in range(NCORES):
        b, h = c // 2, c % 2
        out[b, h * TOWN:(h + 1) * TOWN, :] = res.results[c]["out"].T
    return out


def kernel_unfused(L0_attn_x, L0_attn_Wq, L0_attn_bq, L0_attn_Wk, L0_attn_bk,
           L0_attn_Wv, L0_attn_bv, L0_attn_Wo, L0_attn_bo,
           L0_ffn_w_gate, L0_ffn_w_up, L0_ffn_w_down,
           L1_attn_norm_w, L1_attn_Wq, L1_attn_bq, L1_attn_Wk, L1_attn_bk,
           L1_attn_Wv, L1_attn_bv, L1_attn_Wo, L1_attn_bo,
           L1_ffn_W1, L1_ffn_W2, L1_ffn_Wg, L1_ffn_bg):
    x = np.asarray(L0_attn_x, np.float32)
    x_cores = _shard_x(x)
    x1 = run_l0ffn_unfused(x_cores, L0_ffn_w_gate, L0_ffn_w_up, L0_ffn_w_down,
                           TFULL)
    x2 = run_l0attn_unfused(x1, L0_attn_Wq, L0_attn_bq, L0_attn_Wk, L0_attn_bk,
                            L0_attn_Wv, L0_attn_bv, L0_attn_Wo, L0_attn_bo)
    x3 = run_l0ffn_unfused(x2, L0_ffn_w_gate, L0_ffn_w_up, L0_ffn_w_down, TOWN)
    x4 = run_moe_unfused(x3, L1_ffn_W1, L1_ffn_W2, L1_ffn_Wg, L1_ffn_bg)
    x5 = run_l1attn_unfused(x4, L1_attn_norm_w, L1_attn_Wq, L1_attn_bq,
                            L1_attn_Wk, L1_attn_bk, L1_attn_Wv, L1_attn_bv,
                            L1_attn_Wo, L1_attn_bo)
    x6 = run_moe_unfused(x5, L1_ffn_W1, L1_ffn_W2, L1_ffn_Wg, L1_ffn_bg)
    out = np.empty((4, 4096, 1024), np.float32)
    for c in range(NCORES):
        b, h = c // 2, c % 2
        out[b, h * TOWN:(h + 1) * TOWN, :] = x6[c].T
    return out


# version 16
# speedup vs baseline: 1.1671x; 1.1671x over previous
import numpy as np
from contextlib import ExitStack

import concourse.bass as bass
import concourse.mybir as mybir
from concourse.bass_utils import run_bass_kernel_spmd

F32 = mybir.dt.float32
F32R = mybir.dt.float32r
BF16 = mybir.dt.bfloat16
I32 = mybir.dt.int32
U8 = mybir.dt.uint8
AF = mybir.ActivationFunctionType
ALU = mybir.AluOpType
AX = mybir.AxisListType.X

NCORES = 8
D = 1024
DC = 8
TOWN = 2048
THALO = 512
TFULL = TOWN + THALO
H0 = 3072
HC0 = 24
H1 = 4096
HC1 = 32
NE = 8
NHEAD = 16
HDIM = 64
WIN = 512
LN_EPS = 1e-5
RMS_EPS = 1e-6

ISZ = {F32: 4, F32R: 4, BF16: 2, I32: 4, U8: 1}


class Sem:
    def __init__(self, nc, es, name):
        self.h = es.enter_context(nc.semaphore(name))
        self.n = 0

    def inc(self, instr, k=1):
        instr.then_inc(self.h, k)
        self.n += k
        return self.n

    def dma(self, instr):
        return self.inc(instr, 16)

    def tok(self):
        return (self, self.n)


class Arena:

    def __init__(self, nc, name, kib):
        self.t = nc.alloc_sbuf_tensor(name, [128, kib * 1024], U8)
        self.base = self.t.ap()
        self.off = 0
        self.hi = kib * 1024

    def alloc(self, cols, dt, n=1, align=64):
        out = []
        for _ in range(n):
            sz = cols * ISZ[dt]
            off = (self.off + align - 1) // align * align
            assert off + sz <= self.hi, f"arena overflow {off}+{sz}>{self.hi}"
            out.append(self.base.bitcast(dt)[:, off // ISZ[dt]: off // ISZ[dt] + cols])
            self.off = off + sz
        return out[0] if n == 1 else out

    def row(self, cols, dt=F32, n=1):
        r = self.alloc(cols, dt, n)
        if n == 1:
            return r[0:1, :]
        return [x[0:1, :] for x in r]

    def mark(self):
        return self.off

    def reset(self, m):
        self.off = m


class Pctx:

    def __init__(self, name, arena_kib=200):
        self.nc = bass.Bass("TRN2", target_bir_lowering=False)
        self.es = ExitStack()
        self.A = Arena(self.nc, "A", arena_kib)
        self.PSt = self.nc.alloc_psum_tensor("PS", [128, 4096], F32)
        self.PS = self.PSt.ap()
        self._sems = {}
        self.bank_readers = {}
        self._nt = 0

    def sbuf(self, cols, dt, n=1):
        self._nt += 1
        t = self.nc.alloc_sbuf_tensor(f"T{self._nt}", [128, cols * n], dt).ap()
        out = [t[:, i * cols:(i + 1) * cols] for i in range(n)]
        return out[0] if n == 1 else out

    def sem(self, name):
        if name not in self._sems:
            self._sems[name] = Sem(self.nc, self.es, name)
        return self._sems[name]

    def bank(self, b, w=512):
        return self.PS[:, b * 512: b * 512 + w]

    def bank_bf(self, b, w=1024):
        return self.PS.bitcast(BF16)[:, b * 1024: b * 1024 + w]

    def reader(self, bank, sem):
        d = self.bank_readers.setdefault(bank, {})
        d[id(sem)] = (sem, sem.n)

    def acquire(self, eng, bank):
        for s, v in self.bank_readers.pop(bank, {}).values():
            eng.wait_ge(s.h, v)

    def alloc_consts(self, const_vals=()):
        self.ones_col_f32 = self.A.alloc(1, F32)
        self.ones_row_f32 = self.A.alloc(128, F32)[0:1, :]
        onesr = self.sbuf(129, F32R)
        self.ones_col = onesr[:, 0:1]
        self.ones_row = onesr[0:1, 1:129]
        self.ones_row_bf = self.A.alloc(128, BF16)[0:1, :]
        self.const_tiles = {}
        for v in const_vals:
            t = self.A.alloc(1, F32)
            self.const_tiles[v] = t
            self.nc.const_aps.register(F32, v, t[:, :])

    def alloc_idents(self):
        self.ident_bf = self.A.alloc(128, BF16)
        self.ident_f32 = self.A.alloc(128, F32)
        self._onet_bf = self.A.alloc(128, BF16)
        self._onet_f32 = self.A.alloc(128, F32)

    def init_consts(self, g):
        sGP = self.sem("gp")
        sGP.inc(g.memset(self.ones_col_f32[:, :], 1.0))
        sGP.inc(g.memset(self.ones_row_f32[:, :], 1.0))
        sGP.inc(g.memset(self.ones_row_bf[:, :], 1.0))
        for v, t in self.const_tiles.items():
            sGP.inc(g.memset(t[:, :], v))
        if getattr(self, "ident_bf", None) is not None:
            sGP.inc(g.memset(self._onet_bf[:, :], 1.0))
            sGP.inc(g.memset(self._onet_f32[:, :], 1.0))
            g.wait_ge(sGP.h, sGP.n)
            sGP.inc(g.affine_select(self.ident_bf[:, :], self._onet_bf[:, :],
                                    pattern=[[-1, 128]],
                                    compare_op=ALU.is_equal, fill=0.0, base=0,
                                    channel_multiplier=1))
            sGP.inc(g.affine_select(self.ident_f32[:, :], self._onet_f32[:, :],
                                    pattern=[[-1, 128]],
                                    compare_op=ALU.is_equal, fill=0.0, base=0,
                                    channel_multiplier=1))
        self._n_gp_init = sGP.n

    def init_consts_dve(self, v):
        sGP = self.sem("gp")
        v.wait_ge(sGP.h, self._n_gp_init)
        sGP.inc(v.tensor_copy(self.ones_col[:, :], self.ones_col_f32[:, :]))
        sGP.inc(v.tensor_copy(self.ones_row[:, :], self.ones_row_f32[:, :]))
        self._gate_val = sGP.n

    def gate_consts(self, block):
        sGP = self.sem("gp")

        @block.vector
        def _(v):
            self.init_consts_dve(v)

        @block.scalar
        def _(a):
            a.wait_ge(sGP.h, self._gate_val)

        @block.tensor
        def _(t):
            t.wait_ge(sGP.h, self._gate_val)

        @block.sync
        def _(s):
            s.wait_ge(sGP.h, self._gate_val)


LN_EIGHTH = -2.0794415416798357


def _mk_P(arena_kib=199):
    P = Pctx("k", arena_kib=arena_kib)
    P.alloc_consts(const_vals=(LN_EPS, RMS_EPS, LN_EIGHTH))
    P.alloc_idents()
    P.lb = LnBufs(P, 512, rms2=True, need_xn32=True)
    return P


def phase_barrier(P, block):
    sems = [s for s in P._sems.values() if s.n > 0]

    def wait_all(e):
        for s in sems:
            e.wait_ge(s.h, s.n)

    for name in ("tensor", "vector", "scalar", "gpsimd", "sync"):
        getattr(block, name)(wait_all)



class LnBufs:
    def __init__(self, P, TBmax, rms2=False, need_xn32=True):
        A = P.A
        self.sq = P.sbuf(512, F32R, 2)
        self.t32 = A.alloc(512, F32, 2)
        self.xn32_r = P.sbuf(512, F32R, 2) if need_xn32 else [None, None]
        self.xn32_f = A.alloc(512, F32, 2) if rms2 else [None, None]
        nrows = 5 if rms2 else 4
        rows = A.row(TBmax, F32, nrows)
        self.rows = rows + ([None] if not rms2 else [])
        assert len(self.rows) == 5


def emit_ln(P, block, x, xn, TB, lb, tok_x=None, mode="ln",
            nw_cols=None, logits=None, xn_war=None,
            stats_banks=(6, 7), bc_banks=(4, 5)):
    sPE, sACT, sDVE = P.sem("pe"), P.sem("act"), P.sem("dve")

    def AI(a, instr):
        sACT.inc(instr)
        return sACT.n

    def VI(v, instr):
        sDVE.inc(instr)
        return sDVE.n

    NS = TB // 512
    mu, m2, rstd, nmr, tr = lb.rows
    sb_sx, sb_sq = stats_banks
    bb1, bb2 = bc_banks
    sq, t32 = lb.sq, lb.t32
    xn32 = lb.xn32_r if mode == "moe" else lb.xn32_f
    prev_stats_tok = getattr(P, "_ln_stats_tok", None)
    logit_toks = []

    for sp in range(NS):
        cs = slice(sp * 512, (sp + 1) * 512)
        rs = cs

        @block.tensor
        def _(t, cs=cs):
            P.acquire(t, sb_sx)
            if tok_x is not None:
                seen = {}
                for s, v in tok_x:
                    seen[id(s)] = (s, max(v, seen.get(id(s), (s, 0))[1]))
                for s, v in seen.values():
                    t.wait_ge(s.h, v)
            for c in range(DC):
                mm = t.matmul(P.bank(sb_sx)[0:1, :], P.ones_col_f32,
                              x[c][:, cs], start=(c == 0), stop=(c == DC - 1))
            sPE.inc(mm)
        sGPs = P.sem("gp")
        pe_sq_toks = []
        for c in range(DC):
            @block.gpsimd
            def _(g, c=c, cs=cs):
                if tok_x is not None and sp == 0:
                    s, v = tok_x[c]
                    g.wait_ge(s.h, v)
                if len(pe_sq_toks) >= 2:
                    g.wait_ge(sPE.h, pe_sq_toks[-2])
                elif prev_stats_tok is not None and c < 2:
                    g.wait_ge(sPE.h, prev_stats_tok)
                g.wait_ge(sGPs.h, sGPs.n)
                g.tensor_tensor(sq[c % 2][:, :], x[c][:, cs], x[c][:, cs],
                                ALU.mult).then_inc(sGPs.h, 1)
            sGPs.n += 1
            tok_sqc = sGPs.n

            @block.tensor
            def _(t, c=c, tok=tok_sqc):
                t.wait_ge(sGPs.h, tok)
                if c == 0:
                    P.acquire(t, sb_sq)
                t.matmul(P.bank(sb_sq)[0:1, :], P.ones_col,
                         sq[c % 2][:, :], start=(c == 0),
                         stop=(c == DC - 1)).then_inc(sPE.h, 1)
            sPE.n += 1
            pe_sq_toks.append(sPE.n)
        tok_stats = sPE.n
        prev_stats_tok = tok_stats

        @block.scalar
        def _(a, rs=rs, tok=tok_stats):
            a.wait_ge(sPE.h, tok - 1)
            a.wait_ge(sACT.h, sACT.n)
            AI(a, a.mul(mu[:, 0:512], P.bank(sb_sx)[0:1, :], 1.0 / D))
            a.wait_ge(sPE.h, tok)
            a.wait_ge(sACT.h, sACT.n)
            AI(a, a.activation(m2[:, 0:512], P.bank(sb_sq)[0:1, :], AF.Copy,
                               scale=1.0 / D))
        tok_m = sACT.n
        P.reader(sb_sx, sACT)
        P.reader(sb_sq, sACT)

        @block.vector
        def _(v, rs=rs, tok=tok_m):
            v.wait_ge(sACT.h, tok)
            v.wait_ge(sDVE.h, sDVE.n)
            VI(v, v.tensor_tensor(rstd[:, 0:512], mu[:, 0:512], mu[:, 0:512], ALU.mult))
            v.wait_ge(sDVE.h, sDVE.n)
            VI(v, v.tensor_tensor(m2[:, 0:512], m2[:, 0:512], rstd[:, 0:512],
                                  ALU.subtract))
        tok_var = sDVE.n

        if mode in ("ln", "moe"):
            @block.scalar
            def _(a, rs=rs, tok=tok_var):
                a.wait_ge(sDVE.h, tok)
                a.wait_ge(sACT.h, sACT.n)
                AI(a, a.activation(rstd[:, 0:512], m2[:, 0:512], AF.Ln,
                                   bias=LN_EPS))
                a.wait_ge(sACT.h, sACT.n)
                AI(a, a.activation(rstd[:, 0:512], rstd[:, 0:512], AF.Exp,
                                   scale=-0.5))
            tok_s = sACT.n

            @block.vector
            def _(v, rs=rs, tok=tok_s):
                v.wait_ge(sACT.h, tok)
                v.wait_ge(sDVE.h, sDVE.n)
                VI(v, v.scalar_tensor_tensor(nmr[:, 0:512], mu[:, 0:512], -1.0,
                                             rstd[:, 0:512], ALU.mult, ALU.mult))
            tok_r = sDVE.n
        else:
            @block.scalar
            def _(a, rs=rs, tok=tok_var):
                a.wait_ge(sDVE.h, tok)
                a.wait_ge(sACT.h, sACT.n)
                AI(a, a.add(tr[:, 0:512], m2[:, 0:512], LN_EPS))
            tok_d1 = sACT.n

            @block.vector
            def _(v, rs=rs, tok=tok_d1):
                v.wait_ge(sACT.h, tok)
                v.wait_ge(sDVE.h, sDVE.n)
                VI(v, v.reciprocal(rstd[:, 0:512], tr[:, 0:512]))
                v.wait_ge(sDVE.h, sDVE.n)
                VI(v, v.tensor_tensor(nmr[:, 0:512], m2[:, 0:512], rstd[:, 0:512],
                                      ALU.mult))
            tok_q = sDVE.n

            @block.scalar
            def _(a, rs=rs, tok=tok_q):
                a.wait_ge(sDVE.h, tok)
                a.wait_ge(sACT.h, sACT.n)
                AI(a, a.add(nmr[:, 0:512], nmr[:, 0:512], RMS_EPS))
            tok_d2 = sACT.n

            @block.vector
            def _(v, rs=rs, tok=tok_d2):
                v.wait_ge(sACT.h, tok)
                v.wait_ge(sDVE.h, sDVE.n)
                VI(v, v.tensor_tensor(tr[:, 0:512], tr[:, 0:512], nmr[:, 0:512],
                                      ALU.mult))
            tok_p = sDVE.n

            @block.scalar
            def _(a, rs=rs, tok=tok_p):
                a.wait_ge(sDVE.h, tok)
                a.wait_ge(sACT.h, sACT.n)
                AI(a, a.activation(rstd[:, 0:512], tr[:, 0:512], AF.Ln))
                a.wait_ge(sACT.h, sACT.n)
                AI(a, a.activation(rstd[:, 0:512], rstd[:, 0:512], AF.Exp,
                                   scale=-0.5))
            tok_s = sACT.n

            @block.vector
            def _(v, rs=rs, tok=tok_s):
                v.wait_ge(sACT.h, tok)
                v.wait_ge(sDVE.h, sDVE.n)
                VI(v, v.scalar_tensor_tensor(nmr[:, 0:512], mu[:, 0:512], -1.0,
                                             rstd[:, 0:512], ALU.mult, ALU.mult))
            tok_r = sDVE.n

        @block.tensor
        def _(t, rs=rs, tok=tok_r):
            t.wait_ge(sDVE.h, tok)
            P.acquire(t, bb1)
            t.matmul(P.bank(bb1), P.ones_row_f32, rstd[:, 0:512],
                     start=True, stop=True)
            P.acquire(t, bb2)
            sPE.inc(t.matmul(P.bank(bb2), P.ones_row_f32, nmr[:, 0:512],
                             start=True, stop=True))
        tok_bc = sPE.n

        for c in range(DC):
            if mode == "ln":
                @block.vector
                def _(v, c=c, cs=cs, tok=tok_bc, sp=sp):
                    if c == 0:
                        v.wait_ge(sPE.h, tok)
                        if sp == 0 and xn_war is not None:
                            for ws, wv_ in xn_war:
                                v.wait_ge(ws.h, wv_)
                    v.wait_ge(sDVE.h, sDVE.n)
                    VI(v, v.tensor_tensor(t32[c % 2][:, :], x[c][:, cs],
                                          P.bank(bb1), ALU.mult))
                    v.wait_ge(sDVE.h, sDVE.n)
                    VI(v, v.tensor_tensor(xn[c][:, cs], t32[c % 2][:, :],
                                          P.bank(bb2), ALU.add))
            elif mode == "moe":
                @block.vector
                def _(v, c=c, cs=cs, tok=tok_bc, sp=sp):
                    if c == 0:
                        v.wait_ge(sPE.h, tok)
                        if sp == 0 and xn_war is not None:
                            for ws, wv_ in xn_war:
                                v.wait_ge(ws.h, wv_)
                    if len(logit_toks) >= 2:
                        v.wait_ge(sPE.h, logit_toks[-2])
                    v.wait_ge(sDVE.h, sDVE.n)
                    VI(v, v.tensor_tensor(t32[c % 2][:, :], x[c][:, cs],
                                          P.bank(bb1), ALU.mult))
                    v.wait_ge(sDVE.h, sDVE.n)
                    VI(v, v.tensor_tensor(xn32[c % 2][:, :], t32[c % 2][:, :],
                                          P.bank(bb2), ALU.add))
                    v.wait_ge(sDVE.h, sDVE.n)
                    VI(v, v.tensor_copy(xn[c][:, cs], xn32[c % 2][:, :]))
                tok_xn32 = sDVE.n

                @block.tensor
                def _(t, c=c, tok=tok_xn32):
                    t.wait_ge(sDVE.h, tok)
                    if c == 0:
                        P.acquire(t, logits[1])
                    t.matmul(P.bank(logits[1])[0:NE, :], logits[0][c],
                             xn32[c % 2][:, :],
                             start=(c == 0), stop=(c == DC - 1)).then_inc(sPE.h, 1)
                sPE.n += 1
                logit_toks.append(sPE.n)
            else:
                @block.vector
                def _(v, c=c, cs=cs, tok=tok_bc, sp=sp):
                    if c == 0:
                        v.wait_ge(sPE.h, tok)
                        if sp == 0 and xn_war is not None:
                            for ws, wv_ in xn_war:
                                v.wait_ge(ws.h, wv_)
                    v.wait_ge(sDVE.h, sDVE.n)
                    VI(v, v.tensor_tensor(xn32[c % 2][:, :], x[c][:, cs],
                                          P.bank(bb1), ALU.mult))
                    v.wait_ge(sDVE.h, sDVE.n)
                    VI(v, v.tensor_tensor(xn32[c % 2][:, :], xn32[c % 2][:, :],
                                          P.bank(bb2), ALU.add))
                    v.wait_ge(sDVE.h, sDVE.n)
                    VI(v, v.tensor_scalar(t32[c % 2][:, :], xn32[c % 2][:, :],
                                          nw_cols[c], None, ALU.mult))
                    v.wait_ge(sDVE.h, sDVE.n)
                    VI(v, v.tensor_copy(xn[c][:, cs], t32[c % 2][:, :]))
                    v.wait_ge(sDVE.h, sDVE.n)
                    VI(v, v.tensor_tensor(x[c][:, cs], x[c][:, cs],
                                          t32[c % 2][:, :], ALU.add))
        P.reader(bb1, sDVE)
        P.reader(bb2, sDVE)
        if mode == "moe":
            logits[2](sp, logit_toks[-1])

    P._ln_stats_tok = prev_stats_tok
    return sDVE.tok()



def build_l0ffn(T):
    P = _mk_P()
    nc = P.nc
    aps = {
        "x": nc.dram_tensor("x", [D, T], F32, kind="ExternalInput").ap(),
        "wg": nc.dram_tensor("wg", [D, H0], BF16, kind="ExternalInput").ap(),
        "wu": nc.dram_tensor("wu", [D, H0], BF16, kind="ExternalInput").ap(),
        "wd": nc.dram_tensor("wd", [H0, D], BF16, kind="ExternalInput").ap(),
        "xo": nc.dram_tensor("xo", [D, T], F32, kind="ExternalOutput").ap(),
    }
    with P.es:
        block = P.es.enter_context(nc.Block())

        @block.gpsimd
        def _(g):
            P.init_consts(g)
        P.gate_consts(block)
        emit_l0ffn(P, block, aps, T)
        sST = P.sem("st")

        @block.gpsimd
        def _(g):
            g.wait_ge(sST.h, sST.n)
    return nc


def emit_l0ffn(P, block, aps, T):
    nc = P.nc
    x_d, wg_d, wu_d, wd_d, xo_d = (aps["x"], aps["wg"], aps["wu"], aps["wd"],
                                   aps["xo"])
    if T == 2560:
        blocks = [(0, 1024), (1024, 1024), (2048, 512)]
    else:
        blocks = [(0, 1024), (1024, 1024)]
    TBmax = max(tb for _, tb in blocks)

    A = P.A
    lb = P.lb
    x_blk = A.alloc(TBmax, F32, DC)
    xn_blk = A.alloc(TBmax, BF16, DC)
    h_blk = A.alloc(TBmax, BF16, HC0)
    wd_bf = A.alloc(D, BF16, HC0)
    wgu_st = A.alloc(2048, BF16, 3)
    wd_st = [wgu_st[0][:, 0:1024], wgu_st[1][:, 0:1024]]
    wgu_bf = A.alloc(2048, BF16, 2)
    sg = A.alloc(512, F32, 2)
    assert A.off <= A.hi

    sPE, sACT, sDVE = P.sem("pe"), P.sem("act"), P.sem("dve")
    sLX, sST = P.sem("lx"), P.sem("st")
    sLW = [P.sem("lw0"), P.sem("lw1")]
    sLWG = [P.sem("lwga"), P.sem("lwgb"), P.sem("lwgc")]
    sGP = P.sem("gp")

    if True:
        wd_conv = [0] * HC0
        for hc in range(HC0):
            @block.sync
            def _(s, hc=hc):
                if hc >= 2:
                    s.wait_ge(sDVE.h, wd_conv[hc - 2])
                sLW[hc % 2].dma(s.dma_start(out=wd_st[hc % 2][:, :],
                                            in_=wd_d[hc * 128:(hc + 1) * 128, :]))
            tok_l = sLW[hc % 2].n

            @block.vector
            def _(v, hc=hc, tok_l=tok_l):
                v.wait_ge(sLW[hc % 2].h, tok_l)
                v.wait_ge(sDVE.h, sDVE.n)
                v.tensor_scalar(wd_bf[hc][:, :], wd_st[hc % 2][:, :],
                                1.0 / 127.0, None, ALU.mult).then_inc(sDVE.h, 1)
            sDVE.n += 1
            wd_conv[hc] = sDVE.n
        tok_wd = sDVE.n

        x_store_tok = [0] * DC
        prev_block_loadgate = None
        wgu_mm_last = [0]
        silu_tok = {}
        h_tok = {}
        wgu_conv = {}
        wgu_mm = {}

        for bi, (tb0, TB) in enumerate(blocks):
            NSb = TB // 512
            for c in range(DC):
                @block.gpsimd
                def _(g, c=c, tb0=tb0, TB=TB):
                    if prev_block_loadgate is not None:
                        g.wait_ge(sST.h, prev_block_loadgate)
                    sLX.dma(g.dma_start(out=x_blk[c][:, 0:TB],
                                        in_=x_d[c * 128:(c + 1) * 128, tb0:tb0 + TB]))
            tok_x = [(sLX, sLX.n)] * DC

            xn_war = [(sPE, wgu_mm_last[0])] if bi > 0 else None
            tok_xn = emit_ln(P, block, [x_blk[c][:, 0:TB] for c in range(DC)],
                             [xn_blk[c][:, 0:TB] for c in range(DC)], TB, lb,
                             tok_x=tok_x, mode="ln", xn_war=xn_war)

            for hc in range(HC0):
                sl3 = hc % 3
                @block.sync
                def _(s, hc=hc, bi=bi, sl3=sl3):
                    if bi == 0 and hc < 2:
                        s.wait_ge(sDVE.h, tok_wd)
                    if (bi, hc - 3) in wgu_conv:
                        s.wait_ge(sDVE.h, wgu_conv[(bi, hc - 3)])
                    elif hc < 3 and bi > 0 and (bi - 1, HC0 - 3 + hc) in wgu_conv:
                        s.wait_ge(sDVE.h, wgu_conv[(bi - 1, HC0 - 3 + hc)])
                    sLWG[sl3].dma(s.dma_start(
                        out=wgu_st[sl3][:, 0:1024].rearrange(
                            "p (c h) -> p c h", c=DC),
                        in_=wg_d[:, hc * 128:(hc + 1) * 128].rearrange(
                            "(c p) h -> p c h", p=128)))
                    sLWG[sl3].dma(s.dma_start(
                        out=wgu_st[sl3][:, 1024:2048].rearrange(
                            "p (c h) -> p c h", c=DC),
                        in_=wu_d[:, hc * 128:(hc + 1) * 128].rearrange(
                            "(c p) h -> p c h", p=128)))
                tok_l = sLWG[sl3].n

                @block.vector
                def _(v, hc=hc, bi=bi, sl3=sl3, tok_l=tok_l):
                    v.wait_ge(sLWG[sl3].h, tok_l)
                    key = (bi, hc - 2) if hc >= 2 else (bi - 1, HC0 - 2 + hc)
                    if key in wgu_mm:
                        v.wait_ge(sPE.h, wgu_mm[key])
                    v.wait_ge(sDVE.h, sDVE.n)
                    v.tensor_scalar(wgu_bf[hc % 2][:, :], wgu_st[sl3][:, :],
                                    1.0 / 127.0, None,
                                    ALU.mult).then_inc(sDVE.h, 1)
                sDVE.n += 1
                wgu_conv[(bi, hc)] = sDVE.n

                g_tok = {}
                u_tok = {}

                @block.tensor
                def _(t, hc=hc, bi=bi, NSb=NSb):
                    t.wait_ge(sDVE.h, wgu_conv[(bi, hc)])
                    s, v = tok_xn
                    t.wait_ge(s.h, v)
                    for sp in range(NSb):
                        P.acquire(t, sp)
                        P.acquire(t, 3 + sp)
                    for cin in range(DC):
                        for which in (0, 1):
                            lhs = wgu_bf[hc % 2][:, which * 1024 + cin * 128:
                                                 which * 1024 + (cin + 1) * 128]
                            for sp in range(NSb):
                                bank = sp if which == 0 else 3 + sp
                                mm = t.matmul(P.bank(bank), lhs,
                                              xn_blk[cin][:, sp * 512:(sp + 1) * 512],
                                              start=(cin == 0), stop=(cin == DC - 1))
                                if cin == DC - 1:
                                    sPE.inc(mm)
                                    if which == 0:
                                        g_tok[sp] = sPE.n
                                    else:
                                        u_tok[sp] = sPE.n
                wgu_mm[(bi, hc)] = sPE.n
                wgu_mm_last[0] = sPE.n

                for sp in range(NSb):
                    @block.scalar
                    def _(a, hc=hc, sp=sp, bi=bi):
                        a.wait_ge(sPE.h, g_tok[sp])
                        prev = (bi, hc - 1, sp) if hc > 0 else (bi - 1, HC0 - 1, sp)
                        if prev in h_tok:
                            a.wait_ge(sDVE.h, h_tok[prev])
                        a.wait_ge(sACT.h, sACT.n)
                        a.activation(sg[sp][:, :], P.bank(sp),
                                     AF.Silu).then_inc(sACT.h, 1)
                    sACT.n += 1
                    silu_tok[(bi, hc, sp)] = sACT.n
                    P.reader(sp, sACT)

                    @block.vector
                    def _(v, hc=hc, sp=sp, bi=bi):
                        v.wait_ge(sACT.h, silu_tok[(bi, hc, sp)])
                        v.wait_ge(sPE.h, u_tok[sp])
                        v.wait_ge(sDVE.h, sDVE.n)
                        v.tensor_tensor(h_blk[hc][:, sp * 512:(sp + 1) * 512],
                                        sg[sp][:, :], P.bank(3 + sp),
                                        ALU.mult).then_inc(sDVE.h, 1)
                    sDVE.n += 1
                    h_tok[(bi, hc, sp)] = sDVE.n
                    P.reader(3 + sp, sDVE)

            for sp in range(NSb):
                dmm_tok = {}

                @block.tensor
                def _(t, sp=sp, NSb=NSb, bi=bi):
                    for dc in range(DC):
                        P.acquire(t, dc)
                    t.wait_ge(sDVE.h, tok_wd)
                    t.wait_ge(sDVE.h, h_tok[(bi, HC0 - 1, sp)])
                    for hc in range(HC0):
                        if (hc, sp) not in h_tok:
                            pass
                        for dc in range(DC):
                            mm = t.matmul(P.bank(dc), wd_bf[hc][:, dc * 128:(dc + 1) * 128],
                                          h_blk[hc][:, sp * 512:(sp + 1) * 512],
                                          start=(hc == 0), stop=(hc == HC0 - 1))
                            if hc == HC0 - 1:
                                sPE.inc(mm)
                                dmm_tok[dc] = sPE.n

                for dc in range(DC):
                    @block.vector
                    def _(v, dc=dc, sp=sp):
                        v.wait_ge(sPE.h, dmm_tok[dc])
                        v.wait_ge(sDVE.h, sDVE.n)
                        v.scalar_tensor_tensor(
                            x_blk[dc][:, sp * 512:(sp + 1) * 512], P.bank(dc), 0.5,
                            x_blk[dc][:, sp * 512:(sp + 1) * 512],
                            ALU.mult, ALU.add).then_inc(sDVE.h, 1)
                    sDVE.n += 1
                    x_store_tok[dc] = sDVE.n
                    P.reader(dc, sDVE)

            for c in range(DC):
                @block.gpsimd
                def _(g, c=c, tb0=tb0, TB=TB, tok=x_store_tok[c]):
                    g.wait_ge(sDVE.h, tok)
                    sST.dma(g.dma_start(out=xo_d[c * 128:(c + 1) * 128, tb0:tb0 + TB],
                                        in_=x_blk[c][:, 0:TB]))
            prev_block_loadgate = sST.n



def build_l0attn():
    P = _mk_P()
    nc = P.nc
    aps = {
        "x": nc.dram_tensor("x", [D, TFULL], F32, kind="ExternalInput").ap(),
        "wq": nc.dram_tensor("wq", [D, D], F32, kind="ExternalInput").ap(),
        "wk": nc.dram_tensor("wk", [D, D], F32, kind="ExternalInput").ap(),
        "wv": nc.dram_tensor("wv", [D, D], F32, kind="ExternalInput").ap(),
        "wo": nc.dram_tensor("wo", [D, D], F32, kind="ExternalInput").ap(),
        "bq": nc.dram_tensor("bq", [128, 8], F32, kind="ExternalInput").ap(),
        "bk": nc.dram_tensor("bk", [128, 8], F32, kind="ExternalInput").ap(),
        "bo": nc.dram_tensor("bo", [128, 8], F32, kind="ExternalInput").ap(),
        "bvr": nc.dram_tensor("bvr", [1, D], F32, kind="ExternalInput").ap(),
        "maskA": nc.dram_tensor("maskA", [640, 512], BF16,
                                kind="ExternalInput").ap(),
        "maskB": nc.dram_tensor("maskB", [128, 128], BF16,
                                kind="ExternalInput").ap(),
        "xo": nc.dram_tensor("xo", [D, TOWN], F32, kind="ExternalOutput").ap(),
    }
    with P.es:
        block = P.es.enter_context(nc.Block())

        @block.gpsimd
        def _(g):
            P.init_consts(g)
        P.gate_consts(block)
        emit_l0attn(P, block, aps)
        sST = P.sem("st")

        @block.gpsimd
        def _(g):
            g.wait_ge(sST.h, sST.n)
    return nc


def emit_l0attn(P, block, aps, tag=""):
    nc = P.nc
    x_d, wq_d, wk_d, wv_d, wo_d = (aps["x"], aps["wq"], aps["wk"], aps["wv"],
                                   aps["wo"])
    bq_d, bk_d, bo_d, bvr_d = aps["bq"], aps["bk"], aps["bo"], aps["bvr"]
    mA_d, mB_d, xo_d = aps["maskA"], aps["maskB"], aps["xo"]
    xn_dram = nc.dram_tensor("xn_scr2" + tag, [D, TFULL], BF16).ap()

    A = P.A
    lb = P.lb
    K_fm = A.alloc(TFULL, BF16, DC)
    V_tm = A.alloc(D, BF16, TFULL // 128)
    Q_fm = A.alloc(TOWN, BF16, DC)
    bq_sb = A.alloc(8, F32)
    bk_sb = A.alloc(8, F32)
    bo_sb = A.alloc(8, F32)
    bvB = A.alloc(D, BF16)
    ident = P.ident_bf
    ao = A.alloc(512, BF16, DC)
    markR = A.mark()
    w_bf = A.alloc(8 * D, BF16, 2)
    w_st0 = A.alloc(D, F32)
    w_st1 = A.alloc(D, F32)
    w_st = [w_st0, w_st1]
    bvr_sb = w_st0[0:1, :]
    x_sp = A.alloc(512, F32, DC)
    endA = A.mark()
    A.reset(markR)
    wo_bf = A.alloc(8 * D, BF16)
    wo_st = A.alloc(D, F32)
    mA_sb = A.alloc(512, BF16, 5)
    mB_sb = A.alloc(128, BF16)
    x_sp2 = A.alloc(512, F32, DC)
    Pbuf2 = A.alloc(640, BF16, 2)
    PT2 = A.alloc(640, BF16, 2)
    mrow = A.alloc(1, F32, 2)
    nrm2p = A.alloc(1, F32, 32)
    negm2 = A.alloc(1, F32, 16)
    srow = A.alloc(1, F32, 6)
    endB = A.mark()
    A.off = max(endA, endB)
    xn_sp = ao
    assert A.off <= A.hi, A.off

    sPE, sACT, sDVE = P.sem("pe"), P.sem("act"), P.sem("dve")
    sLX, sST = P.sem("lx"), P.sem("st")
    sLW = [P.sem("lw0"), P.sem("lw1")]

    NSP = TFULL // 512

    if True:
        sGP = P.sem("gp")
        tok_ident = P._gate_val

        sSM = P.sem("sm")

        @block.sync
        def _(s):
            for dst, srcd in ((bq_sb, bq_d), (bk_sb, bk_d), (bo_sb, bo_d)):
                sSM.dma(s.dma_start(out=dst[:, :], in_=srcd[:, :]))
            sSM.dma(s.dma_start(out=bvr_sb[:, :], in_=bvr_d[:, :]))
        tok_small = sSM.n

        @block.tensor
        def _(t):
            t.wait_ge(sSM.h, tok_small)
            t.matmul(P.bank(0), P.ones_row_f32, bvr_sb[:, 0:512],
                     start=True, stop=True)
            sPE.inc(t.matmul(P.bank(1), P.ones_row_f32, bvr_sb[:, 512:1024],
                             start=True, stop=True))
        tok_bvmm = sPE.n

        @block.scalar
        def _(a):
            a.wait_ge(sPE.h, tok_bvmm)
            a.wait_ge(sACT.h, sACT.n)
            sACT.inc(a.activation(bvB[:, 0:512], P.bank(0), AF.Copy))
            a.wait_ge(sACT.h, sACT.n)
            sACT.inc(a.activation(bvB[:, 512:1024], P.bank(1), AF.Copy))
        P.reader(0, sACT)
        P.reader(1, sACT)
        tok_bvB = sACT.n

        def load_weights(wslot, wd_ap, gate_tok=None, dst=None, st2=None):
            toks = []
            for cin in range(DC):
                @block.sync
                def _(s, cin=cin):
                    if gate_tok is not None:
                        s.wait_ge(gate_tok[0].h, gate_tok[1])
                    if st2 is not None:
                        prev = toks[-1] if toks else getattr(P, "_wst_last", 0)
                    else:
                        prev = (toks[-2] if len(toks) >= 2
                                else getattr(P, "_wst_last", 0))
                    if prev:
                        s.wait_ge(sDVE.h, prev)
                    stg = st2 if st2 is not None else w_st[cin % 2]
                    sLW[cin % 2].dma(s.dma_start(
                        out=stg[:, :],
                        in_=wd_ap[cin * 128:(cin + 1) * 128, :]))
                tok_l = sLW[cin % 2].n

                @block.vector
                def _(v, cin=cin, tok_l=tok_l):
                    v.wait_ge(sLW[cin % 2].h, tok_l)
                    v.wait_ge(sDVE.h, sDVE.n)
                    d = dst if dst is not None else w_bf[wslot]
                    stg = st2 if st2 is not None else w_st[cin % 2]
                    v.tensor_scalar(d[:, cin * 1024:(cin + 1) * 1024],
                                    stg[:, :], 1.0, None,
                                    ALU.mult).then_inc(sDVE.h, 1)
                sDVE.n += 1
                toks.append(sDVE.n)
            P._wst_last = sDVE.n
            return sDVE.n

        tok_wq = load_weights(0, wq_d, gate_tok=(sACT, tok_bvB))
        tok_wk = load_weights(1, wk_d)

        qk_mm_toks = {}
        prev_norm_gate = None
        for sp in range(NSP):
            @block.gpsimd
            def _(g, sp=sp):
                if sp >= 1:
                    g.wait_ge(sPE.h, P._ln_stats_tok or 0)
                    g.wait_ge(sDVE.h, sDVE.n)
                for c in range(DC):
                    sLX.dma(g.dma_start(out=x_sp[c][:, :],
                                        in_=x_d[c * 128:(c + 1) * 128,
                                                sp * 512:(sp + 1) * 512]))
            tok_x = [(sLX, sLX.n)] * DC

            xn_war = ([(sPE, qk_mm_toks.get(sp - 1, 0)), (sST, sST.n)]
                      if sp >= 1 else None)
            tok_xn = emit_ln(P, block, x_sp, xn_sp, 512, lb, tok_x=tok_x,
                             mode="ln", xn_war=xn_war)

            @block.gpsimd
            def _(g, sp=sp, tok=tok_xn):
                g.wait_ge(tok[0].h, tok[1])
                for c in range(DC):
                    sST.dma(g.dma_start(
                        out=xn_dram[c * 128:(c + 1) * 128, sp * 512:(sp + 1) * 512],
                        in_=xn_sp[c][:, :]))

            for co in range(DC):
                bank = co % 2

                @block.tensor
                def _(t, sp=sp, co=co, bank=bank, tok=tok_xn):
                    if co == 0:
                        t.wait_ge(tok[0].h, tok[1])
                        if sp == 0:
                            t.wait_ge(sDVE.h, max(tok_wq, tok_wk))
                    P.acquire(t, bank)
                    for cin in range(DC):
                        mm = t.matmul(P.bank(bank),
                                      w_bf[1][:, cin * 1024 + co * 128:
                                              cin * 1024 + co * 128 + 128],
                                      xn_sp[cin][:, :],
                                      start=(cin == 0), stop=(cin == DC - 1))
                    sPE.inc(mm)
                ktok = sPE.n

                @block.vector
                def _(v, sp=sp, co=co, bank=bank, ktok=ktok):
                    v.wait_ge(sPE.h, ktok)
                    v.wait_ge(sDVE.h, sDVE.n)
                    sDVE.inc(v.tensor_scalar(
                        K_fm[co][:, sp * 512:(sp + 1) * 512], P.bank(bank),
                        bk_sb[:, co:co + 1], None, ALU.add))
                P.reader(bank, sDVE)

                if sp >= 1:
                    bank2 = 2 + co % 2

                    @block.tensor
                    def _(t, sp=sp, co=co, bank2=bank2):
                        P.acquire(t, bank2)
                        for cin in range(DC):
                            mm = t.matmul(P.bank(bank2),
                                          w_bf[0][:, cin * 1024 + co * 128:
                                                  cin * 1024 + co * 128 + 128],
                                          xn_sp[cin][:, :],
                                          start=(cin == 0), stop=(cin == DC - 1))
                        sPE.inc(mm)
                    qtok = sPE.n

                    @block.vector
                    def _(v, sp=sp, co=co, bank2=bank2, qtok=qtok):
                        v.wait_ge(sPE.h, qtok)
                        v.wait_ge(sDVE.h, sDVE.n)
                        sDVE.inc(v.tensor_scalar(
                            Q_fm[co][:, (sp - 1) * 512:sp * 512],
                            P.bank(bank2), bq_sb[:, co:co + 1], None, ALU.add))
                    P.reader(bank2, sDVE)
            qk_mm_toks[sp] = sPE.n

        tok_qk_all = (sDVE, sDVE.n)

        tok_wv = load_weights(1, wv_d, gate_tok=(sPE, qk_mm_toks[NSP - 1]))
        tok_xn_store = sST.n
        v_mm_toks = {}
        for sp in range(NSP):
            @block.gpsimd
            def _(g, sp=sp):
                g.wait_ge(sST.h, tok_xn_store)
                if sp >= 1:
                    g.wait_ge(sPE.h, v_mm_toks[sp - 1])
                else:
                    g.wait_ge(sPE.h, qk_mm_toks[NSP - 1])
                for c in range(DC):
                    sLX.dma(g.dma_start(out=xn_sp[c][:, :],
                                        in_=xn_dram[c * 128:(c + 1) * 128,
                                                    sp * 512:(sp + 1) * 512]))
            tok_xnl = sLX.n

            for tci in range(4):
                for dsp in range(2):
                    tc = sp * 4 + tci
                    bank = 2 * (tci % 2) + dsp

                    @block.tensor
                    def _(t, sp=sp, tci=tci, dsp=dsp, bank=bank, tok_xnl=tok_xnl):
                        if tci == 0 and dsp == 0:
                            t.wait_ge(sLX.h, tok_xnl)
                            if sp == 0:
                                t.wait_ge(sDVE.h, tok_wv)
                        P.acquire(t, bank)
                        for cin in range(DC):
                            mm = t.matmul(
                                P.bank(bank),
                                xn_sp[cin][:, tci * 128:(tci + 1) * 128],
                                w_bf[1][:, cin * 1024 + dsp * 512:
                                        cin * 1024 + dsp * 512 + 512],
                                start=(cin == 0), stop=(cin == DC - 1))
                        sPE.inc(mm)
                    vtok = sPE.n

                    @block.vector
                    def _(v, tc=tc, dsp=dsp, bank=bank, vtok=vtok):
                        v.wait_ge(sPE.h, vtok)
                        v.wait_ge(sACT.h, tok_bvB)
                        v.wait_ge(sDVE.h, sDVE.n)
                        v.tensor_tensor(
                            V_tm[tc][:, dsp * 512:(dsp + 1) * 512],
                            P.bank(bank), bvB[:, dsp * 512:(dsp + 1) * 512],
                            ALU.add).then_inc(sDVE.h, 1)
                    sDVE.n += 1
                    P.reader(bank, sDVE)
            v_mm_toks[sp] = sPE.n
        tok_v_all = (sDVE, sDVE.n)

        @block.sync
        def _(s):
            s.wait_ge(sPE.h, v_mm_toks[NSP - 1])
            for i in range(5):
                sSM.dma(s.dma_start(out=mA_sb[i][:, :],
                                    in_=mA_d[i * 128:(i + 1) * 128, :]))
            sSM.dma(s.dma_start(out=mB_sb[:, :], in_=mB_d[:, :]))
        tok_masks = sSM.n
        tok_wo = load_weights(None, wo_d, gate_tok=(sPE, v_mm_toks[NSP - 1]),
                              dst=wo_bf, st2=wo_st)
        sqr = lb.sq
        nrm_mm2 = []
        sq_cnt = [0]
        for which, F, nsp in ((0, Q_fm, 4), (1, K_fm, 5)):
            for c0 in range(DC):
                for sp8 in range(nsp):
                    idx = sq_cnt[0]
                    sq_cnt[0] += 1
                    par = idx % 2

                    @block.vector
                    def _(v, F=F, c0=c0, sp8=sp8, par=par, idx=idx):
                        if idx == 0:
                            s, vv = tok_v_all
                            v.wait_ge(s.h, vv)
                        if len(nrm_mm2) >= 2:
                            v.wait_ge(sPE.h, nrm_mm2[-2])
                        v.wait_ge(sDVE.h, sDVE.n)
                        v.tensor_tensor(
                            sqr[par][:, :],
                            F[c0][:, sp8 * 512:(sp8 + 1) * 512],
                            F[c0][:, sp8 * 512:(sp8 + 1) * 512],
                            ALU.mult).then_inc(sDVE.h, 1)
                    sDVE.n += 1
                    tok_sq = sDVE.n

                    for pbi in (0, 64):
                        hh = c0 * 2 + pbi // 64
                        bank = 2 + idx % 2

                        @block.tensor
                        def _(t, par=par, pbi=pbi, bank=bank, tok_sq=tok_sq):
                            t.wait_ge(sDVE.h, tok_sq)
                            P.acquire(t, bank)
                            t.matmul(P.bank(bank)[0:1, :],
                                     P.ones_col[pbi:pbi + 64, :],
                                     sqr[par][pbi:pbi + 64, :],
                                     start=True, stop=True).then_inc(sPE.h, 1)
                        sPE.n += 1
                        tok_mm = sPE.n

                        @block.vector
                        def _(v, which=which, hh=hh, bank=bank,
                              tok_mm=tok_mm, first=(sp8 == 0)):
                            v.wait_ge(sPE.h, tok_mm)
                            v.wait_ge(sDVE.h, sDVE.n)
                            sDVE.inc(v.reduce_max(mrow[0][0:1, :],
                                                  P.bank(bank)[0:1, :],
                                                  axis=AX))
                            v.wait_ge(sDVE.h, sDVE.n)
                            dst = nrm2p[which * 16 + hh][0:1, :]
                            if first:
                                sDVE.inc(v.tensor_copy(dst, mrow[0][0:1, :]))
                            else:
                                sDVE.inc(v.tensor_tensor(dst, dst,
                                                         mrow[0][0:1, :],
                                                         ALU.max))
                        P.reader(bank, sDVE)
                    nrm_mm2.append(sPE.n)

        for h in range(NHEAD):
            @block.vector
            def _(v, h=h):
                v.wait_ge(sDVE.h, sDVE.n)
                sDVE.inc(v.tensor_tensor(mrow[0][0:1, :], nrm2p[h][0:1, :],
                                         nrm2p[16 + h][0:1, :], ALU.mult))
            tok_t = sDVE.n

            @block.scalar
            def _(a, tok_t=tok_t):
                a.wait_ge(sDVE.h, tok_t)
                a.wait_ge(sACT.h, sACT.n)
                sACT.inc(a.activation(mrow[1][0:1, :], mrow[0][0:1, :], AF.Ln))
                a.wait_ge(sACT.h, sACT.n)
                sACT.inc(a.activation(mrow[1][0:1, :], mrow[1][0:1, :], AF.Exp,
                                      scale=0.5, bias=LN_EIGHTH))
            tok_mh = sACT.n

            @block.vector
            def _(v, tok_mh=tok_mh):
                v.wait_ge(sACT.h, tok_mh)
                v.wait_ge(sDVE.h, sDVE.n)
                sDVE.inc(v.tensor_scalar(mrow[0][0:1, :], mrow[1][0:1, :],
                                         -1.0, None, ALU.mult))
            tok_ng = sDVE.n

            @block.tensor
            def _(t, tok_ng=tok_ng):
                t.wait_ge(sDVE.h, tok_ng)
                P.acquire(t, 2)
                t.matmul(P.bank(2)[:, 0:1], P.ones_row_f32,
                         mrow[0][0:1, 0:1], start=True,
                         stop=True).then_inc(sPE.h, 1)
            sPE.n += 1
            tok_bc = sPE.n

            @block.vector
            def _(v, h=h, tok_bc=tok_bc):
                v.wait_ge(sPE.h, tok_bc)
                v.wait_ge(sDVE.h, sDVE.n)
                sDVE.inc(v.tensor_copy(negm2[h][:, :], P.bank(2)[:, 0:1]))
            P.reader(2, sDVE)
        tok_negm = sDVE.n

        tp_tok = {0: 0, 1: 0}
        pv_tok = {0: 0, 1: 0}
        wo_last = [0]
        xr_toks = {}
        first_it = [True]

        def stage1(h, qt, p):
            c0, pb = h // 2, 64 * (h % 2)
            qo = qt * 128
            bA = 2 * p

            @block.tensor
            def _(t, c0=c0, pb=pb, qo=qo, bA=bA):
                P.acquire(t, bA)
                qsl = Q_fm[c0][pb:pb + 64, qo:qo + 128]
                t.matmul(P.bank(bA), qsl,
                         K_fm[c0][pb:pb + 64, qo:qo + 512],
                         start=True, stop=True).then_inc(sPE.h, 1)
                P.acquire(t, bA + 1)
                t.matmul(P.bank(bA + 1, w=128), qsl,
                         K_fm[c0][pb:pb + 64, qo + 512:qo + 640],
                         start=True, stop=True).then_inc(sPE.h, 1)
            sPE.n += 2
            tok_sc = sPE.n

            @block.scalar
            def _(a, h=h, p=p, bA=bA, tok_sc=tok_sc):
                if first_it[0]:
                    a.wait_ge(sDVE.h, tok_negm)
                    first_it[0] = False
                a.wait_ge(sPE.h, tp_tok[p])
                a.wait_ge(sPE.h, tok_sc)
                a.wait_ge(sACT.h, sACT.n)
                sACT.inc(a.activation(Pbuf2[p][:, :],
                                      P.bank(bA, w=640), AF.Exp,
                                      scale=0.125, bias=negm2[h][:, :]))
            tok_exp = sACT.n
            P.reader(bA, sACT)
            P.reader(bA + 1, sACT)

            @block.vector
            def _(v, p=p, qt=qt, tok_exp=tok_exp):
                v.wait_ge(sACT.h, tok_exp)
                v.wait_ge(sSM.h, tok_masks)
                v.wait_ge(sDVE.h, sDVE.n)
                sDVE.inc(v.scalar_tensor_tensor(
                    Pbuf2[p][:, 0:512], Pbuf2[p][:, 0:512], 1.0,
                    mA_sb[min(qt, 4)][:, :], ALU.mult, ALU.mult,
                    accum_out=srow[3 * p][:, :]))
                v.wait_ge(sDVE.h, sDVE.n)
                sDVE.inc(v.scalar_tensor_tensor(
                    Pbuf2[p][:, 512:640], Pbuf2[p][:, 512:640], 1.0,
                    mB_sb[:, :], ALU.mult, ALU.mult,
                    accum_out=srow[3 * p + 1][:, :]))
                v.wait_ge(sDVE.h, sDVE.n)
                sDVE.inc(v.tensor_tensor(srow[3 * p][:, :], srow[3 * p][:, :],
                                         srow[3 * p + 1][:, :], ALU.add))
                v.wait_ge(sDVE.h, sDVE.n)
                sDVE.inc(v.reciprocal(srow[3 * p][:, :], srow[3 * p][:, :]))
                v.wait_ge(sDVE.h, sDVE.n)
                sDVE.inc(v.tensor_scalar(Pbuf2[p][:, :], Pbuf2[p][:, :],
                                         srow[3 * p][:, :], None, ALU.mult))
            return sDVE.n

        def stage2a(h, qt, p, tok_norm):
            @block.tensor
            def _(t, p=p, tok_norm=tok_norm):
                t.wait_ge(sDVE.h, tok_norm)
                P.acquire(t, 4 + p)
                for k in range(5):
                    mm = t.transpose(
                        P.bank_bf(4 + p, w=640)[:, k * 128:(k + 1) * 128],
                        Pbuf2[p][:, k * 128:(k + 1) * 128], ident[:, :])
                sPE.inc(mm)
            tp_tok[p] = sPE.n
            ttok = sPE.n

            @block.vector
            def _(v, p=p, ttok=ttok):
                v.wait_ge(sPE.h, ttok)
                v.wait_ge(sDVE.h, sDVE.n)
                sDVE.inc(v.tensor_copy(PT2[p][:, :],
                                       P.bank_bf(4 + p, w=640)))
            P.reader(4 + p, sDVE)
            return sDVE.n

        def stage2b(h, qt, p, tok_pt):
            c0, pb = h // 2, 64 * (h % 2)
            qi = qt % 4

            @block.tensor
            def _(t, h=h, qt=qt, p=p, tok_pt=tok_pt):
                t.wait_ge(sDVE.h, tok_pt)
                P.acquire(t, 6 + p)
                for k in range(5):
                    mm = t.matmul(P.bank(6 + p, w=128)[0:64, :],
                                  V_tm[qt + k][:, h * 64:(h + 1) * 64],
                                  PT2[p][:, k * 128:(k + 1) * 128],
                                  start=(k == 0), stop=(k == 4))
                sPE.inc(mm)
            pv_tok[p] = sPE.n
            tok_pv = sPE.n

            @block.vector
            def _(v, p=p, c0=c0, pb=pb, qi=qi, tok_pv=tok_pv):
                v.wait_ge(sPE.h, tok_pv)
                v.wait_ge(sPE.h, wo_last[0])
                v.wait_ge(sDVE.h, sDVE.n)
                sDVE.inc(v.tensor_copy(
                    ao[c0][pb:pb + 64, qi * 128:(qi + 1) * 128],
                    P.bank(6 + p, w=128)[0:64, :]))
            P.reader(6 + p, sDVE)
            return sDVE.n

        def prefetch_x(qs):
            @block.gpsimd
            def _(g, qs=qs):
                g.wait_ge(sDVE.h, sDVE.n)
                g.wait_ge(sST.h, sST.n)
                g.wait_ge(sPE.h, sPE.n if qs == 0 else 0)
                for c in range(DC):
                    sLX.dma(g.dma_start(
                        out=x_sp2[c][:, :],
                        in_=x_d[c * 128:(c + 1) * 128,
                                512 + qs * 512:512 + (qs + 1) * 512]))
            xr_toks[qs] = sLX.n

        def wo_proj(qs, tok_ao):
            tok_xr = xr_toks[qs]
            for dc in range(DC):
                @block.tensor
                def _(t, dc=dc, qs=qs, tok_ao=tok_ao):
                    if dc == 0:
                        t.wait_ge(sDVE.h, tok_ao)
                        if qs == 0:
                            t.wait_ge(sDVE.h, tok_wo)
                    bank = 2 + dc % 2
                    P.acquire(t, bank)
                    for cin in range(DC):
                        mm = t.matmul(P.bank(bank),
                                      wo_bf[:, cin * 1024 + dc * 128:
                                            cin * 1024 + dc * 128 + 128],
                                      ao[cin][:, :],
                                      start=(cin == 0), stop=(cin == DC - 1))
                    sPE.inc(mm)
                tok_wo_mm = sPE.n

                @block.vector
                def _(v, dc=dc, tok=tok_wo_mm, tok_xr=tok_xr):
                    v.wait_ge(sPE.h, tok)
                    v.wait_ge(sLX.h, tok_xr)
                    bank = 2 + dc % 2
                    v.wait_ge(sDVE.h, sDVE.n)
                    v.scalar_tensor_tensor(
                        x_sp2[dc][:, :], P.bank(bank), bo_sb[:, dc:dc + 1],
                        x_sp2[dc][:, :], ALU.add, ALU.add).then_inc(sDVE.h, 1)
                sDVE.n += 1
                P.reader(2 + dc % 2, sDVE)
                tok_res = sDVE.n

                @block.gpsimd
                def _(g, dc=dc, qs=qs, tok=tok_res):
                    g.wait_ge(sDVE.h, tok)
                    sST.dma(g.dma_start(
                        out=xo_d[dc * 128:(dc + 1) * 128,
                                 qs * 512:(qs + 1) * 512],
                        in_=x_sp2[dc][:, :]))
            wo_last[0] = tok_wo_mm

        prefetch_x(0)
        pend1 = None
        pend2 = None
        it = 0

        def run2b(args):
            h2, qt2, p2, tok_pt = args
            tok_ao_e = stage2b(h2, qt2, p2, tok_pt)
            if qt2 % 4 == 3 and h2 == NHEAD - 1:
                pqs = qt2 // 4
                wo_proj(pqs, tok_ao_e)
                if pqs + 1 < 4:
                    prefetch_x(pqs + 1)

        for qs in range(4):
            for h in range(NHEAD):
                for qi in range(4):
                    qt = qs * 4 + qi
                    p = it % 2
                    it += 1
                    tn = stage1(h, qt, p)
                    np1 = (h, qt, p, tn)
                    np2 = None
                    if pend1 is not None:
                        tok_pt = stage2a(*pend1)
                        np2 = (pend1[0], pend1[1], pend1[2], tok_pt)
                    if pend2 is not None:
                        run2b(pend2)
                    pend1, pend2 = np1, np2
        tok_pt = stage2a(*pend1)
        run2b(pend2)
        run2b((pend1[0], pend1[1], pend1[2], tok_pt))



def build_moe():
    P = _mk_P()
    nc = P.nc
    aps = {
        "x": nc.dram_tensor("x", [D, TOWN], F32, kind="ExternalInput").ap(),
        "w1": nc.dram_tensor("w1", [NE, D, H1], F32, kind="ExternalInput").ap(),
        "w2": nc.dram_tensor("w2", [NE, H1, D], F32, kind="ExternalInput").ap(),
        "wg": nc.dram_tensor("wg", [128, 64], F32, kind="ExternalInput").ap(),
        "bg": nc.dram_tensor("bg", [8, 1], F32, kind="ExternalInput").ap(),
        "sel8": nc.dram_tensor("sel8", [8, 1024], BF16,
                               kind="ExternalInput").ap(),
        "xo": nc.dram_tensor("xo", [D, TOWN], F32, kind="ExternalOutput").ap(),
    }
    with P.es:
        block = P.es.enter_context(nc.Block())

        @block.gpsimd
        def _(g):
            P.init_consts(g)
        P.gate_consts(block)
        P._moe_wgr = None
        emit_moe(P, block, aps, tag="t")
        sST = P.sem("st")

        @block.gpsimd
        def _(g):
            g.wait_ge(sST.h, sST.n)
    return nc


def emit_moe(P, block, aps, tag=""):
    nc = P.nc
    x_d, w1_d, w2_d = aps["x"], aps["w1"], aps["w2"]
    wg_d, bg_d, sel8_d, xo_d = aps["wg"], aps["bg"], aps["sel8"], aps["xo"]
    xn_dram = nc.dram_tensor("xn_moe_" + tag, [D, TOWN], BF16).ap()

    A = P.A
    lb = P.lb
    wg_f32 = A.alloc(64, F32)
    if getattr(P, "_moe_wgr", None) is None:
        P._moe_wgr = P.sbuf(64, F32R)
    wg_r = P._moe_wgr
    bg_sb = A.alloc(1, F32)
    x_res = A.alloc(TOWN, F32, DC)
    xn_buf = A.alloc(1024, BF16, DC)
    mxn = A.alloc(1024, BF16, DC)
    hbuf = A.alloc(1024, BF16, 8)
    w1_st = A.alloc(1024, F32, 2)
    w1_bf = A.alloc(1024, BF16, 2)
    w2_st = A.alloc(1024, F32, 2)
    w2_ring = A.alloc(1024, BF16, 16)
    maskT = A.alloc(TOWN, BF16)
    lgsb = A.alloc(512, F32)
    lgtm = A.alloc(8, F32)
    rmx = A.alloc(1, F32)
    msk_tm = A.alloc(8, F32)
    idf32 = P.ident_f32
    mbc = A.alloc(1024, BF16)
    sel8 = A.alloc(1024, BF16)
    assert A.off <= A.hi, A.off

    sPE, sACT, sDVE = P.sem("pe"), P.sem("act"), P.sem("dve")
    sLX, sST = P.sem("lx"), P.sem("st")
    sLW = [P.sem("lw0"), P.sem("lw1")]
    sLW2 = [P.sem("lw2a"), P.sem("lw2b")]
    sSM = P.sem("sm")

    if True:
        sGP = P.sem("gp")

        @block.gpsimd
        def _(g):
            g.wait_ge(sGP.h, P._gate_val)
            sGP.inc(g.memset(maskT[:, :], 0.0))
            sGP.inc(g.memset(sel8[:, :], 0.0))
            sGP.inc(g.memset(lgsb[:, :], 0.0))
            g.wait_ge(sDVE.h, sDVE.n)
            g.wait_ge(sPE.h, sPE.n)
            for c in range(DC):
                sLX.dma(g.dma_start(out=x_res[c][:, :],
                                    in_=x_d[c * 128:(c + 1) * 128, :]))
        tok_ident = sGP.n
        tok_x = [(sLX, sLX.n)] * DC

        @block.sync
        def _(s):
            s.wait_ge(sGP.h, tok_ident)
            s.wait_ge(sDVE.h, sDVE.n)
            sSM.dma(s.dma_start(out=wg_f32[:, :], in_=wg_d[:, :]))
            sSM.dma(s.dma_start(out=bg_sb[0:8, :], in_=bg_d[:, :]))
            sSM.dma(s.dma_start(out=sel8[0:8, :], in_=sel8_d[:, :]))
        tok_wg = sSM.n

        @block.vector
        def _(v):
            v.wait_ge(sSM.h, tok_wg)
            v.wait_ge(sDVE.h, sDVE.n)
            v.tensor_copy(wg_r[:, :], wg_f32[:, :]).then_inc(sDVE.h, 1)
        sDVE.n += 1

        def logits_cb(sp, tok_pe):
            @block.vector
            def _(v, sp=sp, tok=tok_pe):
                v.wait_ge(sPE.h, tok)
                v.wait_ge(sDVE.h, sDVE.n)
                v.tensor_scalar(lgsb[0:8, :], P.bank(3)[0:8, :],
                                bg_sb[0:8, :], None, ALU.add).then_inc(sDVE.h, 1)
            sDVE.n += 1
            tok_lg = sDVE.n
            P.reader(3, sDVE)

            for tc in range(4):
                @block.tensor
                def _(t, sp=sp, tc=tc, tok=tok_lg):
                    if tc == 0:
                        t.wait_ge(sDVE.h, tok)
                        t.wait_ge(sGP.h, tok_ident)
                    P.acquire(t, 2)
                    sPE.inc(t.transpose(P.bank(2)[:, 0:8],
                                        lgsb[0:8, tc * 128:(tc + 1) * 128],
                                        idf32[0:8, 0:8]))
                tok_tp = sPE.n

                @block.vector
                def _(v, tok=tok_tp):
                    v.wait_ge(sPE.h, tok)
                    v.wait_ge(sDVE.h, sDVE.n)
                    sDVE.inc(v.tensor_copy(lgtm[:, :], P.bank(2)[:, 0:8]))
                    v.wait_ge(sDVE.h, sDVE.n)
                    sDVE.inc(v.reduce_max(rmx[:, :], lgtm[:, :], axis=AX))
                    v.wait_ge(sDVE.h, sDVE.n)
                    sDVE.inc(v.tensor_scalar(msk_tm[:, :], lgtm[:, :],
                                             rmx[:, :], None, ALU.is_ge))
                tok_msk = sDVE.n
                P.reader(2, sDVE)

                @block.tensor
                def _(t, tok=tok_msk):
                    t.wait_ge(sDVE.h, tok)
                    P.acquire(t, 2)
                    sPE.inc(t.transpose(P.bank(2)[0:8, 0:128], msk_tm[:, :],
                                        idf32[:, :]))
                tok_tp2 = sPE.n

                @block.scalar
                def _(a, sp=sp, tc=tc, tok=tok_tp2):
                    a.wait_ge(sPE.h, tok)
                    a.wait_ge(sACT.h, sACT.n)
                    sACT.inc(a.activation(
                        maskT[0:8, sp * 512 + tc * 128:sp * 512 + (tc + 1) * 128],
                        P.bank(2)[0:8, 0:128], AF.Copy))
                P.reader(2, sACT)

        wg_cols = [wg_r[:, c * 8:(c + 1) * 8] for c in range(DC)]
        xn_all = [A.base.bitcast(BF16)[:, 0:1] for _ in range(1)]
        xn_full_view = [None] * DC
        ln_store_toks = []
        for sp in range(4):
            xv = [x_res[c][:, sp * 512:(sp + 1) * 512] for c in range(DC)]
            xnv = [mxn[c][:, 0:512] for c in range(DC)]
            xn_war = None
            if ln_store_toks:
                xn_war = [(sST, ln_store_toks[-1])]
            tok_xn = emit_ln(
                P, block, xv, xnv, 512, lb, tok_x=tok_x, mode="moe",
                logits=(wg_cols, 3,
                        (lambda _s, tok, SP=sp: logits_cb(SP, tok))),
                xn_war=xn_war)

            @block.gpsimd
            def _(g, sp=sp, tok=tok_xn):
                g.wait_ge(tok[0].h, tok[1])
                for c in range(DC):
                    sST.dma(g.dma_start(
                        out=xn_dram[c * 128:(c + 1) * 128,
                                    sp * 512:(sp + 1) * 512],
                        in_=mxn[c][:, 0:512]))
            ln_store_toks.append(sST.n)
        tok_ln_done = sST.n
        tok_mask_done = (sACT, sACT.n)

        w1_conv = {}
        w1_mm = {}
        w2_conv = {}
        xn_load_toks = {}
        mxn_toks = {}
        last_mm1 = [0]
        last_mm2 = [0]
        mm2_by_unit = {}
        last_mxn_dve = [0]
        NHCG = 4

        units = [(ts, e, hcg) for ts in range(2) for e in range(NE)
                 for hcg in range(NHCG)]

        for ui, (ts, e, hcg) in enumerate(units):
            first_of_ts = (e == 0 and hcg == 0)
            if first_of_ts:
                @block.gpsimd
                def _(g, ts=ts):
                    g.wait_ge(sST.h, tok_ln_done)
                    if ts > 0:
                        g.wait_ge(sPE.h, sPE.n)
                    for c in range(DC):
                        sLX.dma(g.dma_start(
                            out=xn_buf[c][:, :],
                            in_=xn_dram[c * 128:(c + 1) * 128,
                                        ts * 1024:(ts + 1) * 1024]))
                xn_load_toks[ts] = sLX.n

            def emit_mxn(ts2, e2, gate_mm1):
                @block.tensor
                def _(t, ts2=ts2, e2=e2):
                    s, v = tok_mask_done
                    t.wait_ge(s.h, v)
                    t.wait_ge(sSM.h, tok_wg)
                    t.wait_ge(sLX.h, xn_load_toks[ts2])
                    for half in range(2):
                        P.acquire(t, 4 + half)
                        sPE.inc(t.matmul(
                            P.bank(4 + half),
                            sel8[0:8, e2 * 128:(e2 + 1) * 128],
                            maskT[0:8, ts2 * 1024 + half * 512:
                                  ts2 * 1024 + (half + 1) * 512],
                            start=True, stop=True))
                tok_mbcmm = sPE.n

                @block.scalar
                def _(a, tok=tok_mbcmm):
                    a.wait_ge(sPE.h, tok)
                    a.wait_ge(sDVE.h, last_mxn_dve[0])
                    for half in range(2):
                        a.wait_ge(sACT.h, sACT.n)
                        sACT.inc(a.activation(mbc[:, half * 512:(half + 1) * 512],
                                              P.bank(4 + half), AF.Copy))
                P.reader(4, sACT)
                P.reader(5, sACT)
                tok_mbc = sACT.n

                for c in range(DC):
                    @block.vector
                    def _(v, c=c, tok=tok_mbc, gate_mm1=gate_mm1):
                        if c == 0:
                            v.wait_ge(sACT.h, tok)
                            v.wait_ge(sPE.h, gate_mm1)
                        v.wait_ge(sDVE.h, sDVE.n)
                        v.tensor_tensor(mxn[c][:, :], xn_buf[c][:, :],
                                        mbc[:, :], ALU.mult).then_inc(sDVE.h, 1)
                    sDVE.n += 1
                mxn_toks[(ts2, e2)] = sDVE.n
                last_mxn_dve[0] = sDVE.n

            if (ts, e) not in mxn_toks:
                emit_mxn(ts, e, last_mm1[0])

            gelu_last = [0]
            gelu_toks = []
            w2c_toks = []
            for j in range(8):
                hc = hcg * 8 + j
                par = j % 2
                b0 = 6 if par == 0 else 4

                @block.sync
                def _(s, e=e, hc=hc, par=par):
                    prev = w1_conv.get(("last", par))
                    if prev:
                        s.wait_ge(sGP.h, prev)
                    sLW[par].dma(s.dma_start(
                        out=w1_st[par][:, :].rearrange("p (c h) -> p c h", c=DC),
                        in_=w1_d[e, :, hc * 128:(hc + 1) * 128].rearrange(
                            "(c p) h -> p c h", p=128)))
                tok_l = sLW[par].n

                @block.gpsimd
                def _(g, par=par, tok_l=tok_l):
                    g.wait_ge(sLW[par].h, tok_l)
                    prevmm = w1_mm.get(("last", par))
                    if prevmm:
                        g.wait_ge(sPE.h, prevmm)
                    g.wait_ge(sGP.h, sGP.n)
                    g.tensor_copy(w1_bf[par][:, :],
                                  w1_st[par][:, :]).then_inc(sGP.h, 1)
                sGP.n += 1
                w1_conv[("last", par)] = sGP.n
                tok_conv = sGP.n

                sub_toks = []

                @block.tensor
                def _(t, ts=ts, e=e, j=j, par=par, b0=b0, tok=tok_conv):
                    t.wait_ge(sGP.h, tok)
                    t.wait_ge(sDVE.h, mxn_toks[(ts, e)])
                    P.acquire(t, b0)
                    P.acquire(t, b0 + 1)
                    for cin in range(DC):
                        for sub in range(2):
                            mm = t.matmul(
                                P.bank(b0 + sub),
                                w1_bf[par][:, cin * 128:(cin + 1) * 128],
                                mxn[cin][:, sub * 512:(sub + 1) * 512],
                                start=(cin == 0), stop=(cin == DC - 1))
                            if cin == DC - 1:
                                sPE.inc(mm)
                                sub_toks.append(sPE.n)
                w1_mm[("last", par)] = sPE.n

                @block.scalar
                def _(a, j=j, b0=b0, sub_toks=tuple(sub_toks)):
                    for sub in range(2):
                        a.wait_ge(sPE.h, sub_toks[sub])
                        if sub == 0:
                            a.wait_ge(sPE.h, last_mm2[0])
                        a.wait_ge(sACT.h, sACT.n)
                        sACT.inc(a.activation(
                            hbuf[j][:, sub * 512:(sub + 1) * 512],
                            P.bank(b0 + sub), AF.Gelu_apprx_tanh))
                P.reader(b0, sACT)
                P.reader(b0 + 1, sACT)
                gelu_last[0] = sACT.n
                gelu_toks.append(sACT.n)

                @block.sync
                def _(s, e=e, hc=hc, par=par):
                    prev = w2_conv.get(("last", par))
                    if prev:
                        s.wait_ge(sDVE.h, prev)
                    sLW2[par].dma(s.dma_start(
                        out=w2_st[par][:, :],
                        in_=w2_d[e, hc * 128:(hc + 1) * 128, :]))
                tok_l2 = sLW2[par].n

                @block.vector
                def _(v, j=j, par=par, tok_l2=tok_l2, ui=ui):
                    v.wait_ge(sLW2[par].h, tok_l2)
                    if ui >= 2:
                        v.wait_ge(sPE.h, mm2_by_unit[ui - 2])
                    v.wait_ge(sDVE.h, sDVE.n)
                    v.tensor_copy(w2_ring[(ui % 2) * 8 + j][:, :],
                                  w2_st[par][:, :]).then_inc(sDVE.h, 1)
                sDVE.n += 1
                w2_conv[("last", par)] = sDVE.n
                w2c_toks.append(sDVE.n)
            tok_gelu = sACT.n
            ringbase = (ui % 2) * 8
            tok_w2 = sDVE.n

            if hcg == NHCG - 1 and e < NE - 1:
                cur_mm1 = max(w1_mm.get(("last", 0), 0),
                              w1_mm.get(("last", 1), 0))
                emit_mxn(ts, e + 1, cur_mm1)

            for dgrp, dcs in ((0, tuple(range(6))), (1, (6, 7))):
                for sub in range(2):
                    dmm_tok = {}

                    @block.tensor
                    def _(t, ts=ts, sub=sub, dcs=dcs, dgrp=dgrp,
                          ringbase=ringbase, gelu_toks=tuple(gelu_toks),
                          w2c_toks=tuple(w2c_toks)):
                        first = dgrp == 0 and sub == 0
                        for j in range(8):
                            if first:
                                t.wait_ge(sACT.h, gelu_toks[j])
                                t.wait_ge(sDVE.h, w2c_toks[j])
                            for dc in dcs:
                                bank = dc if dgrp == 0 else dc - 6
                                if j == 0:
                                    P.acquire(t, bank)
                                mm = t.matmul(
                                    P.bank(bank),
                                    w2_ring[ringbase + j][:, dc * 128:
                                                          (dc + 1) * 128],
                                    hbuf[j][:, sub * 512:(sub + 1) * 512],
                                    start=(j == 0), stop=(j == 7))
                                if j == 7:
                                    sPE.inc(mm)
                                    dmm_tok[dc] = sPE.n

                    for dc in dcs:
                        bank = dc if dgrp == 0 else dc - 6

                        @block.vector
                        def _(v, ts=ts, dc=dc, sub=sub, bank=bank,
                              tok=dmm_tok[dc]):
                            v.wait_ge(sPE.h, tok)
                            v.wait_ge(sDVE.h, sDVE.n)
                            v.scalar_tensor_tensor(
                                x_res[dc][:, ts * 1024 + sub * 512:
                                          ts * 1024 + (sub + 1) * 512],
                                P.bank(bank), 0.5,
                                x_res[dc][:, ts * 1024 + sub * 512:
                                          ts * 1024 + (sub + 1) * 512],
                                ALU.mult, ALU.add).then_inc(sDVE.h, 1)
                        sDVE.n += 1
                        P.reader(bank, sDVE)
            last_mm2[0] = sPE.n
            mm2_by_unit[ui] = sPE.n
            last_mm1[0] = w1_mm[("last", 0)]
            if ("last", 1) in w1_mm:
                last_mm1[0] = max(last_mm1[0], w1_mm[("last", 1)])

        @block.gpsimd
        def _(g):
            g.wait_ge(sDVE.h, sDVE.n)
            for c in range(DC):
                sST.dma(g.dma_start(out=xo_d[c * 128:(c + 1) * 128, :],
                                    in_=x_res[c][:, :]))
            g.wait_ge(sST.h, sST.n)



def build_l1attn():
    P = _mk_P()
    nc = P.nc
    aps = {
        "x": nc.dram_tensor("x", [D, TOWN], F32, kind="ExternalInput").ap(),
        "wq": nc.dram_tensor("wq", [D, 512], F32, kind="ExternalInput").ap(),
        "wk": nc.dram_tensor("wk", [D, 512], F32, kind="ExternalInput").ap(),
        "wv": nc.dram_tensor("wv", [D, 512], F32, kind="ExternalInput").ap(),
        "wo": nc.dram_tensor("wo", [D, D], F32, kind="ExternalInput").ap(),
        "bq": nc.dram_tensor("bq", [128, 4], F32, kind="ExternalInput").ap(),
        "bk": nc.dram_tensor("bk", [128, 4], F32, kind="ExternalInput").ap(),
        "bo": nc.dram_tensor("bo", [128, 8], F32, kind="ExternalInput").ap(),
        "bvr": nc.dram_tensor("bvr", [1, 512], F32, kind="ExternalInput").ap(),
        "nw": nc.dram_tensor("nw", [128, 8], F32, kind="ExternalInput").ap(),
        "mcol": nc.dram_tensor("mcol", [128, 1], F32, kind="ExternalInput").ap(),
        "maskB": nc.dram_tensor("maskB", [128, 128], BF16,
                                kind="ExternalInput").ap(),
        "xo": nc.dram_tensor("xo", [D, TOWN], F32, kind="ExternalOutput").ap(),
    }
    with P.es:
        block = P.es.enter_context(nc.Block())

        @block.gpsimd
        def _(g):
            P.init_consts(g)
        P.gate_consts(block)
        emit_l1attn(P, block, aps)
        sST = P.sem("st")

        @block.gpsimd
        def _(g):
            g.wait_ge(sST.h, sST.n)
    return nc


def emit_l1attn(P, block, aps, tag=""):
    nc = P.nc
    x_d, wq_d, wk_d, wv_d, wo_d = (aps["x"], aps["wq"], aps["wk"], aps["wv"],
                                   aps["wo"])
    bq_d, bk_d, bo_d, bvr_d = aps["bq"], aps["bk"], aps["bo"], aps["bvr"]
    nw_d, mcol_d, mB_d, xo_d = (aps["nw"], aps["mcol"], aps["maskB"],
                                aps["xo"])
    x1_dram = nc.dram_tensor("x1_scr" + tag, [D, TOWN], F32).ap()
    cc_xn_in = nc.dram_tensor("cc_xn_in", [D, TOWN], BF16).ap()
    cc_xn_out = nc.dram_tensor("cc_xn_out", [2 * D, TOWN], BF16).ap()
    cc_ao_in = nc.dram_tensor("cc_ao_in", [512, 2 * TOWN], BF16).ap()
    cc_ao_out = nc.dram_tensor("cc_ao_out", [D, 2 * TOWN], BF16).ap()

    SEQ = 2 * TOWN
    A = P.A
    lb = P.lb
    K_fm = A.alloc(SEQ, BF16, 4)
    V_tm = A.alloc(512, BF16, 32)
    bq_sb = A.alloc(4, F32)
    bk_sb = A.alloc(4, F32)
    bo_sb = A.alloc(8, F32)
    nw_sb = A.alloc(8, F32)
    mcol = A.alloc(1, F32)
    omcol = A.alloc(1, F32)
    bvB = A.alloc(512, BF16)
    mB_sb = A.alloc(128, BF16)
    ident = P.ident_bf
    mrow = A.alloc(1, F32, 10)
    ssum = A.alloc(1, F32, 9 * 2)
    negm = A.alloc(1, F32, 8)
    nrm2 = A.alloc(1, F32, 16)
    aotile = A.alloc(128, BF16, 2)
    markP = A.mark()

    Q_fm = A.alloc(SEQ, BF16, 4)
    markR1 = A.mark()
    A.reset(markP)
    _skip = A.alloc(SEQ, BF16, 4)
    markR2 = A.mark()
    wqkv_bf = A.alloc(4096, BF16, 3)
    w_st0 = A.alloc(D, F32)
    markR2_end = A.mark()
    A.reset(markR2)
    Pbuf = A.alloc(4096, BF16, 2)
    PT = A.alloc(4096, BF16, 2)
    sqn = A.alloc(512, F32, 1)
    markR2b = A.mark()
    A.reset(markR2)
    wo_bf = A.alloc(8 * D, BF16)
    wo_st = A.alloc(D, F32)
    gA = A.alloc(512, BF16, DC)
    gB = A.alloc(512, BF16, DC)
    markR2c = A.mark()
    A.off = max(markR2_end, markR2b, markR2c)
    x_sp = A.alloc(512, F32, DC)
    xn_sp = A.alloc(512, BF16, DC)
    assert A.off <= A.hi, A.off
    bvr_sb_view = w_st0[0:1, 0:512]

    sPE, sACT, sDVE = P.sem("pe"), P.sem("act"), P.sem("dve")
    sLX, sST = P.sem("lx"), P.sem("st")
    sLW = [P.sem("lw0"), P.sem("lw1")]
    sSM = P.sem("sm")
    sCC = P.sem("cc")

    if True:
        sGP = P.sem("gp")
        tok_ident = P._gate_val

        @block.sync
        def _(s):
            s.wait_ge(sDVE.h, sDVE.n)
            s.wait_ge(sPE.h, sPE.n)
            for dst, srcd in ((bq_sb[:, 0:4], bq_d), (bk_sb[:, 0:4], bk_d),
                              (bo_sb, bo_d), (nw_sb, nw_d), (mcol, mcol_d)):
                sSM.dma(s.dma_start(out=dst[:, :], in_=srcd[:, :]))
            sSM.dma(s.dma_start(out=bvr_sb_view[:, :], in_=bvr_d[:, :]))
            sSM.dma(s.dma_start(out=mB_sb[:, :], in_=mB_d[:, :]))
        tok_small = sSM.n

        @block.vector
        def _(v):
            v.wait_ge(sSM.h, tok_small)
            v.wait_ge(sDVE.h, sDVE.n)
            v.tensor_scalar(omcol[:, :], mcol[:, :], -1.0, 1.0, ALU.mult,
                            ALU.add).then_inc(sDVE.h, 1)
        sDVE.n += 1
        tok_om = sDVE.n

        @block.tensor
        def _(t):
            t.wait_ge(sSM.h, tok_small)
            P.acquire(t, 0)
            sPE.inc(t.matmul(P.bank(0), P.ones_row_f32, bvr_sb_view[:, :],
                             start=True, stop=True))
        tok_bvmm = sPE.n

        @block.scalar
        def _(a):
            a.wait_ge(sPE.h, tok_bvmm)
            a.wait_ge(sACT.h, sACT.n)
            sACT.inc(a.activation(bvB[:, :], P.bank(0), AF.Copy))
        P.reader(0, sACT)
        tok_bvB = sACT.n

        nw_cols = [nw_sb[:, c:c + 1] for c in range(DC)]
        for sp in range(4):
            @block.gpsimd
            def _(g, sp=sp):
                if sp > 0:
                    g.wait_ge(sDVE.h, sDVE.n)
                    g.wait_ge(sST.h, sST.n)
                for c in range(DC):
                    sLX.dma(g.dma_start(out=x_sp[c][:, :],
                                        in_=x_d[c * 128:(c + 1) * 128,
                                                sp * 512:(sp + 1) * 512]))
            tok_x = [(sLX, sLX.n)] * DC

            tok_xn = emit_ln(P, block, x_sp, xn_sp, 512, lb, tok_x=tok_x,
                             mode="rms2", nw_cols=nw_cols)

            @block.gpsimd
            def _(g, sp=sp, tok=tok_xn):
                g.wait_ge(tok[0].h, tok[1])
                for c in range(DC):
                    sST.dma(g.dma_start(
                        out=cc_xn_in[c * 128:(c + 1) * 128,
                                     sp * 512:(sp + 1) * 512],
                        in_=xn_sp[c][:, :]))
                    sST.dma(g.dma_start(
                        out=x1_dram[c * 128:(c + 1) * 128,
                                    sp * 512:(sp + 1) * 512],
                        in_=x_sp[c][:, :]))
        tok_phaseA = sST.n

        @block.gpsimd
        def _(g):
            g.wait_ge(sST.h, tok_phaseA)
            g.collective_compute(
                "AllGather", ALU.bypass,
                replica_groups=[[0, 1], [2, 3], [4, 5], [6, 7]],
                ins=[cc_xn_in[:, :].opt()],
                outs=[cc_xn_out[:, :].opt()]).then_inc(sCC.h, 1)
        sCC.n += 1
        tok_cc1 = sCC.n

        tok_wq = load_weights_generic(P, block, wqkv_bf[0], w_st0, wq_d, 512,
                                      sLW, gate=(sACT, tok_bvB))
        tok_wk = load_weights_generic(P, block, wqkv_bf[1], w_st0, wk_d, 512,
                                      sLW, gate=None)
        tok_wv = load_weights_generic(P, block, wqkv_bf[2], w_st0, wv_d, 512,
                                      sLW, gate=None)
        qkv_mm_last = [0]
        for sp in range(8):
            @block.gpsimd
            def _(g, sp=sp):
                g.wait_ge(sCC.h, tok_cc1)
                if sp > 0:
                    g.wait_ge(sPE.h, qkv_mm_last[0])
                    g.wait_ge(sST.h, sST.n)
                for c in range(DC):
                    row0 = c * 128 if sp < 4 else D + c * 128
                    col0 = (sp % 4) * 512
                    sLX.dma(g.dma_start(
                        out=xn_sp[c][:, :].bitcast(BF16)[:, 0:512],
                        in_=cc_xn_out[row0:row0 + 128, col0:col0 + 512]))
            tok_xnl = sLX.n

            for which, wslot, bias in ((0, 0, bq_sb), (1, 1, bk_sb)):
                for co in range(4):
                    bank = (2 * co + which) % 4

                    @block.tensor
                    def _(t, sp=sp, which=which, wslot=wslot, co=co, bank=bank,
                          tok_xnl=tok_xnl):
                        if co == 0 and which == 0:
                            t.wait_ge(sLX.h, tok_xnl)
                            if sp == 0:
                                t.wait_ge(sDVE.h, tok_wv)
                        P.acquire(t, bank)
                        for cin in range(DC):
                            mm = t.matmul(
                                P.bank(bank),
                                wqkv_bf[wslot][:, cin * 512 + co * 128:
                                               cin * 512 + co * 128 + 128],
                                xn_sp[cin][:, :].bitcast(BF16)[:, 0:512],
                                start=(cin == 0), stop=(cin == DC - 1))
                        sPE.inc(mm)
                    mtok = sPE.n

                    @block.vector
                    def _(v, sp=sp, which=which, co=co, bank=bank, mtok=mtok,
                          bias=bias):
                        v.wait_ge(sPE.h, mtok)
                        v.wait_ge(sDVE.h, sDVE.n)
                        dst = (Q_fm if which == 0 else K_fm)[co]
                        v.tensor_scalar(dst[:, sp * 512:(sp + 1) * 512],
                                        P.bank(bank), bias[:, co:co + 1],
                                        None, ALU.add).then_inc(sDVE.h, 1)
                    sDVE.n += 1
                    P.reader(bank, sDVE)

            for tc in range(4):
                bank = 4 + tc % 2

                @block.tensor
                def _(t, sp=sp, tc=tc, bank=bank):
                    P.acquire(t, bank)
                    for cin in range(DC):
                        mm = t.matmul(
                            P.bank(bank),
                            xn_sp[cin][:, :].bitcast(BF16)[:, tc * 128:
                                                           (tc + 1) * 128],
                            wqkv_bf[2][:, cin * 512:(cin + 1) * 512],
                            start=(cin == 0), stop=(cin == DC - 1))
                    sPE.inc(mm)
                vtok = sPE.n

                @block.vector
                def _(v, sp=sp, tc=tc, bank=bank, vtok=vtok):
                    v.wait_ge(sPE.h, vtok)
                    v.wait_ge(sACT.h, tok_bvB)
                    v.wait_ge(sDVE.h, sDVE.n)
                    v.tensor_tensor(V_tm[sp * 4 + tc][:, :], P.bank(bank),
                                    bvB[:, :], ALU.add).then_inc(sDVE.h, 1)
                sDVE.n += 1
                P.reader(bank, sDVE)
            qkv_mm_last[0] = sPE.n
        tok_qkv = (sDVE, sDVE.n)

        sqr = P.lb.sq
        sGPq = P.sem("gp")

        nrm_dve = [0]
        sq_toks = []
        mm_toks2 = []
        for which, F in ((0, Q_fm), (1, K_fm)):
            for c0 in range(4):
                for sp8 in range(8):
                    idx = which * 32 + c0 * 8 + sp8
                    par = idx % 2

                    @block.vector
                    def _(v, which=which, c0=c0, sp8=sp8, par=par):
                        if not sq_toks:
                            s, vv = tok_qkv
                            v.wait_ge(s.h, vv)
                        if len(mm_toks2) >= 2:
                            v.wait_ge(sPE.h, mm_toks2[-2])
                        v.wait_ge(sDVE.h, sDVE.n)
                        v.tensor_tensor(
                            sqr[par][:, :], F[c0][:, sp8 * 512:(sp8 + 1) * 512],
                            F[c0][:, sp8 * 512:(sp8 + 1) * 512],
                            ALU.mult).then_inc(sDVE.h, 1)
                    sDVE.n += 1
                    sq_toks.append(sDVE.n)
                    tok_sq = sDVE.n

                    for pbi in (0, 64):
                        hloc = c0 * 2 + (pbi // 64)
                        bank = 2 + (idx % 2)

                        @block.tensor
                        def _(t, par=par, pbi=pbi, bank=bank, tok_sq=tok_sq):
                            t.wait_ge(sDVE.h, tok_sq)
                            P.acquire(t, bank)
                            t.matmul(P.bank(bank)[0:1, :],
                                     P.ones_col[pbi:pbi + 64, :],
                                     sqr[par][pbi:pbi + 64, :],
                                     start=True, stop=True).then_inc(sPE.h, 1)
                        sPE.n += 1
                        tok_mm = sPE.n

                        @block.vector
                        def _(v, which=which, hloc=hloc, bank=bank,
                              tok_mm=tok_mm, first=(sp8 == 0)):
                            v.wait_ge(sPE.h, tok_mm)
                            v.wait_ge(sDVE.h, sDVE.n)
                            sDVE.inc(v.reduce_max(mrow[8][0:1, :],
                                                  P.bank(bank)[0:1, :], axis=AX))
                            v.wait_ge(sDVE.h, sDVE.n)
                            dst = nrm2[which * 8 + hloc][0:1, :]
                            if first:
                                sDVE.inc(v.tensor_copy(dst, mrow[8][0:1, :]))
                            else:
                                sDVE.inc(v.tensor_tensor(dst, dst,
                                                         mrow[8][0:1, :],
                                                         ALU.max))
                        P.reader(bank, sDVE)
                    mm_toks2.append(sPE.n)
        tok_nrm = sDVE.n

        negm_toks = []
        for h in range(8):
            @block.vector
            def _(v, h=h):
                if h == 0:
                    v.wait_ge(sDVE.h, tok_nrm)
                v.wait_ge(sDVE.h, sDVE.n)
                sDVE.inc(v.tensor_tensor(mrow[8][0:1, :], nrm2[h][0:1, :],
                                         nrm2[8 + h][0:1, :], ALU.mult))
            tok_t = sDVE.n

            @block.scalar
            def _(a, h=h, tok_t=tok_t):
                a.wait_ge(sDVE.h, tok_t)
                a.wait_ge(sACT.h, sACT.n)
                sACT.inc(a.activation(mrow[9][0:1, :], mrow[8][0:1, :], AF.Ln))
                a.wait_ge(sACT.h, sACT.n)
                sACT.inc(a.activation(mrow[9][0:1, :], mrow[9][0:1, :], AF.Exp,
                                      scale=0.5, bias=LN_EIGHTH))
            tok_mh = sACT.n

            @block.vector
            def _(v, h=h, tok_mh=tok_mh):
                v.wait_ge(sACT.h, tok_mh)
                v.wait_ge(sDVE.h, sDVE.n)
                sDVE.inc(v.tensor_scalar(mrow[8][0:1, :], mrow[9][0:1, :],
                                         -1.0, None, ALU.mult))
            tok_ng = sDVE.n

            @block.tensor
            def _(t, h=h, tok_ng=tok_ng):
                t.wait_ge(sDVE.h, tok_ng)
                P.acquire(t, 2)
                t.matmul(P.bank(2)[:, 0:1], P.ones_row_f32,
                         mrow[8][0:1, 0:1], start=True,
                         stop=True).then_inc(sPE.h, 1)
            sPE.n += 1
            tok_bc2 = sPE.n

            @block.vector
            def _(v, h=h, tok_bc2=tok_bc2):
                v.wait_ge(sPE.h, tok_bc2)
                v.wait_ge(sDVE.h, sDVE.n)
                sDVE.inc(v.tensor_copy(negm[h][:, :], P.bank(2)[:, 0:1]))
            P.reader(2, sDVE)
            negm_toks.append(sDVE.n)
        tok_negm = sDVE.n

        tp_tok = {0: 0, 1: 0}
        pv_tok = {0: 0, 1: 0}
        ao_st = {0: 0, 1: 0}
        sSTa = [P.sem("sta0"), P.sem("sta1")]
        first_iter = [True]
        act_pred = [sACT.n]

        def stage1(h, qt, p):
            c0, pb = h // 2, 64 * (h % 2)
            qo = qt * 128
            nfull, rem = qo // 512, qo % 512
            spans = [(k * 512, 512) for k in range(nfull)]
            if rem:
                spans.append((nfull * 512, rem))
            spans.append((qo, 128))
            nspan = len(spans)
            sc_toks = []
            for si, (k0, w) in enumerate(spans):
                bank = 2 * p + (si % 2)

                @block.tensor
                def _(t, si=si, k0=k0, w=w, bank=bank, c0=c0, pb=pb, qo=qo):
                    if first_iter[0] and si == 0:
                        t.wait_ge(sDVE.h, tok_negm)
                        first_iter[0] = False
                    P.acquire(t, bank)
                    if si >= 2:
                        t.wait_ge(sACT.h, act_pred[0] + si - 1)
                    t.matmul(P.bank(bank, w=w),
                             Q_fm[c0][pb:pb + 64, qo:qo + 128],
                             K_fm[c0][pb:pb + 64, k0:k0 + w],
                             start=True, stop=True).then_inc(sPE.h, 1)
                sPE.n += 1
                sc_toks.append(sPE.n)

            @block.scalar
            def _(a, h=h, p=p, spans=tuple(spans), sc_toks=tuple(sc_toks)):
                a.wait_ge(sPE.h, tp_tok[p])
                for si, (k0, w) in enumerate(spans):
                    bank = 2 * p + (si % 2)
                    a.wait_ge(sPE.h, sc_toks[si])
                    a.wait_ge(sACT.h, sACT.n)
                    kw = {}
                    if si < len(spans) - 1:
                        kw["accum_out"] = ssum[p * 9 + si][:, :]
                    sACT.inc(a.activation(Pbuf[p][:, k0:k0 + w],
                                          P.bank(bank, w=w), AF.Exp,
                                          scale=0.125, bias=negm[h][:, :],
                                          **kw))
            tok_exp = sACT.n
            assert tok_exp == act_pred[0] + nspan, (tok_exp, act_pred[0], nspan)
            act_pred[0] = tok_exp
            for si in range(nspan):
                P.reader(2 * p + (si % 2), sACT)

            @block.vector
            def _(v, p=p, qo=qo, nspan=nspan, tok_exp=tok_exp):
                v.wait_ge(sACT.h, tok_exp)
                v.wait_ge(sDVE.h, sDVE.n)
                sDVE.inc(v.scalar_tensor_tensor(
                    Pbuf[p][:, qo:qo + 128], Pbuf[p][:, qo:qo + 128], 1.0,
                    mB_sb[:, :], ALU.mult, ALU.mult,
                    accum_out=ssum[p * 9 + nspan - 1][:, :]))
                for si in range(1, nspan):
                    v.wait_ge(sDVE.h, sDVE.n)
                    sDVE.inc(v.tensor_tensor(ssum[p * 9][:, :], ssum[p * 9][:, :],
                                             ssum[p * 9 + si][:, :], ALU.add))
                v.wait_ge(sDVE.h, sDVE.n)
                sDVE.inc(v.reciprocal(ssum[p * 9][:, :], ssum[p * 9][:, :]))
                v.wait_ge(sDVE.h, sDVE.n)
                sDVE.inc(v.tensor_scalar(Pbuf[p][:, 0:qo + 128],
                                         Pbuf[p][:, 0:qo + 128],
                                         ssum[p * 9][:, :], None, ALU.mult))
            return sDVE.n

        def stage2a(h, qt, p, tok_norm):
            nkc = qt + 1
            for grp in range(0, nkc, 8):
                gn = min(8, nkc - grp)

                @block.tensor
                def _(t, grp=grp, gn=gn, p=p, tok_norm=tok_norm):
                    if grp == 0:
                        t.wait_ge(sDVE.h, tok_norm)
                    P.acquire(t, 4 + p)
                    for k in range(gn):
                        mm = t.transpose(
                            P.bank_bf(4 + p)[:, k * 128:(k + 1) * 128],
                            Pbuf[p][:, (grp + k) * 128:(grp + k + 1) * 128],
                            ident[:, :])
                    sPE.inc(mm)
                ttok = sPE.n

                @block.vector
                def _(v, grp=grp, gn=gn, p=p, ttok=ttok):
                    v.wait_ge(sPE.h, ttok)
                    v.wait_ge(sDVE.h, sDVE.n)
                    sDVE.inc(v.tensor_copy(PT[p][:, grp * 128:(grp + gn) * 128],
                                           P.bank_bf(4 + p)[:, 0:gn * 128]))
                P.reader(4 + p, sDVE)
            tp_tok[p] = sPE.n
            return sDVE.n

        def stage2b(h, qt, p, tok_pt):
            nkc = qt + 1

            @block.tensor
            def _(t, h=h, p=p, nkc=nkc, tok_pt=tok_pt):
                t.wait_ge(sDVE.h, tok_pt)
                P.acquire(t, 6 + p)
                for kc in range(nkc):
                    mm = t.matmul(P.bank(6 + p, w=128)[0:64, :],
                                  V_tm[kc][:, h * 64:(h + 1) * 64],
                                  PT[p][:, kc * 128:(kc + 1) * 128],
                                  start=(kc == 0), stop=(kc == nkc - 1))
                sPE.inc(mm)
            pv_tok[p] = sPE.n
            tok_pv = sPE.n

            @block.vector
            def _(v, p=p, tok_pv=tok_pv):
                v.wait_ge(sPE.h, tok_pv)
                v.wait_ge(sSTa[p].h, ao_st[p])
                v.wait_ge(sDVE.h, sDVE.n)
                sDVE.inc(v.tensor_copy(aotile[p][0:64, :],
                                       P.bank(6 + p, w=128)[0:64, :]))
            P.reader(6 + p, sDVE)
            tok_ao = sDVE.n

            @block.gpsimd
            def _(g, h=h, qt=qt, p=p, tok_ao=tok_ao):
                g.wait_ge(sDVE.h, tok_ao)
                sSTa[p].dma(g.dma_start(
                    out=cc_ao_in[h * 64:(h + 1) * 64,
                                 qt * 128:(qt + 1) * 128],
                    in_=aotile[p][0:64, :]))
            ao_st[p] = sSTa[p].n

        pend1 = None
        pend2 = None
        for h in range(8):
            for qt in range(32):
                p = (h * 32 + qt) % 2
                tn = stage1(h, qt, p)
                np1 = (h, qt, p, tn)
                np2 = None
                if pend1 is not None:
                    tok_pt = stage2a(*pend1)
                    np2 = (pend1[0], pend1[1], pend1[2], tok_pt)
                if pend2 is not None:
                    stage2b(*pend2)
                pend1, pend2 = np1, np2
        tok_pt = stage2a(*pend1)
        stage2b(*pend2)
        stage2b(pend1[0], pend1[1], pend1[2], tok_pt)
        tok_attn_st = sST.n

        @block.gpsimd
        def _(g):
            g.wait_ge(sST.h, tok_attn_st)
            g.wait_ge(sSTa[0].h, sSTa[0].n)
            g.wait_ge(sSTa[1].h, sSTa[1].n)
            g.collective_compute(
                "AllGather", ALU.bypass,
                replica_groups=[[0, 1], [2, 3], [4, 5], [6, 7]],
                ins=[cc_ao_in[:, :].opt()],
                outs=[cc_ao_out[:, :].opt()]).then_inc(sCC.h, 1)
        sCC.n += 1
        tok_cc2 = sCC.n

        tok_wo = load_weights_generic(P, block, wo_bf, wo_st, wo_d, D,
                                      sLW, gate=(sPE, max(pv_tok[0], pv_tok[1])))
        for osp in range(4):
            @block.gpsimd
            def _(g, osp=osp):
                g.wait_ge(sCC.h, tok_cc2)
                if osp > 0:
                    g.wait_ge(sPE.h, sPE.n)
                    g.wait_ge(sST.h, sST.n)
                    g.wait_ge(sDVE.h, sDVE.n)
                for c in range(DC):
                    sLX.dma(g.dma_start(
                        out=gA[c][:, :],
                        in_=cc_ao_out[c * 128:(c + 1) * 128,
                                      osp * 512:(osp + 1) * 512]))
                    sLX.dma(g.dma_start(
                        out=gB[c][:, :],
                        in_=cc_ao_out[c * 128:(c + 1) * 128,
                                      TOWN + osp * 512:TOWN + (osp + 1) * 512]))
                    sLX.dma(g.dma_start(
                        out=x_sp[c][:, :],
                        in_=x1_dram[c * 128:(c + 1) * 128,
                                    osp * 512:(osp + 1) * 512]))
            tok_g = sLX.n

            for dc in range(DC):
                mm_toks = []
                for half, gsb in ((0, gA), (1, gB)):
                    bank = 2 * (dc % 2) + half

                    @block.tensor
                    def _(t, dc=dc, half=half, gsb=gsb, bank=bank, osp=osp,
                          tok_g=tok_g):
                        if dc == 0 and half == 0:
                            t.wait_ge(sLX.h, tok_g)
                            if osp == 0:
                                t.wait_ge(sDVE.h, tok_wo)
                        P.acquire(t, bank)
                        for cin in range(DC):
                            mm = t.matmul(
                                P.bank(bank),
                                wo_bf[:, cin * 1024 + dc * 128:
                                      cin * 1024 + dc * 128 + 128],
                                gsb[cin][:, :],
                                start=(cin == 0), stop=(cin == DC - 1))
                        sPE.inc(mm)
                    mm_toks.append(sPE.n)

                @block.vector
                def _(v, dc=dc, osp=osp, mm_toks=tuple(mm_toks)):
                    bA = 2 * (dc % 2)
                    v.wait_ge(sPE.h, mm_toks[1])
                    v.wait_ge(sDVE.h, max(sDVE.n, tok_om))
                    sDVE.inc(v.scalar_tensor_tensor(
                        x_sp[dc][:, :], P.bank(bA), mcol[:, :],
                        x_sp[dc][:, :], ALU.mult, ALU.add))
                    v.wait_ge(sDVE.h, sDVE.n)
                    sDVE.inc(v.scalar_tensor_tensor(
                        x_sp[dc][:, :], P.bank(bA + 1), omcol[:, :],
                        x_sp[dc][:, :], ALU.mult, ALU.add))
                    v.wait_ge(sDVE.h, sDVE.n)
                    sDVE.inc(v.tensor_scalar(x_sp[dc][:, :], x_sp[dc][:, :],
                                             bo_sb[:, dc:dc + 1], None,
                                             ALU.add))
                tok_res = sDVE.n
                P.reader(2 * (dc % 2), sDVE)
                P.reader(2 * (dc % 2) + 1, sDVE)

                @block.gpsimd
                def _(g, dc=dc, osp=osp, tok=tok_res):
                    g.wait_ge(sDVE.h, tok)
                    sST.dma(g.dma_start(
                        out=xo_d[dc * 128:(dc + 1) * 128,
                                 osp * 512:(osp + 1) * 512],
                        in_=x_sp[dc][:, :]))


def load_weights_generic(P, block, dst_bf, st, wd_ap, ncols, sLW, gate=None):
    nc = P.nc
    sDVE = P.sem("dve")
    toks = []
    for cin in range(DC):
        @block.sync
        def _(s, cin=cin):
            if gate is not None:
                s.wait_ge(gate[0].h, gate[1])
            prev = toks[-1] if toks else getattr(P, "_wst_last", 0)
            if prev:
                s.wait_ge(sDVE.h, prev)
            sLW[cin % 2].dma(s.dma_start(
                out=st[:, 0:ncols],
                in_=wd_ap[cin * 128:(cin + 1) * 128, 0:ncols]))
        tok_l = sLW[cin % 2].n

        @block.vector
        def _(v, cin=cin, tok_l=tok_l):
            v.wait_ge(sLW[cin % 2].h, tok_l)
            v.wait_ge(sDVE.h, sDVE.n)
            v.tensor_copy(dst_bf[:, cin * ncols:(cin + 1) * ncols],
                          st[:, 0:ncols]).then_inc(sDVE.h, 1)
        sDVE.n += 1
        toks.append(sDVE.n)
    P._wst_last = sDVE.n
    return sDVE.n



def _shard_x(x):
    out = []
    for c in range(NCORES):
        b, h = c // 2, c % 2
        own = x[b, h * TOWN:(h + 1) * TOWN, :]
        if h == 0:
            halo = np.zeros((THALO, D), np.float32)
        else:
            halo = x[b, TOWN - THALO:TOWN, :]
        xc = np.concatenate([halo, own], axis=0)
        out.append(np.ascontiguousarray(xc.T))
    return out


def _bcol(b):
    return np.ascontiguousarray(b.reshape(8, 128).T.astype(np.float32))


def _p2_masks(core):
    import ml_dtypes
    hv = 1.0 if core % 2 == 1 else 0.0
    qi_ = np.arange(128)[:, None]
    j_ = np.arange(512)[None, :]
    gen = (j_ >= qi_ + 1).astype(np.float32)
    tiles = []
    for qt in range(4):
        kabs = qt * 128 - 512 + j_
        tiles.append(gen * np.where(kabs < 0, hv, 1.0))
    tiles.append(gen)
    maskA = np.concatenate(tiles, axis=0).astype(ml_dtypes.bfloat16)
    maskB = (np.arange(128)[None, :] <= qi_).astype(np.float32).astype(
        ml_dtypes.bfloat16)
    return maskA, maskB


_PROG_CACHE = {}


def _prog(key, builder, *args):
    if key not in _PROG_CACHE:
        _PROG_CACHE[key] = builder(*args)
    return _PROG_CACHE[key]


def run_l0attn_unfused(x_cores, Wq, bq, Wk, bk, Wv, bv, Wo, bo):
    nc = _prog("p2", build_l0attn)
    wqT = np.ascontiguousarray(Wq.T.astype(np.float32))
    wkT = np.ascontiguousarray(Wk.T.astype(np.float32))
    wvT = np.ascontiguousarray(Wv.T.astype(np.float32))
    woT = np.ascontiguousarray(Wo.T.astype(np.float32))
    in_maps = []
    for c, xc in enumerate(x_cores):
        mA, mB = _p2_masks(c)
        in_maps.append({"x": np.ascontiguousarray(xc, np.float32),
                        "wq": wqT, "wk": wkT, "wv": wvT, "wo": woT,
                        "bq": _bcol(bq), "bk": _bcol(bk), "bo": _bcol(bo),
                        "bvr": np.ascontiguousarray(bv[None, :], np.float32),
                        "maskA": mA, "maskB": mB})
    res = run_bass_kernel_spmd(nc, in_maps, core_ids=list(range(len(x_cores))))
    return [r["xo"] for r in res.results]


_P1_CACHE = {}


def run_l0ffn_unfused(x_cores, wg, wu, wd, T):
    key = ("p1", T)
    if key not in _P1_CACHE:
        _P1_CACHE[key] = build_l0ffn(T)
    nc = _P1_CACHE[key]
    import ml_dtypes
    bf = ml_dtypes.bfloat16
    wgT = np.ascontiguousarray(wg.T.astype(bf))
    wuT = np.ascontiguousarray(wu.T.astype(bf))
    wdT = np.ascontiguousarray(wd.T.astype(bf))
    in_maps = [{"x": np.ascontiguousarray(xc), "wg": wgT, "wu": wuT, "wd": wdT}
               for xc in x_cores]
    res = run_bass_kernel_spmd(nc, in_maps, core_ids=list(range(len(x_cores))))
    return [r["xo"] for r in res.results]


def run_moe_unfused(x_cores, W1, W2, Wg, bg):
    import ml_dtypes
    nc = _prog("p4", build_moe)
    wgT8 = np.ascontiguousarray(
        Wg.astype(np.float32).reshape(8, 128, 8).transpose(1, 0, 2).reshape(128, 64))
    sel8 = np.zeros((8, 1024), np.float32)
    for e in range(NE):
        sel8[e, e * 128:(e + 1) * 128] = 1.0
    sel8 = sel8.astype(ml_dtypes.bfloat16)
    W1f = np.ascontiguousarray(W1, np.float32)
    W2f = np.ascontiguousarray(W2, np.float32)
    bgc = np.ascontiguousarray(bg.astype(np.float32)[:, None])
    in_maps = [{"x": np.ascontiguousarray(xc, np.float32), "w1": W1f, "w2": W2f,
                "wg": wgT8, "bg": bgc, "sel8": sel8} for xc in x_cores]
    res = run_bass_kernel_spmd(nc, in_maps, core_ids=list(range(len(x_cores))))
    return [r["xo"] for r in res.results]


def run_l1attn_unfused(x_cores, nw, Wq, bq, Wk, bk, Wv, bv, Wo, bo):
    import ml_dtypes
    nc = _prog("p5", build_l1attn)
    qi_ = np.arange(128)[:, None]
    maskB = (np.arange(128)[None, :] <= qi_).astype(np.float32).astype(
        ml_dtypes.bfloat16)
    woT = np.ascontiguousarray(Wo.T.astype(np.float32))
    nwc = np.ascontiguousarray(nw.astype(np.float32).reshape(8, 128).T)
    boc = _bcol(bo)
    in_maps = []
    for c, xc in enumerate(x_cores):
        hsl = slice((c % 2) * 512, (c % 2) * 512 + 512)
        in_maps.append({
            "x": np.ascontiguousarray(xc, np.float32),
            "wq": np.ascontiguousarray(Wq.T[:, hsl].astype(np.float32)),
            "wk": np.ascontiguousarray(Wk.T[:, hsl].astype(np.float32)),
            "wv": np.ascontiguousarray(Wv.T[:, hsl].astype(np.float32)),
            "wo": woT,
            "bq": np.ascontiguousarray(
                bq[hsl].astype(np.float32).reshape(4, 128).T),
            "bk": np.ascontiguousarray(
                bk[hsl].astype(np.float32).reshape(4, 128).T),
            "bo": boc,
            "bvr": np.ascontiguousarray(bv[None, hsl].astype(np.float32)),
            "nw": nwc,
            "mcol": np.full((128, 1), 1.0 if c % 2 == 0 else 0.0, np.float32),
            "maskB": maskB})
    res = run_bass_kernel_spmd(nc, in_maps, core_ids=list(range(len(x_cores))))
    return [r["xo"] for r in res.results]


def build_fused():
    P = _mk_P()
    nc = P.nc

    def din(name, shape, dt):
        return nc.dram_tensor(name, shape, dt, kind="ExternalInput").ap()

    x0_d = din("x", [D, TFULL], F32)
    wg_d = din("wg", [D, H0], BF16)
    wu_d = din("wu", [D, H0], BF16)
    wd_d = din("wd", [H0, D], BF16)
    wq0 = din("wq0", [D, D], F32)
    wk0 = din("wk0", [D, D], F32)
    wv0 = din("wv0", [D, D], F32)
    wo0 = din("wo0", [D, D], F32)
    bq0 = din("bq0", [128, 8], F32)
    bk0 = din("bk0", [128, 8], F32)
    bo0 = din("bo0", [128, 8], F32)
    bvr0 = din("bvr0", [1, D], F32)
    mA_d = din("maskA", [640, 512], BF16)
    mB_d = din("maskB", [128, 128], BF16)
    w1_d = din("w1", [NE, D, H1], F32)
    w2_d = din("w2", [NE, H1, D], F32)
    wgg_d = din("wgg", [128, 64], F32)
    bg_d = din("bg", [8, 1], F32)
    sel8_d = din("sel8", [8, 1024], BF16)
    wq1 = din("wq1", [D, 512], F32)
    wk1 = din("wk1", [D, 512], F32)
    wv1 = din("wv1", [D, 512], F32)
    wo1 = din("wo1", [D, D], F32)
    bq1 = din("bq1", [128, 4], F32)
    bk1 = din("bk1", [128, 4], F32)
    bo1 = din("bo1", [128, 8], F32)
    bvr1 = din("bvr1", [1, 512], F32)
    nw_d = din("nw", [128, 8], F32)
    mcol_d = din("mcol", [128, 1], F32)
    out_d = nc.dram_tensor("out", [D, TOWN], F32, kind="ExternalOutput").ap()
    x1_d = nc.dram_tensor("x1i", [D, TFULL], F32).ap()
    x2_d = nc.dram_tensor("x2i", [D, TOWN], F32).ap()
    x3_d = nc.dram_tensor("x3i", [D, TOWN], F32).ap()
    x4_d = nc.dram_tensor("x4i", [D, TOWN], F32).ap()
    x5_d = nc.dram_tensor("x5i", [D, TOWN], F32).ap()

    with P.es:
        block = P.es.enter_context(nc.Block())

        @block.gpsimd
        def _(g):
            P.init_consts(g)
        P.gate_consts(block)
        P._moe_wgr = None
        mark = P.A.mark()

        emit_l0ffn(P, block, {"x": x0_d, "wg": wg_d, "wu": wu_d, "wd": wd_d,
                              "xo": x1_d}, TFULL)
        phase_barrier(P, block)
        P.A.reset(mark)

        emit_l0attn(P, block, {"x": x1_d, "wq": wq0, "wk": wk0, "wv": wv0,
                               "wo": wo0, "bq": bq0, "bk": bk0, "bo": bo0,
                               "bvr": bvr0, "maskA": mA_d, "maskB": mB_d,
                               "xo": x2_d})
        phase_barrier(P, block)
        P.A.reset(mark)

        emit_l0ffn(P, block, {"x": x2_d, "wg": wg_d, "wu": wu_d, "wd": wd_d,
                              "xo": x3_d}, TOWN)
        phase_barrier(P, block)
        P.A.reset(mark)

        emit_moe(P, block, {"x": x3_d, "w1": w1_d, "w2": w2_d, "wg": wgg_d,
                            "bg": bg_d, "sel8": sel8_d, "xo": x4_d}, tag="a")
        phase_barrier(P, block)
        P.A.reset(mark)

        emit_l1attn(P, block, {"x": x4_d, "wq": wq1, "wk": wk1, "wv": wv1,
                               "wo": wo1, "bq": bq1, "bk": bk1, "bo": bo1,
                               "bvr": bvr1, "nw": nw_d, "mcol": mcol_d,
                               "maskB": mB_d, "xo": x5_d})
        phase_barrier(P, block)
        P.A.reset(mark)

        emit_moe(P, block, {"x": x5_d, "w1": w1_d, "w2": w2_d, "wg": wgg_d,
                            "bg": bg_d, "sel8": sel8_d, "xo": out_d}, tag="b")

        sST = P.sem("st")

        @block.gpsimd
        def _(g):
            g.wait_ge(sST.h, sST.n)
    return nc


def kernel(L0_attn_x, L0_attn_Wq, L0_attn_bq, L0_attn_Wk, L0_attn_bk,
           L0_attn_Wv, L0_attn_bv, L0_attn_Wo, L0_attn_bo,
           L0_ffn_w_gate, L0_ffn_w_up, L0_ffn_w_down,
           L1_attn_norm_w, L1_attn_Wq, L1_attn_bq, L1_attn_Wk, L1_attn_bk,
           L1_attn_Wv, L1_attn_bv, L1_attn_Wo, L1_attn_bo,
           L1_ffn_W1, L1_ffn_W2, L1_ffn_Wg, L1_ffn_bg):
    import ml_dtypes
    x = np.asarray(L0_attn_x, np.float32)
    x_cores = _shard_x(x)
    nc = _prog("fused", build_fused)

    import ml_dtypes
    f32 = np.float32
    bf = ml_dtypes.bfloat16
    shared = {
        "wg": np.ascontiguousarray(L0_ffn_w_gate.T.astype(bf)),
        "wu": np.ascontiguousarray(L0_ffn_w_up.T.astype(bf)),
        "wd": np.ascontiguousarray(L0_ffn_w_down.T.astype(bf)),
        "wq0": np.ascontiguousarray(L0_attn_Wq.T.astype(f32)),
        "wk0": np.ascontiguousarray(L0_attn_Wk.T.astype(f32)),
        "wv0": np.ascontiguousarray(L0_attn_Wv.T.astype(f32)),
        "wo0": np.ascontiguousarray(L0_attn_Wo.T.astype(f32)),
        "bq0": _bcol(L0_attn_bq), "bk0": _bcol(L0_attn_bk),
        "bo0": _bcol(L0_attn_bo),
        "bvr0": np.ascontiguousarray(L0_attn_bv[None, :].astype(f32)),
        "maskB": (np.arange(128)[None, :] <= np.arange(128)[:, None]
                  ).astype(f32).astype(ml_dtypes.bfloat16),
        "w1": np.ascontiguousarray(L1_ffn_W1, f32),
        "w2": np.ascontiguousarray(L1_ffn_W2, f32),
        "wgg": np.ascontiguousarray(
            L1_ffn_Wg.astype(f32).reshape(8, 128, 8).transpose(1, 0, 2
                                                               ).reshape(128, 64)),
        "bg": np.ascontiguousarray(L1_ffn_bg.astype(f32)[:, None]),
        "wo1": np.ascontiguousarray(L1_attn_Wo.T.astype(f32)),
        "bo1": _bcol(L1_attn_bo),
        "nw": np.ascontiguousarray(L1_attn_norm_w.astype(f32).reshape(8, 128).T),
    }
    sel8 = np.zeros((8, 1024), f32)
    for e in range(NE):
        sel8[e, e * 128:(e + 1) * 128] = 1.0
    shared["sel8"] = sel8.astype(ml_dtypes.bfloat16)

    in_maps = []
    for c in range(NCORES):
        mA, _mB = _p2_masks(c)
        hsl = slice((c % 2) * 512, (c % 2) * 512 + 512)
        m = dict(shared)
        m["x"] = np.ascontiguousarray(x_cores[c], f32)
        m["maskA"] = mA
        m["wq1"] = np.ascontiguousarray(L1_attn_Wq.T[:, hsl].astype(f32))
        m["wk1"] = np.ascontiguousarray(L1_attn_Wk.T[:, hsl].astype(f32))
        m["wv1"] = np.ascontiguousarray(L1_attn_Wv.T[:, hsl].astype(f32))
        m["bq1"] = np.ascontiguousarray(
            L1_attn_bq[hsl].astype(f32).reshape(4, 128).T)
        m["bk1"] = np.ascontiguousarray(
            L1_attn_bk[hsl].astype(f32).reshape(4, 128).T)
        m["bvr1"] = np.ascontiguousarray(L1_attn_bv[None, hsl].astype(f32))
        m["mcol"] = np.full((128, 1), 1.0 if c % 2 == 0 else 0.0, f32)
        in_maps.append(m)

    res = run_bass_kernel_spmd(nc, in_maps, core_ids=list(range(NCORES)))
    out = np.empty((4, 4096, 1024), np.float32)
    for c in range(NCORES):
        b, h = c // 2, c % 2
        out[b, h * TOWN:(h + 1) * TOWN, :] = res.results[c]["out"].T
    return out


def kernel_unfused(L0_attn_x, L0_attn_Wq, L0_attn_bq, L0_attn_Wk, L0_attn_bk,
           L0_attn_Wv, L0_attn_bv, L0_attn_Wo, L0_attn_bo,
           L0_ffn_w_gate, L0_ffn_w_up, L0_ffn_w_down,
           L1_attn_norm_w, L1_attn_Wq, L1_attn_bq, L1_attn_Wk, L1_attn_bk,
           L1_attn_Wv, L1_attn_bv, L1_attn_Wo, L1_attn_bo,
           L1_ffn_W1, L1_ffn_W2, L1_ffn_Wg, L1_ffn_bg):
    x = np.asarray(L0_attn_x, np.float32)
    x_cores = _shard_x(x)
    x1 = run_l0ffn_unfused(x_cores, L0_ffn_w_gate, L0_ffn_w_up, L0_ffn_w_down,
                           TFULL)
    x2 = run_l0attn_unfused(x1, L0_attn_Wq, L0_attn_bq, L0_attn_Wk, L0_attn_bk,
                            L0_attn_Wv, L0_attn_bv, L0_attn_Wo, L0_attn_bo)
    x3 = run_l0ffn_unfused(x2, L0_ffn_w_gate, L0_ffn_w_up, L0_ffn_w_down, TOWN)
    x4 = run_moe_unfused(x3, L1_ffn_W1, L1_ffn_W2, L1_ffn_Wg, L1_ffn_bg)
    x5 = run_l1attn_unfused(x4, L1_attn_norm_w, L1_attn_Wq, L1_attn_bq,
                            L1_attn_Wk, L1_attn_bk, L1_attn_Wv, L1_attn_bv,
                            L1_attn_Wo, L1_attn_bo)
    x6 = run_moe_unfused(x5, L1_ffn_W1, L1_ffn_W2, L1_ffn_Wg, L1_ffn_bg)
    out = np.empty((4, 4096, 1024), np.float32)
    for c in range(NCORES):
        b, h = c // 2, c % 2
        out[b, h * TOWN:(h + 1) * TOWN, :] = x6[c].T
    return out
